# version 02
# baseline (speedup 1.0000x reference)
import numpy as np
from contextlib import ExitStack

import concourse.bass as bass
import concourse.mybir as mybir
from concourse.bass_utils import run_bass_kernel_spmd

F32 = mybir.dt.float32
BF16 = mybir.dt.bfloat16
AF = mybir.ActivationFunctionType
ALU = mybir.AluOpType
AX = mybir.AxisListType

NCORE = 8
BLOC = 2
S = 2048
D = 1024
DI = 2048
NST = 16
DTR = 64
DH = 4096
TOK = BLOC * S
EPS = 1e-5

TC = 256
NCH = S // TC
SEG = TC + 1
NG = 4
NSTG = 8
NIT = 32
FT = 512
NFT = TOK // FT

SP_CW = 0
SP_CB = 64
SP_DTB = 80
SP_D = 96
SP_AL = 112
SP_NW = 368
SP_S1 = 376
SP_S2 = 408
SP_H1S = 440
SP_H2S = 472
SP_W3S = 504
SPC = 512

STAGES = 5
BUILD_PARTS = ("cast", "ingest", "barrier")
PARANOID = False
MAMBA_NOSCAN = False
DBU_DVE = False
import os as _os
SCAN_DELAY = _os.environ.get("SCAN_DELAY", "")
DEBUG_TAPS = False
SERIAL_W = False


class Sem:
    def __init__(self, nc, es, name):
        self.h = es.enter_context(nc.semaphore(name))
        self.n = 0

    def bump(self, inst, k=1):
        inst.then_inc(self.h, k)
        self.n += k
        return inst


def build(stages=STAGES):
    nc = bass.Bass("TRN2", target_bir_lowering=False, detect_race_conditions=False)

    x_in = nc.dram_tensor("x_in", [TOK, D], F32, kind="ExternalInput").ap()
    sp_in = nc.dram_tensor("sp_in", [128, SPC], F32, kind="ExternalInput").ap()
    ipw = nc.dram_tensor("ipw", [D, 2 * DI], F32, kind="ExternalInput").ap()
    xpw = nc.dram_tensor("xpw", [DI, DTR + 2 * NST], F32, kind="ExternalInput").ap()
    dpw = nc.dram_tensor("dpw", [DTR, DI], F32, kind="ExternalInput").ap()
    opw = nc.dram_tensor("opw", [DI, D], F32, kind="ExternalInput").ap()
    wq = nc.dram_tensor("wq", [D, D], F32, kind="ExternalInput").ap()
    wk = nc.dram_tensor("wk", [D, D], F32, kind="ExternalInput").ap()
    wv = nc.dram_tensor("wv", [D, D], F32, kind="ExternalInput").ap()
    wo = nc.dram_tensor("wo", [D, D], F32, kind="ExternalInput").ap()
    w1t = nc.dram_tensor("w1t", [D, DH], F32, kind="ExternalInput").ap()
    w2t = nc.dram_tensor("w2t", [D, DH], F32, kind="ExternalInput").ap()
    h1t = nc.dram_tensor("h1t", [DH, DH], F32, kind="ExternalInput").ap()
    h2t = nc.dram_tensor("h2t", [DH, DH], F32, kind="ExternalInput").ap()
    w3t = nc.dram_tensor("w3t", [DH, D], F32, kind="ExternalInput").ap()

    out_t = nc.dram_tensor("out", [TOK, D], F32, kind="ExternalOutput").ap()

    resid = nc.dram_tensor("resid", [D, TOK], F32, kind="Internal").ap()
    ipb = nc.dram_tensor("ipb", [32, 128, 8, 128], BF16, kind="Internal").ap()
    opb = nc.dram_tensor("opb", [8, 128, 16, 128], BF16, kind="Internal").ap()
    wqb = nc.dram_tensor("wqb", [8, 128, 8, 128], BF16, kind="Internal").ap()
    wkb = nc.dram_tensor("wkb", [8, 128, 8, 128], BF16, kind="Internal").ap()
    wob = nc.dram_tensor("wob", [8, 128, 8, 128], BF16, kind="Internal").ap()
    wvb = nc.dram_tensor("wvb", [D, D], BF16, kind="Internal").ap()
    w1b = nc.dram_tensor("w1b", [32, 128, 8, 128], BF16, kind="Internal").ap()
    w2b = nc.dram_tensor("w2b", [32, 128, 8, 128], BF16, kind="Internal").ap()
    h1b = nc.dram_tensor("h1b", [32, 128, 32, 128], BF16, kind="Internal").ap()
    h2b = nc.dram_tensor("h2b", [32, 128, 32, 128], BF16, kind="Internal").ap()
    w3b = nc.dram_tensor("w3b", [8, 128, 32, 128], BF16, kind="Internal").ap()

    taps = {}
    if DEBUG_TAPS:
        taps["xn"] = nc.dram_tensor("tap_xn", [NCH, 128, 8 * TC], BF16, kind="ExternalOutput").ap()
        taps["xi"] = nc.dram_tensor("tap_xi", [NCH, 128, 16 * (TC + 3)], F32, kind="ExternalOutput").ap()
        taps["res"] = nc.dram_tensor("tap_res", [NCH, 128, 16 * TC], F32, kind="ExternalOutput").ap()
        taps["dlt"] = nc.dram_tensor("tap_dlt", [NCH, 128, 16 * TC], F32, kind="ExternalOutput").ap()
        taps["u0"] = nc.dram_tensor("tap_u0", [NCH, 128, 16 * TC], F32, kind="ExternalOutput").ap()
        taps["y"] = nc.dram_tensor("tap_y", [NCH, 128, 16 * TC], F32, kind="ExternalOutput").ap()
        taps["newx"] = nc.dram_tensor("tap_newx", [NCH, 128, 8 * TC], F32, kind="ExternalOutput").ap()
        taps["sub"] = nc.dram_tensor("tap_sub", [NCH, 128, 8 * TC], F32, kind="ExternalOutput").ap()
        taps["bcst"] = nc.dram_tensor("tap_bcst", [NCH, 128, 3 * TC], F32, kind="ExternalOutput").ap()
        taps["row"] = nc.dram_tensor("tap_row", [NCH, 1, 6 * TC], F32, kind="ExternalOutput").ap()
        taps["xld"] = nc.dram_tensor("tap_xld", [NCH, 128, 4 * TC], F32, kind="ExternalOutput").ap()

    es = ExitStack()
    with es:
        sp = es.enter_context(nc.sbuf_tensor("sp", [128, SPC], F32))
        a_sb = es.enter_context(nc.sbuf_tensor("a_sb", [128, 256], F32))
        ident_f = es.enter_context(nc.sbuf_tensor("ident_f", [128, 128], F32))
        ident_b = es.enter_context(nc.sbuf_tensor("ident_b", [128, 128], BF16))
        mask_sb = es.enter_context(nc.sbuf_tensor("mask_sb", [128, 128], F32))
        ones_col_b = es.enter_context(nc.sbuf_tensor("ones_col_b", [128, 1], BF16))
        ones_row_f = es.enter_context(nc.sbuf_tensor("ones_row_f", [1, 128], F32))

        sL = Sem(nc, es, "sL")
        sLw = [Sem(nc, es, "sLw0"), Sem(nc, es, "sLw1")]
        sSt = Sem(nc, es, "sSt")
        sStp = [Sem(nc, es, "sStp0"), Sem(nc, es, "sStp1")]
        sPE = Sem(nc, es, "sPE")
        sACT = Sem(nc, es, "sACT")
        sDVE = Sem(nc, es, "sDVE")
        sPOOL = Sem(nc, es, "sPOOL")
        sBar = Sem(nc, es, "sBar")

        _uid = [0]

        def sbuf(name, shape, dtype):
            _uid[0] += 1
            return nc.sbuf_tensor(f"{name}_{_uid[0]}", shape, dtype)

        def psum(name, shape, dtype=F32):
            _uid[0] += 1
            return nc.psum_tensor(f"{name}_{_uid[0]}", shape, dtype)

        block = es.enter_context(nc.Block())
        ENG = dict(tensor=block.tensor, vector=block.vector, scalar=block.scalar,
                   gpsimd=block.gpsimd, sync=block.sync)

        _in_barrier = [False]

        def sec(name, fn):
            ENG[name](fn)
            if PARANOID and not _in_barrier[0]:
                barrier()

        def barrier():
            _in_barrier[0] = True
            for en in ("tensor", "vector", "scalar", "gpsimd"):
                def _f(e):
                    e.drain()
                    e.sem_inc(sBar.h, 1)
                sec(en, _f)
                sBar.n += 1
            tgt, pl = sBar.n, sPOOL.n
            dma_tot = [(s.h, s.n) for s in (sL, sLw[0], sLw[1], sSt, sStp[0], sStp[1])]
            for en in ("tensor", "vector", "scalar", "gpsimd", "sync"):
                def _f(e, tgt=tgt, pl=pl, dma_tot=dma_tot):
                    e.wait_ge(sBar.h, tgt)
                    for h, v in dma_tot:
                        e.wait_ge(h, v)
                    e.wait_ge(sPOOL.h, pl)
                sec(en, _f)
            _in_barrier[0] = False

        def p0():
            def _sync(e):
                sL.bump(e.dma_start(out=sp[:], in_=sp_in), 16)
            sec("sync", _sync)
            ld_sp = sL.n

            def _gp(e):
                e.memset(ident_f[:], 0.0)
                e.memset(ident_b[:], 0.0)
                e.memset(mask_sb[:], 0.0)
                e.memset(ones_col_b[:], 1.0)
                e.memset(ones_row_f[:], 1.0)
                e.affine_select(out=ident_f[:], in_=ident_f[:], compare_op=ALU.not_equal,
                                fill=1.0, base=0, pattern=[[-1, 128]], channel_multiplier=1)
                e.affine_select(out=ident_b[:], in_=ident_b[:], compare_op=ALU.not_equal,
                                fill=1.0, base=0, pattern=[[-1, 128]], channel_multiplier=1)
                inst = e.affine_select(out=mask_sb[:], in_=mask_sb[:], compare_op=ALU.is_ge,
                                       fill=-3.0e38, base=0, pattern=[[-1, 128]], channel_multiplier=1)
                sPOOL.bump(inst)
            sec("gpsimd", _gp)

            def _act(e, ld_sp=ld_sp):
                e.wait_ge(sL.h, ld_sp)
                sACT.bump(e.activation(out=a_sb[:], in_=sp[:, SP_AL:SP_AL + 256], func=AF.Exp))
            sec("scalar", _act)
            act_a = sACT.n

            def _dve(e, act_a=act_a):
                e.wait_ge(sACT.h, act_a)
                sDVE.bump(e.tensor_scalar_mul(out=a_sb[:], in0=a_sb[:], scalar1=-1.0))
            sec("vector", _dve)

        def cast_weights():
            jobs = [(ipw, ipb, 8, 4096), (opw, opb, 16, 1024)]
            if stages >= 4:
                jobs += [(wq, wqb, 8, 1024), (wk, wkb, 8, 1024), (wo, wob, 8, 1024),
                         (wv, None, 8, 1024)]
            if stages >= 3:
                jobs += [(w1t, w1b, 8, 4096), (w2t, w2b, 8, 4096),
                         (h1t, h1b, 32, 4096), (h2t, h2b, 32, 4096), (w3t, w3b, 32, 1024)]
            tiles = []
            for src, dst, KC, M in jobs:
                for kc in range(KC):
                    tiles.append((src, dst, kc, M))
            n = len(tiles)
            with (sbuf("castf", [128, 2 * 4096], F32) as castf,
                  sbuf("castb", [128, 2 * 4096], BF16) as castb):
                dve0 = sDVE.n
                ldb = [sLw[0].n, sLw[1].n]
                stb = [sStp[0].n, sStp[1].n]
                st_counts = []
                acc = [stb[0], stb[1]]
                for i, (src, dst, kc, M) in enumerate(tiles):
                    n16 = 16 if dst is None else 16 * ((M // 128 + 7) // 8)
                    acc[i % 2] += n16
                    st_counts.append(acc[i % 2])

                def _sync(e):
                    for i, (src, dst, kc, M) in enumerate(tiles):
                        par = i % 2
                        if i >= 2:
                            e.wait_ge(sDVE.h, dve0 + (i - 1))
                        sLw[par].bump(e.dma_start(out=castf[:, par * 4096: par * 4096 + M],
                                                  in_=src[kc * 128:(kc + 1) * 128, :]), 16)
                sec("sync", _sync)

                def _dve(e):
                    for i, (src, dst, kc, M) in enumerate(tiles):
                        par = i % 2
                        e.wait_ge(sLw[par].h, ldb[par] + 16 * (i // 2 + 1))
                        if i >= 2:
                            e.wait_ge(sStp[par].h, st_counts[i - 2])
                        sDVE.bump(e.tensor_copy(out=castb[:, par * 4096: par * 4096 + M],
                                                in_=castf[:, par * 4096: par * 4096 + M]))
                sec("vector", _dve)

                def _gp(e):
                    for i, (src, dst, kc, M) in enumerate(tiles):
                        par = i % 2
                        e.wait_ge(sDVE.h, dve0 + i + 1)
                        if dst is None:
                            sStp[par].bump(e.dma_start(out=wvb[kc * 128:(kc + 1) * 128, :],
                                                       in_=castb[:, par * 4096: par * 4096 + M]), 16)
                        else:
                            MC = M // 128
                            for m0 in range(0, MC, 8):
                                mn = min(8, MC - m0)
                                dv = dst[m0:m0 + mn, :, kc:kc + 1, :].transpose([1, 0, 2, 3])
                                sv = castb[:, par * 4096 + m0 * 128: par * 4096 + (m0 + mn) * 128]
                                sv = sv.rearrange("p (a b) -> p a b", b=128).unsqueeze(2)
                                sStp[par].bump(e.dma_start(out=dv, in_=sv), 16)
                sec("gpsimd", _gp)

        def ingest():
            with (sbuf("ing_f", [128, 2 * 1024], F32) as ing_f,
                  sbuf("ing_o", [128, 2 * 1024], F32) as ing_o,
                  psum("ps_ing0", [128, 512], F32) as ps0,
                  psum("ps_ing1", [128, 512], F32) as ps1,
                  psum("ps_ing2", [128, 512], F32) as ps2,
                  psum("ps_ing3", [128, 512], F32) as ps3):
                psl = [ps0, ps1, ps2, ps3]
                NTI = TOK // 128
                pe0, dve0 = sPE.n, sDVE.n
                ldb = [sLw[0].n, sLw[1].n]
                stb = [sStp[0].n, sStp[1].n]

                def _sync(e):
                    for i in range(NTI):
                        if i >= 2:
                            e.wait_ge(sPE.h, pe0 + (i - 1) * 9)
                        sLw[i % 2].bump(e.dma_start(out=ing_f[:, (i % 2) * 1024:(i % 2 + 1) * 1024],
                                                    in_=x_in[i * 128:(i + 1) * 128, :]), 16)
                sec("sync", _sync)

                def _pe(e):
                    for i in range(NTI):
                        par = i % 2
                        e.wait_ge(sLw[par].h, ldb[par] + 16 * (i // 2 + 1))
                        if i >= 1:
                            e.wait_ge(sDVE.h, dve0 + i * 8)
                        for dc in range(8):
                            pb = psl[dc % 4]
                            sPE.bump(e.transpose(pb[:, 0:128],
                                                 ing_f[:, par * 1024 + dc * 128: par * 1024 + (dc + 1) * 128],
                                                 ident_f[:]))
                            if dc >= 3:
                                e.wait_ge(sDVE.h, dve0 + i * 8 + dc - 2)
                        sPE.bump(e.transpose(psl[3][:, 0:128],
                                             ing_f[:, par * 1024 + 7 * 128: par * 1024 + 8 * 128],
                                             ident_f[:]))
                sec("tensor", _pe)

                def _dve(e):
                    for i in range(NTI):
                        par = i % 2
                        if i >= 2:
                            e.wait_ge(sStp[par].h, stb[par] + 16 * (i // 2))
                        for dc in range(8):
                            pb = psl[dc % 4]
                            e.wait_ge(sPE.h, pe0 + i * 9 + dc + 2)
                            sDVE.bump(e.tensor_copy(
                                out=ing_o[:, par * 1024 + dc * 128: par * 1024 + (dc + 1) * 128],
                                in_=pb[:, 0:128]))
                sec("vector", _dve)

                def _gp(e):
                    for i in range(NTI):
                        par = i % 2
                        e.wait_ge(sDVE.h, dve0 + (i + 1) * 8)
                        dv = resid.rearrange("(c p) t -> p c t", p=128)[:, :, i * 128:(i + 1) * 128]
                        sv = ing_o[:, par * 1024:(par + 1) * 1024].rearrange("p (c t) -> p c t", c=8)
                        sStp[par].bump(e.dma_start(out=dv, in_=sv), 16)
                sec("gpsimd", _gp)

        def output_phase():
            with (sbuf("o_f", [128, 2 * 1024], F32) as o_f,
                  sbuf("o_o", [128, 2 * 1024], F32) as o_o,
                  psum("ps_of0", [128, 512], F32) as ps0,
                  psum("ps_of1", [128, 512], F32) as ps1,
                  psum("ps_of2", [128, 512], F32) as ps2,
                  psum("ps_of3", [128, 512], F32) as ps3):
                psl = [ps0, ps1, ps2, ps3]
                NTI = TOK // 128
                pe0, dve0 = sPE.n, sDVE.n
                ldb = [sLw[0].n, sLw[1].n]
                stb = [sStp[0].n, sStp[1].n]

                def _sync(e):
                    for i in range(NTI):
                        if i >= 2:
                            e.wait_ge(sPE.h, pe0 + (i - 1) * 9)
                        sv = resid.rearrange("(c p) t -> p c t", p=128)[:, :, i * 128:(i + 1) * 128]
                        dv = o_f[:, (i % 2) * 1024:(i % 2 + 1) * 1024].rearrange("p (c t) -> p c t", c=8)
                        sLw[i % 2].bump(e.dma_start(out=dv, in_=sv), 16)
                sec("sync", _sync)

                def _pe(e):
                    for i in range(NTI):
                        par = i % 2
                        e.wait_ge(sLw[par].h, ldb[par] + 16 * (i // 2 + 1))
                        if i >= 1:
                            e.wait_ge(sDVE.h, dve0 + i * 8)
                        for dc in range(8):
                            pb = psl[dc % 4]
                            sPE.bump(e.transpose(pb[:, 0:128],
                                                 o_f[:, par * 1024 + dc * 128: par * 1024 + (dc + 1) * 128],
                                                 ident_f[:]))
                            if dc >= 3:
                                e.wait_ge(sDVE.h, dve0 + i * 8 + dc - 2)
                        sPE.bump(e.transpose(psl[3][:, 0:128],
                                             o_f[:, par * 1024 + 7 * 128: par * 1024 + 8 * 128],
                                             ident_f[:]))
                sec("tensor", _pe)

                def _dve(e):
                    for i in range(NTI):
                        par = i % 2
                        if i >= 2:
                            e.wait_ge(sStp[par].h, stb[par] + 16 * (i // 2))
                        for dc in range(8):
                            pb = psl[dc % 4]
                            e.wait_ge(sPE.h, pe0 + i * 9 + dc + 2)
                            sDVE.bump(e.tensor_copy(
                                out=o_o[:, par * 1024 + dc * 128: par * 1024 + (dc + 1) * 128],
                                in_=pb[:, 0:128]))
                sec("vector", _dve)

                def _gp(e):
                    for i in range(NTI):
                        par = i % 2
                        e.wait_ge(sDVE.h, dve0 + (i + 1) * 8)
                        sStp[par].bump(e.dma_start(out=out_t[i * 128:(i + 1) * 128, :],
                                                   in_=o_o[:, par * 1024:(par + 1) * 1024]), 16)
                    e.wait_ge(sStp[0].h, sStp[0].n)
                    e.wait_ge(sStp[1].h, sStp[1].n)
                    e.wait_ge(sSt.h, sSt.n)
                sec("gpsimd", _gp)

        def emit_ln(W, xf, xbf, sqbf, out_bf, rows, bcast, p_sx, p_sq, p_b1, p_b2, wait_ld,
                    out_off=0, out_stride=None):
            if out_stride is None:
                out_stride = W

            pe_now = sPE.n

            def _dve(e, pe_now=pe_now):
                e.wait_ge(sL.h, wait_ld)
                e.wait_ge(sPE.h, pe_now)
                sDVE.bump(e.tensor_copy(out=xbf[:, 0:8 * W], in_=xf[:, 0:8 * W]))
            sec("vector", _dve)
            c1 = sDVE.n

            def _act(e, c1=c1):
                e.wait_ge(sDVE.h, c1)
                sACT.bump(e.activation(out=sqbf[:, 0:8 * W], in_=xbf[:, 0:8 * W], func=AF.Square))
            sec("scalar", _act)
            c2 = sACT.n

            def _pe(e, c2=c2):
                e.wait_ge(sACT.h, c2)
                for dc in range(8):
                    e.matmul(p_sx[0:1, 0:W], ones_col_b[:, 0:1], xbf[:, dc * W:(dc + 1) * W],
                             start=(dc == 0), stop=(dc == 7))
                for dc in range(8):
                    inst = e.matmul(p_sq[0:1, 0:W], ones_col_b[:, 0:1], sqbf[:, dc * W:(dc + 1) * W],
                                    start=(dc == 0), stop=(dc == 7))
                sPE.bump(inst)
            sec("tensor", _pe)
            c3 = sPE.n

            mean_r, rstd_r, t0_r, t1_r = (rows[0:1, i * W:(i + 1) * W] for i in range(4))

            def _dve2(e, c3=c3):
                e.wait_ge(sPE.h, c3)
                e.tensor_scalar_mul(out=mean_r, in0=p_sx[0:1, 0:W], scalar1=1.0 / D)
                e.tensor_scalar_mul(out=t0_r, in0=p_sq[0:1, 0:W], scalar1=1.0 / D)
                e.tensor_mul(out=t1_r, in0=mean_r, in1=mean_r)
                e.tensor_sub(out=t0_r, in0=t0_r, in1=t1_r)
                sDVE.bump(e.tensor_scalar_add(out=t0_r, in0=t0_r, scalar1=EPS))
            sec("vector", _dve2)
            c4 = sDVE.n

            def _act2(e, c4=c4):
                e.wait_ge(sDVE.h, c4)
                e.activation(out=t1_r, in_=t0_r, func=AF.Ln)
                sACT.bump(e.activation(out=rstd_r, in_=t1_r, func=AF.Exp, scale=-0.5))
            sec("scalar", _act2)
            c5 = sACT.n

            def _pe2(e, c5=c5):
                e.wait_ge(sACT.h, c5)
                e.matmul(p_b1[:, 0:W], ones_row_f[0:1, :], mean_r, start=True, stop=True)
                sPE.bump(e.matmul(p_b2[:, 0:W], ones_row_f[0:1, :], rstd_r, start=True, stop=True))
            sec("tensor", _pe2)
            c6 = sPE.n

            def _act3(e, c6=c6):
                e.wait_ge(sPE.h, c6)
                e.activation(out=bcast[:, 0:W], in_=p_b1[:, 0:W], func=AF.Copy)
                sACT.bump(e.activation(out=bcast[:, W:2 * W], in_=p_b2[:, 0:W], func=AF.Copy))
            sec("scalar", _act3)
            c7 = sACT.n

            def _dve3(e, c7=c7):
                e.wait_ge(sACT.h, c7)
                for dc in range(8):
                    e.tensor_sub(out=xf[:, dc * W:(dc + 1) * W],
                                 in0=xf[:, dc * W:(dc + 1) * W], in1=bcast[:, 0:W])
                    inst = e.tensor_mul(out=out_bf[:, out_off + dc * out_stride:
                                               out_off + dc * out_stride + W],
                                        in0=xf[:, dc * W:(dc + 1) * W], in1=bcast[:, W:2 * W])
                sDVE.bump(inst)
            sec("vector", _dve3)
            return sDVE.n

        def mamba(b):
            with ExitStack() as ms:
                x_sb = ms.enter_context(sbuf("m_x", [128, 8 * TC], F32))
                xi_sb = ms.enter_context(sbuf("m_xi", [128, 16 * (TC + 3)], F32))
                xi2_sb = ms.enter_context(sbuf("m_xi2", [128, 16 * TC], F32))
                res_sb = ms.enter_context(sbuf("m_res", [128, 16 * TC], F32))
                dlt_sb = ms.enter_context(sbuf("m_dlt", [128, 16 * TC], F32))
                y_sb = ms.enter_context(sbuf("m_y", [128, 16 * TC], F32))
                xi2b_sb = ms.enter_context(sbuf("m_xi2b", [128, 16 * TC], BF16))
                xbf_sb = xi2b_sb[:, 0:8 * TC]
                sqbf_sb = xi2b_sb[:, 8 * TC:16 * TC]
                xn_sb = xi2b_sb[:, 0:8 * TC]
                drb_sb = ms.enter_context(sbuf("m_drb", [64, TC], BF16))
                bc_sb = ms.enter_context(sbuf("m_bc", [32, TC], F32))
                bb_sb = ms.enter_context(sbuf("m_bb", [128, NSTG * TC], F32))
                cb_sb = ms.enter_context(sbuf("m_cb", [128, NSTG * TC], F32))
                da_sb = [xi_sb[:, 0:NSTG * SEG],
                         xi_sb[:, NSTG * SEG:2 * NSTG * SEG]]
                du_sb = [ms.enter_context(sbuf(f"m_du{i}", [128, NSTG * SEG], F32))
                         for i in range(2)]
                s_sb = [ms.enter_context(sbuf(f"m_s{i}", [128, NSTG * SEG], F32))
                        for i in range(2)]
                ct_sb = ms.enter_context(sbuf("m_ct", [128, NSTG * TC], F32))
                yt_sb = ms.enter_context(sbuf("m_yt", [128, TC], F32))
                st_sb = ms.enter_context(sbuf("m_st", [128, 256], F32))
                halo_sb = ms.enter_context(sbuf("m_halo", [128, 16 * 3], F32))
                row_sb = ms.enter_context(sbuf("m_row", [1, 6 * TC], F32))
                bcst_sb = ms.enter_context(sbuf("m_bcst", [128, 3 * TC], F32))
                wi_sb = [ms.enter_context(sbuf(f"m_wi{i}", [128, 8 * 128], BF16))
                         for i in range(2)]
                wo_sb = [ms.enter_context(sbuf(f"m_wo{i}", [128, 16 * 128], BF16))
                         for i in range(2)]
                xp_sb = ms.enter_context(sbuf("m_xp", [128, 16 * 96], BF16))
                dp_sb = ms.enter_context(sbuf("m_dp", [64, DI], BF16))
                xpf_sb = y_sb[:, 0:16 * 96]
                dpf_sb = dlt_sb[0:64, 0:DI]
                sptmp = [ms.enter_context(sbuf(f"m_spt{i}", [128, TC], F32))
                         for i in range(2)]
                sel = ms.enter_context(sbuf("m_sel", [32, 32 * 128], F32))
                xld_sb = ct_sb[:, 0:4 * TC] if DEBUG_TAPS else None

                def _gpsel(e):
                    e.memset(sel[:], 0.0)
                    e.affine_select(out=sel[:], in_=sel[:], compare_op=ALU.not_equal,
                                    fill=1.0, base=0, pattern=[[-1, 32], [0, 128]],
                                    channel_multiplier=1)
                    sPOOL.bump(e.tensor_copy(out=sel[0:32, 0:4], in_=sel[0:32, 0:4]))
                sec("gpsimd", _gpsel)
                sel_done = sPOOL.n

                ps_w = [ms.enter_context(psum(f"m_psw{i}", [128, 512], F32))
                        for i in range(2)]
                ps_sx = ms.enter_context(psum("m_pssx", [1, TC], F32))
                ps_sq = ms.enter_context(psum("m_pssq", [1, TC], F32))
                ps_b1 = ms.enter_context(psum("m_psb1", [128, 512], F32))
                ps_b2 = ms.enter_context(psum("m_psb2", [128, 512], F32))
                ps_b3 = ms.enter_context(psum("m_psb3", [128, 512], F32))
                ps_x = ms.enter_context(psum("m_psx", [96, TC], F32))

                ld0 = sL.n

                def _sync0(e):
                    sv = xpw.rearrange("(c p) f -> p c f", p=128)
                    sL.bump(e.dma_start(out=xpf_sb.rearrange("p (c f) -> p c f", c=16),
                                        in_=sv), 16)
                    sL.bump(e.dma_start(out=dpf_sb, in_=dpw), 16)
                sec("sync", _sync0)

                def _dve0(e, ld0=sL.n):
                    e.wait_ge(sL.h, ld0)
                    e.tensor_copy(out=xp_sb[:], in_=xpf_sb)
                    e.tensor_copy(out=dp_sb[:], in_=dpf_sb)
                    e.memset(st_sb[:], 0.0)
                    sDVE.bump(e.memset(halo_sb[:], 0.0))
                sec("vector", _dve0)
                init_done = sDVE.n

                chunk_last_dve = init_done
                chunk_last_store = sSt.n

                for c in range(NCH):
                    col0 = b * S + c * TC

                    def _syncx(e, col0=col0, cld=chunk_last_dve, cls=chunk_last_store):
                        e.wait_ge(sDVE.h, cld)
                        e.wait_ge(sSt.h, cls)
                        sv = resid.rearrange("(c p) t -> p c t", p=128)[:, :, col0:col0 + TC]
                        sL.bump(e.dma_start(out=x_sb[:].rearrange("p (c t) -> p c t", c=8),
                                            in_=sv), 16)
                    sec("sync", _syncx)
                    ld_x = sL.n

                    def _dveA(e, ld_x=ld_x):
                        e.wait_ge(sL.h, ld_x)
                        if DEBUG_TAPS:
                            e.tensor_copy(out=xld_sb, in_=x_sb[:, 0:4 * TC])
                        sDVE.bump(e.tensor_copy(out=xbf_sb, in_=x_sb[:]))
                    sec("vector", _dveA)
                    c1 = sDVE.n

                    def _actA(e, c1=c1):
                        e.wait_ge(sDVE.h, c1)
                        sACT.bump(e.activation(out=sqbf_sb, in_=xbf_sb, func=AF.Square))
                    sec("scalar", _actA)
                    c2 = sACT.n

                    def _peA(e, c2=c2):
                        e.wait_ge(sACT.h, c2)
                        for dc in range(8):
                            e.matmul(ps_sx[0:1, :], ones_col_b[:, 0:1],
                                     xbf_sb[:, dc * TC:(dc + 1) * TC],
                                     start=(dc == 0), stop=(dc == 7))
                        for dc in range(8):
                            e.matmul(ps_sq[0:1, :], ones_col_b[:, 0:1],
                                     sqbf_sb[:, dc * TC:(dc + 1) * TC],
                                     start=(dc == 0), stop=(dc == 7))
                        sPE.bump(e.matmul(ps_b1[:, 0:128], ones_row_f[0:1, :],
                                          ones_row_f[0:1, 0:128], start=True, stop=True))
                    sec("tensor", _peA)
                    c3 = sPE.n

                    mean_r = row_sb[0:1, 0:TC]
                    rs1_r = row_sb[0:1, TC:2 * TC]
                    rA_r = row_sb[0:1, 2 * TC:3 * TC]
                    t0_r = row_sb[0:1, 3 * TC:4 * TC]
                    t1_r = row_sb[0:1, 4 * TC:5 * TC]
                    var_r = row_sb[0:1, 5 * TC:6 * TC]

                    def _dveB(e, c3=c3):
                        e.wait_ge(sPE.h, c3)
                        e.tensor_scalar_mul(out=mean_r, in0=ps_sx[0:1, :], scalar1=1.0 / D)
                        e.tensor_scalar_mul(out=t0_r, in0=ps_sq[0:1, :], scalar1=1.0 / D)
                        e.tensor_mul(out=t1_r, in0=mean_r, in1=mean_r)
                        e.tensor_sub(out=var_r, in0=t0_r, in1=t1_r)
                        sDVE.bump(e.tensor_scalar_add(out=t0_r, in0=var_r, scalar1=EPS))
                    sec("vector", _dveB)
                    c4 = sDVE.n

                    def _actB(e, c4=c4):
                        e.wait_ge(sDVE.h, c4)
                        e.activation(out=t1_r, in_=t0_r, func=AF.Ln)
                        sACT.bump(e.activation(out=rs1_r, in_=t1_r, func=AF.Exp, scale=-0.5))
                    sec("scalar", _actB)
                    c5 = sACT.n

                    def _dveC(e, c5=c5):
                        e.wait_ge(sACT.h, c5)
                        e.tensor_mul(out=t1_r, in0=rs1_r, in1=rs1_r)
                        e.tensor_mul(out=t1_r, in0=var_r, in1=t1_r)
                        sDVE.bump(e.tensor_scalar_add(out=t1_r, in0=t1_r, scalar1=EPS))
                    sec("vector", _dveC)
                    c6 = sDVE.n

                    def _actC(e, c6=c6):
                        e.wait_ge(sDVE.h, c6)
                        e.activation(out=t0_r, in_=t1_r, func=AF.Ln)
                        sACT.bump(e.activation(out=t1_r, in_=t0_r, func=AF.Exp, scale=-0.5))
                    sec("scalar", _actC)
                    c7 = sACT.n

                    def _dveD(e, c7=c7):
                        e.wait_ge(sACT.h, c7)
                        sDVE.bump(e.tensor_mul(out=rA_r, in0=rs1_r, in1=t1_r))
                    sec("vector", _dveD)
                    c8 = sDVE.n

                    def _peB(e, c8=c8):
                        e.wait_ge(sDVE.h, c8)
                        e.matmul(ps_b1[:, 0:TC], ones_row_f[0:1, :], mean_r, start=True, stop=True)
                        e.matmul(ps_b2[:, 0:TC], ones_row_f[0:1, :], rs1_r, start=True, stop=True)
                        sPE.bump(e.matmul(ps_b3[:, 0:TC], ones_row_f[0:1, :], rA_r,
                                          start=True, stop=True))
                    sec("tensor", _peB)
                    c9 = sPE.n

                    def _actD(e, c9=c9):
                        e.wait_ge(sPE.h, c9)
                        e.activation(out=bcst_sb[:, 0:TC], in_=ps_b1[:, 0:TC], func=AF.Copy)
                        e.activation(out=bcst_sb[:, TC:2 * TC], in_=ps_b2[:, 0:TC], func=AF.Copy)
                        sACT.bump(e.activation(out=bcst_sb[:, 2 * TC:3 * TC], in_=ps_b3[:, 0:TC],
                                               func=AF.Copy))
                    sec("scalar", _actD)
                    c10 = sACT.n

                    def _dveE(e, c10=c10):
                        e.wait_ge(sACT.h, c10)
                        mb = bcst_sb[:, 0:TC].unsqueeze(1).broadcast_to([128, 8, TC])
                        rab = bcst_sb[:, 2 * TC:3 * TC].unsqueeze(1).broadcast_to([128, 8, TC])
                        yv = y_sb[:, 0:8 * TC].rearrange("p (c t) -> p c t", c=8)
                        e.tensor_sub(out=yv,
                                     in0=x_sb[:].rearrange("p (c t) -> p c t", c=8), in1=mb)
                        sDVE.bump(e.tensor_mul(out=yv, in0=yv, in1=rab))
                    sec("vector", _dveE)
                    c11 = sDVE.n

                    def _actE(e, c11=c11):
                        e.wait_ge(sDVE.h, c11)
                        for dc in range(8):
                            inst = e.activation(out=xn_sb[:, dc * TC:(dc + 1) * TC],
                                                in_=y_sb[:, dc * TC:(dc + 1) * TC], func=AF.Copy,
                                                scale=sp[:, SP_NW + dc:SP_NW + dc + 1])
                        sACT.bump(inst)
                    sec("scalar", _actE)
                    act_xn = sACT.n

                    ip_pe0, ip_act0 = sPE.n, sACT.n
                    ip_ldb = [sLw[0].n, sLw[1].n]

                    def _syncW(e, ip_pe0=ip_pe0, ip_act0=ip_act0):
                        for m in range(32):
                            if SERIAL_W and m >= 1:
                                e.wait_ge(sACT.h, ip_act0 + m)
                            elif m >= 2:
                                e.wait_ge(sPE.h, ip_pe0 + m - 1)
                            sLw[m % 2].bump(e.dma_start(
                                out=wi_sb[m % 2][:].rearrange("p (c f) -> p c f", c=8),
                                in_=ipb[m, :, :, :]), 16)
                    sec("sync", _syncW)

                    def _peC(e, act_xn=act_xn, ip_ldb=ip_ldb, ip_act0=ip_act0):
                        e.wait_ge(sACT.h, act_xn)
                        for m in range(32):
                            pb = ps_w[m % 2]
                            e.wait_ge(sLw[m % 2].h, ip_ldb[m % 2] + 16 * (m // 2 + 1))
                            if m >= 2:
                                e.wait_ge(sACT.h, ip_act0 + m - 1)
                            for kc in range(8):
                                inst = e.matmul(pb[:, 0:TC],
                                                wi_sb[m % 2][:, kc * 128:(kc + 1) * 128],
                                                xn_sb[:, kc * TC:(kc + 1) * TC],
                                                start=(kc == 0), stop=(kc == 7))
                            sPE.bump(inst)
                    sec("tensor", _peC)

                    def _actF(e, ip_pe0=ip_pe0):
                        for m in range(32):
                            pb = ps_w[m % 2]
                            e.wait_ge(sPE.h, ip_pe0 + m + 1)
                            if m < 16:
                                dst = xi_sb[:, m * (TC + 3) + 3: m * (TC + 3) + 3 + TC]
                                inst = e.activation(out=dst, in_=pb[:, 0:TC], func=AF.Copy)
                            else:
                                dst = res_sb[:, (m - 16) * TC:(m - 15) * TC]
                                inst = e.activation(out=dst, in_=pb[:, 0:TC], func=AF.Silu)
                            sACT.bump(inst)
                    sec("scalar", _actF)
                    act_ip = sACT.n

                    def _dveF(e, act_ip=act_ip):
                        e.wait_ge(sACT.h, act_ip)
                        xiv = xi_sb[:].rearrange("p (c t) -> p c t", c=16)
                        e.tensor_copy(out=xiv[:, :, 0:3],
                                      in_=halo_sb[:].rearrange("p (c h) -> p c h", c=16))
                        for dc in range(16):
                            off = dc * (TC + 3)
                            for j in range(4):
                                src = xi_sb[:, off + j: off + j + TC]
                                scl = sp[:, SP_CW + j * 16 + dc: SP_CW + j * 16 + dc + 1]
                                if j == 0:
                                    e.tensor_scalar_mul(out=xi2_sb[:, dc * TC:(dc + 1) * TC],
                                                        in0=src, scalar1=scl)
                                else:
                                    e.scalar_tensor_tensor(
                                        out=xi2_sb[:, dc * TC:(dc + 1) * TC], in0=src, scalar=scl,
                                        in1=xi2_sb[:, dc * TC:(dc + 1) * TC],
                                        op0=ALU.mult, op1=ALU.add)
                        e.tensor_copy(out=halo_sb[:].rearrange("p (c h) -> p c h", c=16),
                                      in_=xiv[:, :, TC:TC + 3])
                        e.memset(da_sb[0].rearrange("p (n s) -> p n s", n=NSTG)[:, :, 0:1], 0.0)
                        sDVE.bump(e.memset(
                            da_sb[1].rearrange("p (n s) -> p n s", n=NSTG)[:, :, 0:1], 0.0))
                    sec("vector", _dveF)
                    c12 = sDVE.n

                    def _actG(e, c12=c12):
                        e.wait_ge(sDVE.h, c12)
                        for dc in range(16):
                            inst = e.activation(out=xi2_sb[:, dc * TC:(dc + 1) * TC],
                                                in_=xi2_sb[:, dc * TC:(dc + 1) * TC], func=AF.Silu,
                                                bias=sp[:, SP_CB + dc:SP_CB + dc + 1])
                        sACT.bump(inst)
                    sec("scalar", _actG)
                    c13 = sACT.n

                    def _dveG(e, c13=c13):
                        e.wait_ge(sACT.h, c13)
                        sDVE.bump(e.tensor_copy(out=xi2b_sb[:], in_=xi2_sb[:]))
                    sec("vector", _dveG)
                    c14 = sDVE.n

                    def _peD(e, c14=c14):
                        e.wait_ge(sDVE.h, c14)
                        for kc in range(16):
                            inst = e.matmul(ps_x[:, :], xp_sb[:, kc * 96:(kc + 1) * 96],
                                            xi2b_sb[:, kc * TC:(kc + 1) * TC],
                                            start=(kc == 0), stop=(kc == 15))
                        sPE.bump(inst)
                    sec("tensor", _peD)
                    c15 = sPE.n

                    def _actH(e, c15=c15):
                        e.wait_ge(sPE.h, c15)
                        e.activation(out=drb_sb[:, :], in_=ps_x[0:64, :], func=AF.Copy)
                        sACT.bump(e.activation(out=bc_sb[:, :], in_=ps_x[64:96, :], func=AF.Copy))
                    sec("scalar", _actH)
                    act_xdbl = sACT.n

                    dt_pe0, dt_act0, dt_dve0 = sPE.n, sACT.n, sDVE.n

                    def _peE(e, act_xdbl=act_xdbl, dt_dve0=dt_dve0):
                        e.wait_ge(sACT.h, act_xdbl)
                        for mc in range(16):
                            pb = ps_w[mc % 2]
                            if mc >= 2:
                                e.wait_ge(sDVE.h, dt_dve0 + 2 * mc - 3)
                            inst = e.matmul(pb[:, 0:TC], dp_sb[0:64, mc * 128:(mc + 1) * 128],
                                            drb_sb[0:64, :], start=True, stop=True)
                            sPE.bump(inst)
                        sPE.bump(e.matmul(ps_x[:, 0:64], ones_row_f[0:1, 0:96],
                                          ones_row_f[0:1, 0:64], start=True, stop=True))
                    sec("tensor", _peE)

                    def _dveH1(e, dt_pe0=dt_pe0, dt_act0=dt_act0):
                        for mc in range(16):
                            pb = ps_w[mc % 2]
                            e.wait_ge(sPE.h, dt_pe0 + mc + 2)
                            z = dlt_sb[:, mc * TC:(mc + 1) * TC]
                            sDVE.bump(e.tensor_scalar_add(
                                out=z, in0=pb[:, 0:TC],
                                scalar1=sp[:, SP_DTB + mc:SP_DTB + mc + 1]))
                            e.wait_ge(sACT.h, dt_act0 + mc + 1)
                            inst = e.scalar_tensor_tensor(out=z, in0=z, scalar=0.0,
                                                          in1=sptmp[mc % 2][:, :],
                                                          op0=ALU.max, op1=ALU.add)
                            sDVE.bump(inst)
                    sec("vector", _dveH1)

                    def _actI(e, dt_dve0=dt_dve0):
                        for mc in range(16):
                            e.wait_ge(sDVE.h, dt_dve0 + 2 * mc + 1)
                            if mc >= 2:
                                e.wait_ge(sDVE.h, dt_dve0 + 2 * mc - 2)
                            z = dlt_sb[:, mc * TC:(mc + 1) * TC]
                            t = sptmp[mc % 2][:, :]
                            e.activation(out=t, in_=z, func=AF.Abs)
                            e.activation(out=t, in_=t, func=AF.Exp, scale=-1.0)
                            inst = e.activation(out=t, in_=t, func=AF.Ln, bias=1.0)
                            sACT.bump(inst)
                    sec("scalar", _actI)
                    dve_dt = sDVE.n

                    def _dveI(e, dve_dt=dve_dt):
                        e.wait_ge(sDVE.h, dve_dt)
                        for dc in range(16):
                            e.tensor_scalar_mul(out=y_sb[:, dc * TC:(dc + 1) * TC],
                                                in0=xi2_sb[:, dc * TC:(dc + 1) * TC],
                                                scalar1=sp[:, SP_D + dc:SP_D + dc + 1])
                        sDVE.bump(e.tensor_mul(out=xi2_sb[:], in0=dlt_sb[:], in1=xi2_sb[:]))
                    sec("vector", _dveI)
                    dve_u0 = sDVE.n
                    u0_sb = xi2_sb

                    A0, P0, D0, PE0 = sACT.n, sPOOL.n, sDVE.n, sPE.n
                    bank_b = [ps_b1, ps_b2]
                    bank_c = [ps_b3, ps_w[0]]

                    a_evac_c = {}; a_evac_done_c = {}; a_cover_c = {}; a_ext_c = {}
                    p_ct_c = {}; p_dbu_c = {}; d_tts_c = {}; d_acc_c = {}; pe_mm_c = {}
                    _a, _p, _d, _pe = A0, P0, D0, PE0
                    for g2 in range(2):
                        for mi in range(17):
                            _pe += 1; pe_mm_c[(g2, mi)] = _pe
                        for ei in range(8):
                            _a += 1; a_evac_c[(g2, ei)] = _a
                        _a += 1; a_evac_done_c[g2] = _a
                        for dc in range(16):
                            it = g2 * 16 + dc
                            _a += 1; a_cover_c[it] = _a
                            if it >= 1:
                                _a += 1; a_ext_c[it - 1] = _a
                    _a += 1; a_ext_c[NIT - 1] = _a
                    for it in range(NIT):
                        if it >= 1:
                            _p += 1; p_ct_c[it - 1] = _p
                        _p += 1; p_dbu_c[it] = _p
                        _d += 1; d_tts_c[it] = _d
                        if it >= 1:
                            _d += 1; d_acc_c[it - 1] = _d
                    _p += 1; p_ct_c[NIT - 1] = _p
                    _d += 1; d_acc_c[NIT - 1] = _d

                    if not MAMBA_NOSCAN:
                        def _peBC(e, act_xdbl=act_xdbl, dve_dt=dve_dt):
                            for g2 in range(2):
                                if g2 == 0:
                                    e.wait_ge(sACT.h, act_xdbl)
                                    e.wait_ge(sPOOL.h, sel_done)
                                    e.wait_ge(sDVE.h, dve_dt)
                                else:
                                    e.wait_ge(sACT.h, a_evac_done_c[0])
                                    e.wait_ge(sPOOL.h, p_ct_c[15])
                                for p in range(4):
                                    if p >= 2:
                                        e.wait_ge(sACT.h, a_evac_c[(g2, 2 * (p - 2) + 1)])
                                    for h in range(2):
                                        n = g2 * NSTG + 2 * p + h
                                        sPE.bump(e.matmul(
                                            bank_b[p % 2][:, h * TC:(h + 1) * TC],
                                            sel[:, n * 128:(n + 1) * 128],
                                            bc_sb[0:32, :], start=True, stop=True))
                                    for h in range(2):
                                        n = g2 * NSTG + 2 * p + h
                                        sPE.bump(e.matmul(
                                            bank_c[p % 2][:, h * TC:(h + 1) * TC],
                                            sel[:, (16 + n) * 128:(17 + n) * 128],
                                            bc_sb[0:32, :], start=True, stop=True))
                                n = g2 * NSTG + 7
                                sPE.bump(e.matmul(bank_c[1][:, TC:2 * TC],
                                                  sel[:, (16 + n) * 128:(17 + n) * 128],
                                                  bc_sb[0:32, :], start=True, stop=True))
                        sec("tensor", _peBC)

                        def _actSC(e, dve_dt=dve_dt):
                            e.wait_ge(sDVE.h, dve_dt)
                            for g2 in range(2):
                                for p in range(4):
                                    e.wait_ge(sPE.h, pe_mm_c[(g2, 4 * p + 2)])
                                    sACT.bump(e.activation(
                                        out=bb_sb[:, 2 * p * TC:(2 * p + 2) * TC],
                                        in_=bank_b[p % 2][:, 0:2 * TC], func=AF.Copy))
                                    e.wait_ge(sPE.h, pe_mm_c[(g2, 16) if p == 3
                                                             else (g2, 4 * p + 4)])
                                    sACT.bump(e.activation(
                                        out=cb_sb[:, 2 * p * TC:(2 * p + 2) * TC],
                                        in_=bank_c[p % 2][:, 0:2 * TC], func=AF.Copy))
                                sACT.bump(e.activation(out=sptmp[0][:, 4:8],
                                                       in_=sptmp[0][:, 4:8], func=AF.Copy))
                                for dc in range(16):
                                    it = g2 * 16 + dc
                                    par = it % 2
                                    if it >= 2:
                                        e.wait_ge(sDVE.h, d_tts_c[it - 2])
                                    for n8 in range(NSTG):
                                        n = g2 * NSTG + n8
                                        e.activation(
                                            out=da_sb[par][:, n8 * SEG + 1:(n8 + 1) * SEG],
                                            in_=dlt_sb[:, dc * TC:(dc + 1) * TC], func=AF.Exp,
                                            scale=a_sb[:, dc * 16 + n: dc * 16 + n + 1])
                                    duv = du_sb[par][:].rearrange("p (n s) -> p n s", n=NSTG)
                                    e.activation(
                                        out=duv[:, :, 0:1].squeeze(2),
                                        in_=st_sb[:, dc * 16 + g2 * NSTG:
                                                  dc * 16 + g2 * NSTG + NSTG], func=AF.Copy)
                                    sACT.bump(e.activation(out=sptmp[0][:, 0:4],
                                                           in_=sptmp[0][:, 0:4], func=AF.Copy))
                                    if it >= 1:
                                        j = it - 1
                                        e.wait_ge(sDVE.h, d_tts_c[j])
                                        sv = s_sb[j % 2][:].rearrange("p (n s) -> p n s",
                                                                      n=NSTG)
                                        sACT.bump(e.activation(
                                            out=st_sb[:, (j % 16) * 16 + (j // 16) * NSTG:
                                                      (j % 16) * 16 + (j // 16) * NSTG + NSTG],
                                            in_=sv[:, :, SEG - 1:SEG].squeeze(2), func=AF.Copy))
                            j = NIT - 1
                            e.wait_ge(sDVE.h, d_tts_c[j])
                            sv = s_sb[j % 2][:].rearrange("p (n s) -> p n s", n=NSTG)
                            sACT.bump(e.activation(
                                out=st_sb[:, 15 * 16 + NSTG: 15 * 16 + 2 * NSTG],
                                in_=sv[:, :, SEG - 1:SEG].squeeze(2), func=AF.Copy))
                        sec("scalar", _actSC)
                        assert sACT.n == a_ext_c[NIT - 1], (sACT.n, a_ext_c[NIT - 1])

                        def _poolSC(e, dve_u0=dve_u0):
                            e.wait_ge(sDVE.h, dve_u0)
                            for it in range(NIT):
                                g2, dc = it // 16, it % 16
                                par = it % 2
                                if it >= 1:
                                    j = it - 1
                                    e.wait_ge(sDVE.h, d_tts_c[j])
                                    if j >= 1:
                                        e.wait_ge(sDVE.h, d_acc_c[j - 1])
                                    sjv = s_sb[j % 2][:].rearrange("p (n s) -> p n s",
                                                                   n=NSTG)
                                    e.tensor_mul(
                                        out=ct_sb[:].rearrange("p (n t) -> p n t",
                                                               n=NSTG),
                                        in0=sjv[:, :, 1:SEG],
                                        in1=cb_sb[:].rearrange("p (n t) -> p n t", n=NSTG))
                                    sPOOL.bump(e.tensor_copy(out=sptmp[1][:, 0:4],
                                                             in_=sptmp[1][:, 0:4]))
                                if dc == 0:
                                    e.wait_ge(sACT.h, a_evac_done_c[g2])
                                if it >= 2:
                                    e.wait_ge(sDVE.h, d_tts_c[it - 2])
                                u0v = u0_sb[:, dc * TC:(dc + 1) * TC].unsqueeze(1) \
                                    .broadcast_to([128, NSTG, TC])
                                duv = du_sb[par][:].rearrange("p (n s) -> p n s",
                                                              n=NSTG)[:, :, 1:SEG]
                                e.tensor_mul(out=duv, in0=u0v,
                                             in1=bb_sb[:].rearrange("p (n t) -> p n t",
                                                                    n=NSTG))
                                sPOOL.bump(e.tensor_copy(out=sptmp[1][:, 0:4],
                                                         in_=sptmp[1][:, 0:4]))
                            j = NIT - 1
                            e.wait_ge(sDVE.h, d_tts_c[j])
                            e.wait_ge(sDVE.h, d_acc_c[j - 1])
                            sjv = s_sb[j % 2][:].rearrange("p (n s) -> p n s", n=NSTG)
                            e.tensor_mul(out=ct_sb[:].rearrange("p (n t) -> p n t",
                                                                n=NSTG),
                                         in0=sjv[:, :, 1:SEG],
                                         in1=cb_sb[:].rearrange("p (n t) -> p n t", n=NSTG))
                            sPOOL.bump(e.tensor_copy(out=sptmp[1][:, 0:4],
                                                     in_=sptmp[1][:, 0:4]))
                        sec("gpsimd", _poolSC)
                        assert sPOOL.n == p_ct_c[NIT - 1], (sPOOL.n, p_ct_c[NIT - 1])

                        def _dveSC(e):
                            for it in range(NIT):
                                par = it % 2
                                e.wait_ge(sACT.h, a_cover_c[it])
                                e.wait_ge(sPOOL.h, p_dbu_c[it])
                                if it >= 2:
                                    e.wait_ge(sPOOL.h, p_ct_c[it - 2])
                                    e.wait_ge(sACT.h, a_ext_c[it - 2])
                                e.tensor_tensor_scan(out=s_sb[par][:], data0=da_sb[par],
                                                     data1=du_sb[par][:], initial=0.0,
                                                     op0=ALU.mult, op1=ALU.add)
                                sDVE.bump(e.tensor_copy(out=yt_sb[:, 0:4],
                                                        in_=yt_sb[:, 0:4]))
                                if it >= 1:
                                    j = it - 1
                                    e.wait_ge(sPOOL.h, p_ct_c[j])
                                    e.tensor_reduce(
                                        out=yt_sb[:, :],
                                        in_=ct_sb[:].rearrange("p (n t) -> p t n",
                                                               n=NSTG),
                                        axis=AX.X, op=ALU.add)
                                    jdc = j % 16
                                    sDVE.bump(e.tensor_add(
                                        out=y_sb[:, jdc * TC:(jdc + 1) * TC],
                                        in0=y_sb[:, jdc * TC:(jdc + 1) * TC],
                                        in1=yt_sb[:, :]))
                            j = NIT - 1
                            e.wait_ge(sPOOL.h, p_ct_c[j])
                            e.tensor_reduce(out=yt_sb[:, :],
                                            in_=ct_sb[:].rearrange("p (n t) -> p t n",
                                                                   n=NSTG),
                                            axis=AX.X, op=ALU.add)
                            sDVE.bump(e.tensor_add(out=y_sb[:, 15 * TC:16 * TC],
                                                   in0=y_sb[:, 15 * TC:16 * TC],
                                                   in1=yt_sb[:, :]))
                        sec("vector", _dveSC)
                        assert sPE.n == pe_mm_c[(1, 16)], (sPE.n, pe_mm_c[(1, 16)])

                    scan_end = d_acc_c[NIT - 1] if not MAMBA_NOSCAN else dve_u0
                    assert scan_end == sDVE.n, (scan_end, sDVE.n)

                    def _dveJ(e, scan_end=scan_end):
                        e.wait_ge(sDVE.h, scan_end)
                        e.tensor_mul(out=y_sb[:], in0=y_sb[:], in1=res_sb[:])
                        sDVE.bump(e.tensor_copy(out=xi2b_sb[:], in_=y_sb[:]))
                    sec("vector", _dveJ)
                    dve_ybf = sDVE.n
                    ybf_sb = xi2b_sb

                    op_pe0, op_dve0 = sPE.n, sDVE.n
                    op_ldb = [sLw[0].n, sLw[1].n]

                    def _syncO(e, op_pe0=op_pe0, op_dve0=op_dve0):
                        for m in range(8):
                            if SERIAL_W and m >= 1:
                                e.wait_ge(sDVE.h, op_dve0 + 4 * m)
                            elif m >= 2:
                                e.wait_ge(sPE.h, op_pe0 + m - 1)
                            sLw[m % 2].bump(e.dma_start(
                                out=wo_sb[m % 2][:].rearrange("p (c f) -> p c f", c=16),
                                in_=opb[m, :, :, :]), 16)
                    sec("sync", _syncO)

                    def _peG(e, dve_ybf=dve_ybf, op_ldb=op_ldb, op_dve0=op_dve0):
                        e.wait_ge(sDVE.h, dve_ybf)
                        for m in range(8):
                            pb = ps_w[m % 2]
                            e.wait_ge(sLw[m % 2].h, op_ldb[m % 2] + 16 * (m // 2 + 1))
                            if m >= 2:
                                e.wait_ge(sDVE.h, op_dve0 + 4 * (m - 1))
                            for kc in range(16):
                                inst = e.matmul(pb[:, 0:TC],
                                                wo_sb[m % 2][:, kc * 128:(kc + 1) * 128],
                                                ybf_sb[:, kc * TC:(kc + 1) * TC],
                                                start=(kc == 0), stop=(kc == 15))
                            sPE.bump(inst)
                        sPE.bump(e.matmul(ps_x[:, 0:64], ones_row_f[0:1, 0:96],
                                          ones_row_f[0:1, 0:64], start=True, stop=True))
                    sec("tensor", _peG)

                    def _dveK(e, op_pe0=op_pe0):
                        for m in range(8):
                            pb = ps_w[m % 2]
                            e.wait_ge(sPE.h, op_pe0 + min(m + 2, 9))
                            mb = bcst_sb[:, 0:TC]
                            rb = bcst_sb[:, TC:2 * TC]
                            yv = y_sb[:, m * TC:(m + 1) * TC]
                            e.tensor_sub(out=yv, in0=x_sb[:, m * TC:(m + 1) * TC], in1=mb)
                            e.tensor_mul(out=yv, in0=yv, in1=rb)
                            e.tensor_add(out=yv, in0=yv, in1=pb[:, 0:TC])
                            inst = e.tensor_add(out=x_sb[:, m * TC:(m + 1) * TC],
                                                in0=x_sb[:, m * TC:(m + 1) * TC],
                                                in1=yv)
                            sDVE.bump(inst, 4)
                    sec("vector", _dveK)
                    chunk_last_dve = sDVE.n

                    def _gpO(e, col0=col0, cld=chunk_last_dve, b=b, c=c):
                        e.wait_ge(sDVE.h, cld)
                        dv = resid.rearrange("(c p) t -> p c t", p=128)[:, :, col0:col0 + TC]
                        sSt.bump(e.dma_start(out=dv,
                                             in_=x_sb[:].rearrange("p (c t) -> p c t", c=8)), 16)
                        if DEBUG_TAPS and b == 0:
                            for nm, buf in (("xn", xn_sb), ("xi", xi_sb), ("res", res_sb),
                                            ("dlt", dlt_sb), ("u0", xi2_sb), ("y", y_sb),
                                            ("newx", x_sb), ("sub", y_sb),
                                            ("bcst", bcst_sb), ("row", row_sb),
                                            ("xld", None)):
                                if nm == "xld":
                                    sSt.bump(e.dma_start(out=taps[nm][c], in_=xld_sb), 16)
                                    continue
                                sSt.bump(e.dma_start(out=taps[nm][c], in_=buf[:]), 16)
                    sec("gpsimd", _gpO)
                    chunk_last_store = sSt.n

        def ffn():
            with ExitStack() as fs:
                xf = fs.enter_context(sbuf("f_xf", [128, 8 * FT], F32))
                xn = fs.enter_context(sbuf("f_xn", [128, 8 * FT], BF16))
                h_sb = fs.enter_context(sbuf("f_h", [128, 32 * FT], BF16))
                g_sb = fs.enter_context(sbuf("f_g", [128, 32 * FT], BF16))
                tf = [fs.enter_context(sbuf(f"f_tf{i}", [128, 2 * FT], F32))
                      for i in range(2)]
                rows = fs.enter_context(sbuf("f_rows", [1, 4 * FT], F32))
                bcast = fs.enter_context(sbuf("f_bcast", [128, 2 * FT], F32))
                wh = [fs.enter_context(sbuf(f"f_wh{i}", [128, 32 * 128], BF16))
                      for i in range(2)]
                psA = [fs.enter_context(psum(f"f_psA{i}", [128, FT], F32))
                       for i in range(2)]
                psB = [fs.enter_context(psum(f"f_psB{i}", [128, FT], F32))
                       for i in range(2)]
                p_sx = fs.enter_context(psum("f_pssx", [1, FT], F32))
                p_sq = fs.enter_context(psum("f_pssq", [1, FT], F32))
                p_b1 = fs.enter_context(psum("f_psb1", [128, FT], F32))
                p_b2 = fs.enter_context(psum("f_psb2", [128, FT], F32))

                prev_dve = sDVE.n
                prev_st = sSt.n

                for ti in range(NFT):
                    col0 = ti * FT
                    xbf = g_sb[:, 0:8 * FT]
                    sqbf = h_sb[:, 0:8 * FT]

                    def _syncx(e, col0=col0, pd=prev_dve, pst=prev_st):
                        e.wait_ge(sDVE.h, pd)
                        e.wait_ge(sSt.h, pst)
                        sv = resid.rearrange("(c p) t -> p c t", p=128)[:, :, col0:col0 + FT]
                        sL.bump(e.dma_start(out=xf[:].rearrange("p (c t) -> p c t", c=8),
                                            in_=sv), 16)
                    sec("sync", _syncx)

                    apply_done = emit_ln(FT, xf[:], xbf, sqbf, xn[:], rows[:], bcast[:],
                                         p_sx, p_sq, p_b1, p_b2, sL.n)

                    s_pe0, s_act0, s_dve0 = sPE.n, sACT.n, sDVE.n
                    s_ldb = [sLw[0].n, sLw[1].n]

                    def _syncW(e, s_pe0=s_pe0):
                        for m in range(32):
                            par = m % 2
                            if m >= 2:
                                e.wait_ge(sPE.h, s_pe0 + 2 * (m - 1))
                            e.dma_start(
                                out=wh[par][:, 0:1024].rearrange("p (c f) -> p c f", c=8),
                                in_=w1b[m, :, :, :]).then_inc(sLw[par].h, 16)
                            sLw[par].n += 16
                            sLw[par].bump(e.dma_start(
                                out=wh[par][:, 1024:2048].rearrange("p (c f) -> p c f", c=8),
                                in_=w2b[m, :, :, :]), 16)
                    sec("sync", _syncW)

                    def _peW(e, apply_done=apply_done, s_ldb=s_ldb, s_act0=s_act0):
                        e.wait_ge(sDVE.h, apply_done)
                        for m in range(32):
                            par = m % 2
                            e.wait_ge(sLw[par].h, s_ldb[par] + 32 * (m // 2 + 1))
                            if m >= 2:
                                e.wait_ge(sACT.h, s_act0 + 2 * (m - 1))
                            for kc in range(8):
                                e.matmul(psA[par][:, :], wh[par][:, kc * 128:(kc + 1) * 128],
                                         xn[:, kc * FT:(kc + 1) * FT],
                                         start=(kc == 0), stop=(kc == 7))
                            for kc in range(8):
                                inst = e.matmul(psB[par][:, :],
                                                wh[par][:, 1024 + kc * 128:1024 + (kc + 1) * 128],
                                                xn[:, kc * FT:(kc + 1) * FT],
                                                start=(kc == 0), stop=(kc == 7))
                            sPE.bump(inst, 2)
                    sec("tensor", _peW)

                    def _actW(e, s_pe0=s_pe0, s_dve0=s_dve0):
                        for m in range(32):
                            par = m % 2
                            e.wait_ge(sPE.h, s_pe0 + 2 * (m + 1))
                            if m >= 2:
                                e.wait_ge(sDVE.h, s_dve0 + m - 1)
                            mc = m
                            e.activation(out=tf[par][:, 0:FT], in_=psA[par][:, :], func=AF.Silu,
                                         scale=sp[:, SP_S1 + mc:SP_S1 + mc + 1])
                            inst = e.activation(out=tf[par][:, FT:2 * FT], in_=psB[par][:, :],
                                                func=AF.Copy, scale=sp[:, SP_S2 + mc:SP_S2 + mc + 1])
                            sACT.bump(inst, 2)
                    sec("scalar", _actW)

                    def _dveW(e, s_act0=s_act0):
                        for m in range(32):
                            par = m % 2
                            e.wait_ge(sACT.h, s_act0 + 2 * (m + 1))
                            sDVE.bump(e.tensor_mul(out=h_sb[:, m * FT:(m + 1) * FT],
                                                   in0=tf[par][:, 0:FT], in1=tf[par][:, FT:2 * FT]))
                    sec("vector", _dveW)
                    h_done = sDVE.n

                    def hstage(wsrc, rhs_sb, out_sb, scol, ready_dve, ready_act):
                        t_pe0, t_act0 = sPE.n, sACT.n
                        t_ldb = [sLw[0].n, sLw[1].n]

                        def _syncH(e, t_pe0=t_pe0):
                            for m in range(32):
                                par = m % 2
                                if m < 2:
                                    e.wait_ge(sPE.h, t_pe0)
                                else:
                                    e.wait_ge(sPE.h, t_pe0 + m - 1)
                                sLw[par].bump(e.dma_start(
                                    out=wh[par][:].rearrange("p (c f) -> p c f", c=32),
                                    in_=wsrc[m, :, :, :]), 16)
                        sec("sync", _syncH)

                        def _peH(e, t_ldb=t_ldb, t_act0=t_act0, ready_dve=ready_dve,
                                 ready_act=ready_act):
                            if ready_dve is not None:
                                e.wait_ge(sDVE.h, ready_dve)
                            if ready_act is not None:
                                e.wait_ge(sACT.h, ready_act)
                            for m in range(32):
                                par = m % 2
                                e.wait_ge(sLw[par].h, t_ldb[par] + 16 * (m // 2 + 1))
                                if m >= 2:
                                    e.wait_ge(sACT.h, t_act0 + m - 1)
                                for kc in range(32):
                                    inst = e.matmul(psA[par][:, :],
                                                    wh[par][:, kc * 128:(kc + 1) * 128],
                                                    rhs_sb[:, kc * FT:(kc + 1) * FT],
                                                    start=(kc == 0), stop=(kc == 31))
                                sPE.bump(inst)
                        sec("tensor", _peH)

                        def _actHs(e, t_pe0=t_pe0):
                            for m in range(32):
                                par = m % 2
                                e.wait_ge(sPE.h, t_pe0 + m + 1)
                                inst = e.activation(out=out_sb[:, m * FT:(m + 1) * FT],
                                                    in_=psA[par][:, :], func=AF.Gelu,
                                                    scale=sp[:, scol + m:scol + m + 1])
                                sACT.bump(inst)
                        sec("scalar", _actHs)
                        return sACT.n

                    g_done = hstage(h1b, h_sb, g_sb, SP_H1S, h_done, None)
                    hh_done = hstage(h2b, g_sb, h_sb, SP_H2S, None, g_done)

                    def _syncx2(e, col0=col0, apply_done=apply_done):
                        e.wait_ge(sDVE.h, apply_done)
                        sv = resid.rearrange("(c p) t -> p c t", p=128)[:, :, col0:col0 + FT]
                        sL.bump(e.dma_start(out=xf[:].rearrange("p (c t) -> p c t", c=8),
                                            in_=sv), 16)
                    sec("sync", _syncx2)
                    xre_ld = sL.n

                    w_pe0, w_act0, w_dve0 = sPE.n, sACT.n, sDVE.n
                    w_ldb = [sLw[0].n, sLw[1].n]

                    def _syncW3(e, w_pe0=w_pe0):
                        for m in range(8):
                            par = m % 2
                            if m < 2:
                                e.wait_ge(sPE.h, w_pe0)
                            else:
                                e.wait_ge(sPE.h, w_pe0 + m - 1)
                            sLw[par].bump(e.dma_start(
                                out=wh[par][:].rearrange("p (c f) -> p c f", c=32),
                                in_=w3b[m, :, :, :]), 16)
                    sec("sync", _syncW3)

                    def _peW3(e, hh_done=hh_done, w_ldb=w_ldb, w_act0=w_act0):
                        e.wait_ge(sACT.h, hh_done)
                        for m in range(8):
                            par = m % 2
                            e.wait_ge(sLw[par].h, w_ldb[par] + 16 * (m // 2 + 1))
                            if m >= 2:
                                e.wait_ge(sACT.h, w_act0 + m - 1)
                            for kc in range(32):
                                inst = e.matmul(psA[par][:, :],
                                                wh[par][:, kc * 128:(kc + 1) * 128],
                                                h_sb[:, kc * FT:(kc + 1) * FT],
                                                start=(kc == 0), stop=(kc == 31))
                            sPE.bump(inst)
                    sec("tensor", _peW3)

                    def _actW3(e, w_pe0=w_pe0, w_dve0=w_dve0):
                        for m in range(8):
                            par = m % 2
                            e.wait_ge(sPE.h, w_pe0 + m + 1)
                            if m >= 2:
                                e.wait_ge(sDVE.h, w_dve0 + m - 1)
                            inst = e.activation(out=tf[par][:, 0:FT], in_=psA[par][:, :],
                                                func=AF.Copy,
                                                scale=sp[:, SP_W3S + m:SP_W3S + m + 1])
                            sACT.bump(inst)
                    sec("scalar", _actW3)

                    def _dveW3(e, w_act0=w_act0, xre_ld=xre_ld):
                        e.wait_ge(sL.h, xre_ld)
                        for m in range(8):
                            par = m % 2
                            e.wait_ge(sACT.h, w_act0 + m + 1)
                            sDVE.bump(e.tensor_add(out=xf[:, m * FT:(m + 1) * FT],
                                                   in0=xf[:, m * FT:(m + 1) * FT],
                                                   in1=tf[par][:, 0:FT]))
                    sec("vector", _dveW3)
                    prev_dve = sDVE.n

                    def _gpS(e, col0=col0, pd=prev_dve):
                        e.wait_ge(sDVE.h, pd)
                        dv = resid.rearrange("(c p) t -> p c t", p=128)[:, :, col0:col0 + FT]
                        sSt.bump(e.dma_start(out=dv,
                                             in_=xf[:].rearrange("p (c t) -> p c t", c=8)), 16)
                    sec("gpsimd", _gpS)
                    prev_st = sSt.n

        def attn(b):
            with ExitStack() as ats:
                ln3 = ats.enter_context(sbuf("a_ln3", [128, 8 * S], BF16))
                qb = ats.enter_context(sbuf("a_q", [128, 8 * S], BF16))
                kb = ats.enter_context(sbuf("a_k", [128, 8 * S], BF16))
                vb = ats.enter_context(sbuf("a_v", [128, 16 * 1024], BF16))
                s_f = ats.enter_context(sbuf("a_s", [128, S], F32))
                p_bf = ats.enter_context(sbuf("a_p", [128, S], BF16))
                pt_bf = ats.enter_context(sbuf("a_pt", [128, 16 * 128], BF16))
                o1 = ats.enter_context(sbuf("a_o1", [128, 8 * FT], BF16))
                rf = ats.enter_context(sbuf("a_rf", [128, 8 * FT], F32))
                nm = ats.enter_context(sbuf("a_nm", [128, 2], F32))
                ws = [ats.enter_context(sbuf(f"a_ws{i}", [128, 8 * 128], BF16))
                      for i in range(2)]
                wv_sb = ats.enter_context(sbuf("a_wv", [128, 8 * 1024], BF16))

                with ExitStack() as lns:
                    rows = lns.enter_context(sbuf("a_rows", [1, 4 * FT], F32))
                    bcast = lns.enter_context(sbuf("a_bcast", [128, 2 * FT], F32))
                    p_sx = lns.enter_context(psum("a_pssx", [1, FT], F32))
                    p_sq = lns.enter_context(psum("a_pssq", [1, FT], F32))
                    p_b1 = lns.enter_context(psum("a_psb1", [128, FT], F32))
                    p_b2 = lns.enter_context(psum("a_psb2", [128, FT], F32))
                    prev_apply = None
                    for sl in range(4):
                        col0 = b * S + sl * FT

                        def _syncx(e, col0=col0, pa=prev_apply):
                            if pa is not None:
                                e.wait_ge(sDVE.h, pa)
                            sv = resid.rearrange("(c p) t -> p c t", p=128)[:, :, col0:col0 + FT]
                            sL.bump(e.dma_start(out=rf[:].rearrange("p (c t) -> p c t", c=8),
                                                in_=sv), 16)
                        sec("sync", _syncx)
                        prev_apply = emit_ln(FT, rf[:], qb[:, 0:8 * FT], kb[:, 0:8 * FT],
                                             ln3[:], rows[:], bcast[:], p_sx, p_sq, p_b1, p_b2,
                                             sL.n, out_off=sl * FT, out_stride=S)
                    ln_done = prev_apply

                with ExitStack() as qs:
                    ps_s = [qs.enter_context(psum(f"a_pss{i}", [128, FT], F32))
                            for i in range(2)]
                    ps_t = [qs.enter_context(psum(f"a_pst{i}", [128, 128], BF16))
                            for i in range(2)]
                    ps_av = [qs.enter_context(psum(f"a_psav{i}", [128, 128], F32))
                             for i in range(2)]
                    ps_o = [qs.enter_context(psum(f"a_pso{i}", [128, FT], F32))
                            for i in range(2)]

                    def qk_stage(wsrc, out_sb, scale):
                        t_pe0, t_act0 = sPE.n, sACT.n
                        t_ldb = [sLw[0].n, sLw[1].n]

                        def _syncq(e, t_pe0=t_pe0):
                            for dkc in range(8):
                                par = dkc % 2
                                if dkc < 2:
                                    e.wait_ge(sPE.h, t_pe0)
                                else:
                                    e.wait_ge(sPE.h, t_pe0 + 4 * (dkc - 1))
                                sLw[par].bump(e.dma_start(
                                    out=ws[par][:].rearrange("p (c f) -> p c f", c=8),
                                    in_=wsrc[dkc, :, :, :]), 16)
                        sec("sync", _syncq)

                        def _peq(e, t_ldb=t_ldb, t_act0=t_act0):
                            e.wait_ge(sDVE.h, ln_done)
                            for dkc in range(8):
                                par = dkc % 2
                                e.wait_ge(sLw[par].h, t_ldb[par] + 16 * (dkc // 2 + 1))
                                for ts in range(4):
                                    u = dkc * 4 + ts
                                    if u >= 2:
                                        e.wait_ge(sACT.h, t_act0 + u - 1)
                                    for kc in range(8):
                                        inst = e.matmul(ps_s[u % 2][:, :],
                                                        ws[par][:, kc * 128:(kc + 1) * 128],
                                                        ln3[:, kc * S + ts * FT: kc * S + (ts + 1) * FT],
                                                        start=(kc == 0), stop=(kc == 7))
                                    sPE.bump(inst)
                        sec("tensor", _peq)

                        def _actq(e, t_pe0=t_pe0):
                            for dkc in range(8):
                                for ts in range(4):
                                    u = dkc * 4 + ts
                                    e.wait_ge(sPE.h, t_pe0 + u + 1)
                                    inst = e.activation(
                                        out=out_sb[:, dkc * S + ts * FT: dkc * S + (ts + 1) * FT],
                                        in_=ps_s[u % 2][:, :], func=AF.Copy, scale=scale)
                                    sACT.bump(inst)
                        sec("scalar", _actq)

                    qk_stage(wqb, qb, 0.03125)
                    qk_stage(wkb, kb, 1.0)

                    v_pe0, v_act0 = sPE.n, sACT.n

                    def _syncv(e):
                        for kc in range(8):
                            sLw[kc % 2].bump(e.dma_start(out=wv_sb[:, kc * 1024:(kc + 1) * 1024],
                                                         in_=wvb[kc * 128:(kc + 1) * 128, :]), 16)
                    sec("sync", _syncv)

                    v_ldt = [sLw[0].n, sLw[1].n]

                    def _pev(e, v_ldt=v_ldt, v_act0=v_act0):
                        e.wait_ge(sLw[0].h, v_ldt[0])
                        e.wait_ge(sLw[1].h, v_ldt[1])
                        for tt in range(16):
                            for half in range(2):
                                u = tt * 2 + half
                                if u < 2:
                                    e.wait_ge(sACT.h, v_act0)
                                else:
                                    e.wait_ge(sACT.h, v_act0 + u - 1)
                                for kc in range(8):
                                    inst = e.matmul(ps_s[u % 2][:, :],
                                                    ln3[:, kc * S + tt * 128: kc * S + (tt + 1) * 128],
                                                    wv_sb[:, kc * 1024 + half * 512: kc * 1024 + (half + 1) * 512],
                                                    start=(kc == 0), stop=(kc == 7))
                                sPE.bump(inst)
                    sec("tensor", _pev)

                    def _actv(e, v_pe0=v_pe0):
                        for tt in range(16):
                            for half in range(2):
                                u = tt * 2 + half
                                e.wait_ge(sPE.h, v_pe0 + u + 1)
                                inst = e.activation(
                                    out=vb[:, tt * 1024 + half * 512: tt * 1024 + (half + 1) * 512],
                                    in_=ps_s[u % 2][:, :], func=AF.Copy)
                                sACT.bump(inst)
                    sec("scalar", _actv)
                    v_done = sACT.n

                    prev_store = sSt.n
                    prev_rf_dve = sDVE.n
                    for qg in range(4):
                        o1_ev_last = None
                        for qq in range(4):
                            qi = qg * 4 + qq
                            valid = (qi + 1) * 128
                            nks = (valid + FT - 1) // FT

                            q_pe0, q_act0 = sPE.n, sACT.n

                            def _peS(e, qi=qi, valid=valid, nks=nks, q_act0=q_act0,
                                     vd=v_done):
                                if qi == 0:
                                    e.wait_ge(sACT.h, vd)
                                for ks in range(nks):
                                    w = min(FT, valid - ks * FT)
                                    if ks < 2:
                                        e.wait_ge(sACT.h, q_act0)
                                    else:
                                        e.wait_ge(sACT.h, q_act0 + ks - 1)
                                    for kc in range(8):
                                        inst = e.matmul(
                                            ps_s[ks % 2][:, 0:w],
                                            qb[:, kc * S + qi * 128: kc * S + (qi + 1) * 128],
                                            kb[:, kc * S + ks * FT: kc * S + ks * FT + w],
                                            start=(kc == 0), stop=(kc == 7))
                                    sPE.bump(inst)
                            sec("tensor", _peS)

                            def _actS(e, qi=qi, valid=valid, nks=nks, q_pe0=q_pe0,
                                      prd=prev_rf_dve):
                                e.wait_ge(sDVE.h, prd)
                                for ks in range(nks):
                                    w = min(FT, valid - ks * FT)
                                    e.wait_ge(sPE.h, q_pe0 + ks + 1)
                                    inst = e.activation(out=s_f[:, ks * FT: ks * FT + w],
                                                        in_=ps_s[ks % 2][:, 0:w], func=AF.Copy)
                                    sACT.bump(inst)
                            sec("scalar", _actS)
                            s_ev = sACT.n

                            def _dveS(e, qi=qi, valid=valid, s_ev=s_ev):
                                e.wait_ge(sACT.h, s_ev)
                                e.tensor_add(out=s_f[:, qi * 128:(qi + 1) * 128],
                                             in0=s_f[:, qi * 128:(qi + 1) * 128], in1=mask_sb[:])
                                sDVE.bump(e.tensor_reduce(out=nm[:, 0:1], in_=s_f[:, 0:valid],
                                                          axis=AX.X, op=ALU.max, negate=True))
                            sec("vector", _dveS)
                            nm_done = sDVE.n

                            def _actE2(e, valid=valid, nm_done=nm_done):
                                e.wait_ge(sDVE.h, nm_done)
                                inst = e.activation(out=p_bf[:, 0:valid], in_=s_f[:, 0:valid],
                                                    func=AF.Exp, bias=nm[:, 0:1],
                                                    accum_out=nm[:, 1:2])
                                sACT.bump(inst)
                            sec("scalar", _actE2)
                            exp_done = sACT.n

                            rc_cnt = sDVE.n + 1

                            def _dveN(e, valid=valid, exp_done=exp_done, rc_cnt=rc_cnt):
                                e.wait_ge(sACT.h, exp_done)
                                sDVE.bump(e.reciprocal(out=nm[:, 0:1], in_=nm[:, 1:2]))
                                e.wait_ge(sDVE.h, rc_cnt)
                                sDVE.bump(e.tensor_scalar_mul(out=p_bf[:, 0:valid],
                                                              in0=p_bf[:, 0:valid],
                                                              scalar1=nm[:, 0:1]))
                            sec("vector", _dveN)
                            pn_done = sDVE.n

                            t_pe0, t_dve0 = sPE.n, sDVE.n

                            def _peT(e, qi=qi, pn_done=pn_done, t_dve0=t_dve0):
                                e.wait_ge(sDVE.h, pn_done)
                                for kc in range(qi + 1):
                                    if kc >= 2:
                                        e.wait_ge(sDVE.h, t_dve0 + kc - 1)
                                    sPE.bump(e.transpose(ps_t[kc % 2][:, :],
                                                         p_bf[:, kc * 128:(kc + 1) * 128],
                                                         ident_b[:]))
                                sPE.bump(e.transpose(ps_t[qi % 2][:, :],
                                                     p_bf[:, qi * 128:(qi + 1) * 128],
                                                     ident_b[:]))
                            sec("tensor", _peT)

                            def _dveT(e, qi=qi, t_pe0=t_pe0):
                                for kc in range(qi + 1):
                                    e.wait_ge(sPE.h, t_pe0 + min(kc + 2, qi + 2))
                                    sDVE.bump(e.tensor_copy(out=pt_bf[:, kc * 128:(kc + 1) * 128],
                                                            in_=ps_t[kc % 2][:, :]))
                            sec("vector", _dveT)
                            pt_done = sDVE.n

                            a_pe0, a_act0 = sPE.n, sACT.n

                            def _peAV(e, qi=qi, qq=qq, pt_done=pt_done, a_act0=a_act0):
                                e.wait_ge(sDVE.h, pt_done)
                                for dvc in range(8):
                                    if dvc < 2:
                                        e.wait_ge(sACT.h, a_act0)
                                    else:
                                        e.wait_ge(sACT.h, a_act0 + dvc - 1)
                                    for kc in range(qi + 1):
                                        inst = e.matmul(ps_av[dvc % 2][:, :],
                                                        vb[:, kc * 1024 + dvc * 128: kc * 1024 + (dvc + 1) * 128],
                                                        pt_bf[:, kc * 128:(kc + 1) * 128],
                                                        start=(kc == 0), stop=(kc == qi))
                                    sPE.bump(inst)
                            sec("tensor", _peAV)

                            def _actAV(e, qq=qq, a_pe0=a_pe0):
                                for dvc in range(8):
                                    e.wait_ge(sPE.h, a_pe0 + dvc + 1)
                                    inst = e.activation(
                                        out=o1[:, dvc * FT + qq * 128: dvc * FT + (qq + 1) * 128],
                                        in_=ps_av[dvc % 2][:, :], func=AF.Copy)
                                    sACT.bump(inst)
                            sec("scalar", _actAV)
                            o1_ev_last = sACT.n

                        def _syncrf(e, qg=qg, ps=prev_store, prd=prev_rf_dve):
                            e.wait_ge(sSt.h, ps)
                            e.wait_ge(sDVE.h, prd)
                            sv = resid.rearrange("(c p) t -> p c t", p=128)[
                                :, :, b * S + qg * FT: b * S + (qg + 1) * FT]
                            sL.bump(e.dma_start(out=rf[:].rearrange("p (c t) -> p c t", c=8),
                                                in_=sv), 16)
                        sec("sync", _syncrf)
                        rf_ld = sL.n

                        o_pe0, o_act0, o_dve0 = sPE.n, sACT.n, sDVE.n
                        o_ldb = [sLw[0].n, sLw[1].n]

                        def _syncWo(e, o_pe0=o_pe0):
                            for dmc in range(8):
                                par = dmc % 2
                                if dmc < 2:
                                    e.wait_ge(sPE.h, o_pe0)
                                else:
                                    e.wait_ge(sPE.h, o_pe0 + dmc - 1)
                                sLw[par].bump(e.dma_start(
                                    out=ws[par][:].rearrange("p (c f) -> p c f", c=8),
                                    in_=wob[dmc, :, :, :]), 16)
                        sec("sync", _syncWo)

                        def _peWo(e, o1_ev_last=o1_ev_last, o_ldb=o_ldb, o_act0=o_act0):
                            e.wait_ge(sACT.h, o1_ev_last)
                            for dmc in range(8):
                                par = dmc % 2
                                e.wait_ge(sLw[par].h, o_ldb[par] + 16 * (dmc // 2 + 1))
                                if dmc >= 2:
                                    e.wait_ge(sACT.h, o_act0 + dmc - 1)
                                for dvc in range(8):
                                    inst = e.matmul(ps_o[dmc % 2][:, :],
                                                    ws[par][:, dvc * 128:(dvc + 1) * 128],
                                                    o1[:, dvc * FT:(dvc + 1) * FT],
                                                    start=(dvc == 0), stop=(dvc == 7))
                                sPE.bump(inst)
                        sec("tensor", _peWo)

                        def _actWo(e, o_pe0=o_pe0, o_dve0=o_dve0):
                            for dmc in range(8):
                                par = dmc % 2
                                e.wait_ge(sPE.h, o_pe0 + dmc + 1)
                                if dmc >= 2:
                                    e.wait_ge(sDVE.h, o_dve0 + dmc - 1)
                                inst = e.activation(out=s_f[:, (dmc % 2) * FT:(dmc % 2) * FT + FT],
                                                    in_=ps_o[dmc % 2][:, :], func=AF.Copy)
                                sACT.bump(inst)
                        sec("scalar", _actWo)

                        def _dveWo(e, o_act0=o_act0, rf_ld=rf_ld):
                            e.wait_ge(sL.h, rf_ld)
                            for dmc in range(8):
                                e.wait_ge(sACT.h, o_act0 + dmc + 1)
                                sDVE.bump(e.tensor_add(
                                    out=rf[:, dmc * FT:(dmc + 1) * FT],
                                    in0=rf[:, dmc * FT:(dmc + 1) * FT],
                                    in1=s_f[:, (dmc % 2) * FT:(dmc % 2) * FT + FT]))
                        sec("vector", _dveWo)
                        prev_rf_dve = sDVE.n

                        def _gpWo(e, qg=qg, prd=prev_rf_dve):
                            e.wait_ge(sDVE.h, prd)
                            dv = resid.rearrange("(c p) t -> p c t", p=128)[
                                :, :, b * S + qg * FT: b * S + (qg + 1) * FT]
                            sSt.bump(e.dma_start(out=dv,
                                                 in_=rf[:].rearrange("p (c t) -> p c t", c=8)), 16)
                        sec("gpsimd", _gpWo)
                        prev_store = sSt.n

        parts = BUILD_PARTS
        p0()
        if "cast" in parts:
            cast_weights()
        if "ingest" in parts:
            ingest()
        if "barrier" in parts:
            barrier()
        if stages >= 2:
            for b in range(BLOC):
                mamba(b)
                barrier()
        if stages >= 3:
            ffn()
            barrier()
        if stages >= 4:
            for b in range(BLOC):
                attn(b)
                barrier()
        if stages >= 5:
            ffn()
            barrier()
        output_phase()

    return nc


def _sp_pack(inp):
    sp = np.zeros((128, SPC), np.float32)
    cw = np.asarray(inp["L0_attn_a_conv_w"], np.float32).reshape(DI, 4)
    for j in range(4):
        sp[:, SP_CW + j * 16: SP_CW + (j + 1) * 16] = cw[:, j].reshape(16, 128).T
    sp[:, SP_CB:SP_CB + 16] = np.asarray(inp["L0_attn_a_conv_b"], np.float32).reshape(16, 128).T
    sp[:, SP_DTB:SP_DTB + 16] = np.asarray(inp["L0_attn_a_dt_proj_b"], np.float32).reshape(16, 128).T
    sp[:, SP_D:SP_D + 16] = np.asarray(inp["L0_attn_a_D"], np.float32).reshape(16, 128).T
    al = np.asarray(inp["L0_attn_a_A_log"], np.float32).reshape(16, 128, NST)
    sp[:, SP_AL:SP_AL + 256] = al.transpose(1, 0, 2).reshape(128, 256)
    sp[:, SP_NW:SP_NW + 8] = np.asarray(inp["L0_attn_a_norm_w"], np.float32).reshape(8, 128).T
    sp[:, SP_S1:SP_S1 + 32] = np.asarray(inp["L0_ffn_w1_s"], np.float32).reshape(32, 128).T
    sp[:, SP_S2:SP_S2 + 32] = np.asarray(inp["L0_ffn_w2_s"], np.float32).reshape(32, 128).T
    sp[:, SP_H1S:SP_H1S + 32] = np.asarray(inp["L0_ffn_h1_s"], np.float32).reshape(32, 128).T
    sp[:, SP_H2S:SP_H2S + 32] = np.asarray(inp["L0_ffn_h2_s"], np.float32).reshape(32, 128).T
    sp[:, SP_W3S:SP_W3S + 8] = np.asarray(inp["L0_ffn_w3_s"], np.float32).reshape(8, 128).T
    return sp


def kernel(**inp):
    x = np.ascontiguousarray(np.asarray(inp["L0_attn_a_x"], np.float32))
    sp = _sp_pack(inp)
    shared = dict(
        sp_in=sp,
        ipw=np.ascontiguousarray(np.asarray(inp["L0_attn_a_in_proj_w"], np.float32)),
        xpw=np.ascontiguousarray(np.asarray(inp["L0_attn_a_x_proj_w"], np.float32)),
        dpw=np.ascontiguousarray(np.asarray(inp["L0_attn_a_dt_proj_w"], np.float32)),
        opw=np.ascontiguousarray(np.asarray(inp["L0_attn_a_out_proj_w"], np.float32)),
        wq=np.ascontiguousarray(np.asarray(inp["L0_attn_b_Wq"], np.float32)),
        wk=np.ascontiguousarray(np.asarray(inp["L0_attn_b_Wk"], np.float32)),
        wv=np.ascontiguousarray(np.asarray(inp["L0_attn_b_Wv"], np.float32)),
        wo=np.ascontiguousarray(np.asarray(inp["L0_attn_b_Wo"], np.float32)),
        w1t=np.ascontiguousarray(np.asarray(inp["L0_ffn_w1_w"], np.float32).T),
        w2t=np.ascontiguousarray(np.asarray(inp["L0_ffn_w2_w"], np.float32).T),
        h1t=np.ascontiguousarray(np.asarray(inp["L0_ffn_h1_w"], np.float32).T),
        h2t=np.ascontiguousarray(np.asarray(inp["L0_ffn_h2_w"], np.float32).T),
        w3t=np.ascontiguousarray(np.asarray(inp["L0_ffn_w3_w"], np.float32).T),
    )
    nc = build(STAGES)
    in_maps = []
    for c in range(NCORE):
        m = dict(shared)
        m["x_in"] = np.ascontiguousarray(x[c * BLOC:(c + 1) * BLOC].reshape(TOK, D))
        in_maps.append(m)
    res = run_bass_kernel_spmd(nc, in_maps, core_ids=list(range(NCORE)))
    out = np.concatenate(
        [np.asarray(r["out"], np.float32).reshape(BLOC, S, D) for r in res.results], axis=0)
    return out


# version 03
# speedup vs baseline: 1.0698x; 1.0698x over previous
import numpy as np
from contextlib import ExitStack

import concourse.bass as bass
import concourse.mybir as mybir
from concourse.bass_utils import run_bass_kernel_spmd

F32 = mybir.dt.float32
BF16 = mybir.dt.bfloat16
AF = mybir.ActivationFunctionType
ALU = mybir.AluOpType
AX = mybir.AxisListType

NCORE = 8
BLOC = 2
S = 2048
D = 1024
DI = 2048
NST = 16
DTR = 64
DH = 4096
TOK = BLOC * S
EPS = 1e-5

TC = 256
NCH = S // TC
SEG = TC + 1
NG = 4
NSTG = 8
NIT = 32
FT = 512
NFT = TOK // FT

SP_CW = 0
SP_CB = 64
SP_DTB = 80
SP_D = 96
SP_AL = 112
SP_NW = 368
SP_S1 = 376
SP_S2 = 408
SP_H1S = 440
SP_H2S = 472
SP_W3S = 504
SPC = 512

STAGES = 5
BUILD_PARTS = ("cast", "ingest", "barrier")
PARANOID = False
MAMBA_NOSCAN = False
DBU_DVE = False
import os as _os
SCAN_DELAY = _os.environ.get("SCAN_DELAY", "")
DEBUG_TAPS = False
SERIAL_W = False


class Sem:
    def __init__(self, nc, es, name):
        self.h = es.enter_context(nc.semaphore(name))
        self.n = 0

    def bump(self, inst, k=1):
        inst.then_inc(self.h, k)
        self.n += k
        return inst


def build(stages=STAGES):
    nc = bass.Bass("TRN2", target_bir_lowering=False, detect_race_conditions=False)

    x_in = nc.dram_tensor("x_in", [TOK, D], F32, kind="ExternalInput").ap()
    sp_in = nc.dram_tensor("sp_in", [128, SPC], F32, kind="ExternalInput").ap()
    ipw = nc.dram_tensor("ipw", [D, 2 * DI], F32, kind="ExternalInput").ap()
    xpw = nc.dram_tensor("xpw", [DI, DTR + 2 * NST], F32, kind="ExternalInput").ap()
    dpw = nc.dram_tensor("dpw", [DTR, DI], F32, kind="ExternalInput").ap()
    opw = nc.dram_tensor("opw", [DI, D], F32, kind="ExternalInput").ap()
    wq = nc.dram_tensor("wq", [D, D], F32, kind="ExternalInput").ap()
    wk = nc.dram_tensor("wk", [D, D], F32, kind="ExternalInput").ap()
    wv = nc.dram_tensor("wv", [D, D], F32, kind="ExternalInput").ap()
    wo = nc.dram_tensor("wo", [D, D], F32, kind="ExternalInput").ap()
    w1t = nc.dram_tensor("w1t", [D, DH], F32, kind="ExternalInput").ap()
    w2t = nc.dram_tensor("w2t", [D, DH], F32, kind="ExternalInput").ap()
    h1t = nc.dram_tensor("h1t", [DH, DH], F32, kind="ExternalInput").ap()
    h2t = nc.dram_tensor("h2t", [DH, DH], F32, kind="ExternalInput").ap()
    w3t = nc.dram_tensor("w3t", [DH, D], F32, kind="ExternalInput").ap()

    out_t = nc.dram_tensor("out", [TOK, D], F32, kind="ExternalOutput").ap()

    resid = nc.dram_tensor("resid", [D, TOK], F32, kind="Internal").ap()
    ipb = nc.dram_tensor("ipb", [32, 128, 8, 128], BF16, kind="Internal").ap()
    opb = nc.dram_tensor("opb", [8, 128, 16, 128], BF16, kind="Internal").ap()
    wqb = nc.dram_tensor("wqb", [8, 128, 8, 128], BF16, kind="Internal").ap()
    wkb = nc.dram_tensor("wkb", [8, 128, 8, 128], BF16, kind="Internal").ap()
    wob = nc.dram_tensor("wob", [8, 128, 8, 128], BF16, kind="Internal").ap()
    wvb = nc.dram_tensor("wvb", [D, D], BF16, kind="Internal").ap()
    w1b = nc.dram_tensor("w1b", [32, 128, 8, 128], BF16, kind="Internal").ap()
    w2b = nc.dram_tensor("w2b", [32, 128, 8, 128], BF16, kind="Internal").ap()
    h1b = nc.dram_tensor("h1b", [32, 128, 32, 128], BF16, kind="Internal").ap()
    h2b = nc.dram_tensor("h2b", [32, 128, 32, 128], BF16, kind="Internal").ap()
    w3b = nc.dram_tensor("w3b", [8, 128, 32, 128], BF16, kind="Internal").ap()

    taps = {}
    if DEBUG_TAPS:
        taps["xn"] = nc.dram_tensor("tap_xn", [NCH, 128, 8 * TC], BF16, kind="ExternalOutput").ap()
        taps["xi"] = nc.dram_tensor("tap_xi", [NCH, 128, 16 * (TC + 3)], F32, kind="ExternalOutput").ap()
        taps["res"] = nc.dram_tensor("tap_res", [NCH, 128, 16 * TC], F32, kind="ExternalOutput").ap()
        taps["dlt"] = nc.dram_tensor("tap_dlt", [NCH, 128, 16 * TC], F32, kind="ExternalOutput").ap()
        taps["u0"] = nc.dram_tensor("tap_u0", [NCH, 128, 16 * TC], F32, kind="ExternalOutput").ap()
        taps["y"] = nc.dram_tensor("tap_y", [NCH, 128, 16 * TC], F32, kind="ExternalOutput").ap()
        taps["newx"] = nc.dram_tensor("tap_newx", [NCH, 128, 8 * TC], F32, kind="ExternalOutput").ap()
        taps["sub"] = nc.dram_tensor("tap_sub", [NCH, 128, 8 * TC], F32, kind="ExternalOutput").ap()
        taps["bcst"] = nc.dram_tensor("tap_bcst", [NCH, 128, 3 * TC], F32, kind="ExternalOutput").ap()
        taps["row"] = nc.dram_tensor("tap_row", [NCH, 1, 6 * TC], F32, kind="ExternalOutput").ap()
        taps["xld"] = nc.dram_tensor("tap_xld", [NCH, 128, 4 * TC], F32, kind="ExternalOutput").ap()

    es = ExitStack()
    with es:
        sp = es.enter_context(nc.sbuf_tensor("sp", [128, SPC], F32))
        a_sb = es.enter_context(nc.sbuf_tensor("a_sb", [128, 256], F32))
        ident_f = es.enter_context(nc.sbuf_tensor("ident_f", [128, 128], F32))
        ident_b = es.enter_context(nc.sbuf_tensor("ident_b", [128, 128], BF16))
        mask_sb = es.enter_context(nc.sbuf_tensor("mask_sb", [128, 128], F32))
        ones_col_b = es.enter_context(nc.sbuf_tensor("ones_col_b", [128, 1], BF16))
        ones_row_f = es.enter_context(nc.sbuf_tensor("ones_row_f", [1, 128], F32))

        sL = Sem(nc, es, "sL")
        sLw = [Sem(nc, es, "sLw0"), Sem(nc, es, "sLw1")]
        sSt = Sem(nc, es, "sSt")
        sStp = [Sem(nc, es, "sStp0"), Sem(nc, es, "sStp1")]
        sPE = Sem(nc, es, "sPE")
        sACT = Sem(nc, es, "sACT")
        sDVE = Sem(nc, es, "sDVE")
        sPOOL = Sem(nc, es, "sPOOL")
        sBar = Sem(nc, es, "sBar")

        _uid = [0]

        def sbuf(name, shape, dtype):
            _uid[0] += 1
            return nc.sbuf_tensor(f"{name}_{_uid[0]}", shape, dtype)

        def psum(name, shape, dtype=F32):
            _uid[0] += 1
            return nc.psum_tensor(f"{name}_{_uid[0]}", shape, dtype)

        block = es.enter_context(nc.Block())
        ENG = dict(tensor=block.tensor, vector=block.vector, scalar=block.scalar,
                   gpsimd=block.gpsimd, sync=block.sync)

        _in_barrier = [False]

        def sec(name, fn):
            ENG[name](fn)
            if PARANOID and not _in_barrier[0]:
                barrier()

        def barrier():
            _in_barrier[0] = True
            for en in ("tensor", "vector", "scalar", "gpsimd"):
                def _f(e):
                    e.drain()
                    e.sem_inc(sBar.h, 1)
                sec(en, _f)
                sBar.n += 1
            tgt, pl = sBar.n, sPOOL.n
            dma_tot = [(s.h, s.n) for s in (sL, sLw[0], sLw[1], sSt, sStp[0], sStp[1])]
            for en in ("tensor", "vector", "scalar", "gpsimd", "sync"):
                def _f(e, tgt=tgt, pl=pl, dma_tot=dma_tot):
                    e.wait_ge(sBar.h, tgt)
                    for h, v in dma_tot:
                        e.wait_ge(h, v)
                    e.wait_ge(sPOOL.h, pl)
                sec(en, _f)
            _in_barrier[0] = False

        def p0():
            def _sync(e):
                sL.bump(e.dma_start(out=sp[:], in_=sp_in), 16)
            sec("sync", _sync)
            ld_sp = sL.n

            def _gp(e):
                e.memset(ident_f[:], 0.0)
                e.memset(ident_b[:], 0.0)
                e.memset(mask_sb[:], 0.0)
                e.memset(ones_col_b[:], 1.0)
                e.memset(ones_row_f[:], 1.0)
                e.affine_select(out=ident_f[:], in_=ident_f[:], compare_op=ALU.not_equal,
                                fill=1.0, base=0, pattern=[[-1, 128]], channel_multiplier=1)
                e.affine_select(out=ident_b[:], in_=ident_b[:], compare_op=ALU.not_equal,
                                fill=1.0, base=0, pattern=[[-1, 128]], channel_multiplier=1)
                inst = e.affine_select(out=mask_sb[:], in_=mask_sb[:], compare_op=ALU.is_ge,
                                       fill=-3.0e38, base=0, pattern=[[-1, 128]], channel_multiplier=1)
                sPOOL.bump(inst)
            sec("gpsimd", _gp)

            def _act(e, ld_sp=ld_sp):
                e.wait_ge(sL.h, ld_sp)
                sACT.bump(e.activation(out=a_sb[:], in_=sp[:, SP_AL:SP_AL + 256], func=AF.Exp))
            sec("scalar", _act)
            act_a = sACT.n

            def _dve(e, act_a=act_a):
                e.wait_ge(sACT.h, act_a)
                sDVE.bump(e.tensor_scalar_mul(out=a_sb[:], in0=a_sb[:], scalar1=-1.0))
            sec("vector", _dve)

        def cast_weights():
            jobs = [(ipw, ipb, 8, 4096), (opw, opb, 16, 1024)]
            if stages >= 4:
                jobs += [(wq, wqb, 8, 1024), (wk, wkb, 8, 1024), (wo, wob, 8, 1024),
                         (wv, None, 8, 1024)]
            if stages >= 3:
                jobs += [(w1t, w1b, 8, 4096), (w2t, w2b, 8, 4096),
                         (h1t, h1b, 32, 4096), (h2t, h2b, 32, 4096), (w3t, w3b, 32, 1024)]
            tiles = []
            for src, dst, KC, M in jobs:
                for kc in range(KC):
                    tiles.append((src, dst, kc, M))
            n = len(tiles)
            with (sbuf("castf", [128, 2 * 4096], F32) as castf,
                  sbuf("castb", [128, 2 * 4096], BF16) as castb):
                dve0 = sDVE.n
                ldb = [sLw[0].n, sLw[1].n]
                stb = [sStp[0].n, sStp[1].n]
                st_counts = []
                acc = [stb[0], stb[1]]
                for i, (src, dst, kc, M) in enumerate(tiles):
                    n16 = 16 if dst is None else 16 * ((M // 128 + 7) // 8)
                    acc[i % 2] += n16
                    st_counts.append(acc[i % 2])

                def _sync(e):
                    for i, (src, dst, kc, M) in enumerate(tiles):
                        par = i % 2
                        if i >= 2:
                            e.wait_ge(sDVE.h, dve0 + (i - 1))
                        sLw[par].bump(e.dma_start(out=castf[:, par * 4096: par * 4096 + M],
                                                  in_=src[kc * 128:(kc + 1) * 128, :]), 16)
                sec("sync", _sync)

                def _dve(e):
                    for i, (src, dst, kc, M) in enumerate(tiles):
                        par = i % 2
                        e.wait_ge(sLw[par].h, ldb[par] + 16 * (i // 2 + 1))
                        if i >= 2:
                            e.wait_ge(sStp[par].h, st_counts[i - 2])
                        sDVE.bump(e.tensor_copy(out=castb[:, par * 4096: par * 4096 + M],
                                                in_=castf[:, par * 4096: par * 4096 + M]))
                sec("vector", _dve)

                def _gp(e):
                    for i, (src, dst, kc, M) in enumerate(tiles):
                        par = i % 2
                        e.wait_ge(sDVE.h, dve0 + i + 1)
                        if dst is None:
                            sStp[par].bump(e.dma_start(out=wvb[kc * 128:(kc + 1) * 128, :],
                                                       in_=castb[:, par * 4096: par * 4096 + M]), 16)
                        else:
                            MC = M // 128
                            for m0 in range(0, MC, 8):
                                mn = min(8, MC - m0)
                                dv = dst[m0:m0 + mn, :, kc:kc + 1, :].transpose([1, 0, 2, 3])
                                sv = castb[:, par * 4096 + m0 * 128: par * 4096 + (m0 + mn) * 128]
                                sv = sv.rearrange("p (a b) -> p a b", b=128).unsqueeze(2)
                                sStp[par].bump(e.dma_start(out=dv, in_=sv), 16)
                sec("gpsimd", _gp)

        def ingest():
            with (sbuf("ing_f", [128, 2 * 1024], F32) as ing_f,
                  sbuf("ing_o", [128, 2 * 1024], F32) as ing_o,
                  psum("ps_ing0", [128, 512], F32) as ps0,
                  psum("ps_ing1", [128, 512], F32) as ps1,
                  psum("ps_ing2", [128, 512], F32) as ps2,
                  psum("ps_ing3", [128, 512], F32) as ps3):
                psl = [ps0, ps1, ps2, ps3]
                NTI = TOK // 128
                pe0, dve0 = sPE.n, sDVE.n
                ldb = [sLw[0].n, sLw[1].n]
                stb = [sStp[0].n, sStp[1].n]

                def _sync(e):
                    for i in range(NTI):
                        if i >= 2:
                            e.wait_ge(sPE.h, pe0 + (i - 1) * 9)
                        sLw[i % 2].bump(e.dma_start(out=ing_f[:, (i % 2) * 1024:(i % 2 + 1) * 1024],
                                                    in_=x_in[i * 128:(i + 1) * 128, :]), 16)
                sec("sync", _sync)

                def _pe(e):
                    for i in range(NTI):
                        par = i % 2
                        e.wait_ge(sLw[par].h, ldb[par] + 16 * (i // 2 + 1))
                        if i >= 1:
                            e.wait_ge(sDVE.h, dve0 + i * 8)
                        for dc in range(8):
                            pb = psl[dc % 4]
                            sPE.bump(e.transpose(pb[:, 0:128],
                                                 ing_f[:, par * 1024 + dc * 128: par * 1024 + (dc + 1) * 128],
                                                 ident_f[:]))
                            if dc >= 3:
                                e.wait_ge(sDVE.h, dve0 + i * 8 + dc - 2)
                        sPE.bump(e.transpose(psl[3][:, 0:128],
                                             ing_f[:, par * 1024 + 7 * 128: par * 1024 + 8 * 128],
                                             ident_f[:]))
                sec("tensor", _pe)

                def _dve(e):
                    for i in range(NTI):
                        par = i % 2
                        if i >= 2:
                            e.wait_ge(sStp[par].h, stb[par] + 16 * (i // 2))
                        for dc in range(8):
                            pb = psl[dc % 4]
                            e.wait_ge(sPE.h, pe0 + i * 9 + dc + 2)
                            sDVE.bump(e.tensor_copy(
                                out=ing_o[:, par * 1024 + dc * 128: par * 1024 + (dc + 1) * 128],
                                in_=pb[:, 0:128]))
                sec("vector", _dve)

                def _gp(e):
                    for i in range(NTI):
                        par = i % 2
                        e.wait_ge(sDVE.h, dve0 + (i + 1) * 8)
                        dv = resid.rearrange("(c p) t -> p c t", p=128)[:, :, i * 128:(i + 1) * 128]
                        sv = ing_o[:, par * 1024:(par + 1) * 1024].rearrange("p (c t) -> p c t", c=8)
                        sStp[par].bump(e.dma_start(out=dv, in_=sv), 16)
                sec("gpsimd", _gp)

        def output_phase():
            with (sbuf("o_f", [128, 2 * 1024], F32) as o_f,
                  sbuf("o_o", [128, 2 * 1024], F32) as o_o,
                  psum("ps_of0", [128, 512], F32) as ps0,
                  psum("ps_of1", [128, 512], F32) as ps1,
                  psum("ps_of2", [128, 512], F32) as ps2,
                  psum("ps_of3", [128, 512], F32) as ps3):
                psl = [ps0, ps1, ps2, ps3]
                NTI = TOK // 128
                pe0, dve0 = sPE.n, sDVE.n
                ldb = [sLw[0].n, sLw[1].n]
                stb = [sStp[0].n, sStp[1].n]

                def _sync(e):
                    for i in range(NTI):
                        if i >= 2:
                            e.wait_ge(sPE.h, pe0 + (i - 1) * 9)
                        sv = resid.rearrange("(c p) t -> p c t", p=128)[:, :, i * 128:(i + 1) * 128]
                        dv = o_f[:, (i % 2) * 1024:(i % 2 + 1) * 1024].rearrange("p (c t) -> p c t", c=8)
                        sLw[i % 2].bump(e.dma_start(out=dv, in_=sv), 16)
                sec("sync", _sync)

                def _pe(e):
                    for i in range(NTI):
                        par = i % 2
                        e.wait_ge(sLw[par].h, ldb[par] + 16 * (i // 2 + 1))
                        if i >= 1:
                            e.wait_ge(sDVE.h, dve0 + i * 8)
                        for dc in range(8):
                            pb = psl[dc % 4]
                            sPE.bump(e.transpose(pb[:, 0:128],
                                                 o_f[:, par * 1024 + dc * 128: par * 1024 + (dc + 1) * 128],
                                                 ident_f[:]))
                            if dc >= 3:
                                e.wait_ge(sDVE.h, dve0 + i * 8 + dc - 2)
                        sPE.bump(e.transpose(psl[3][:, 0:128],
                                             o_f[:, par * 1024 + 7 * 128: par * 1024 + 8 * 128],
                                             ident_f[:]))
                sec("tensor", _pe)

                def _dve(e):
                    for i in range(NTI):
                        par = i % 2
                        if i >= 2:
                            e.wait_ge(sStp[par].h, stb[par] + 16 * (i // 2))
                        for dc in range(8):
                            pb = psl[dc % 4]
                            e.wait_ge(sPE.h, pe0 + i * 9 + dc + 2)
                            sDVE.bump(e.tensor_copy(
                                out=o_o[:, par * 1024 + dc * 128: par * 1024 + (dc + 1) * 128],
                                in_=pb[:, 0:128]))
                sec("vector", _dve)

                def _gp(e):
                    for i in range(NTI):
                        par = i % 2
                        e.wait_ge(sDVE.h, dve0 + (i + 1) * 8)
                        sStp[par].bump(e.dma_start(out=out_t[i * 128:(i + 1) * 128, :],
                                                   in_=o_o[:, par * 1024:(par + 1) * 1024]), 16)
                    e.wait_ge(sStp[0].h, sStp[0].n)
                    e.wait_ge(sStp[1].h, sStp[1].n)
                    e.wait_ge(sSt.h, sSt.n)
                sec("gpsimd", _gp)

        def emit_ln(W, xf, xbf, sqbf, out_bf, rows, bcast, p_sx, p_sq, p_b1, p_b2, wait_ld,
                    out_off=0, out_stride=None):
            if out_stride is None:
                out_stride = W

            pe_now = sPE.n

            def _dve(e, pe_now=pe_now):
                e.wait_ge(sL.h, wait_ld)
                e.wait_ge(sPE.h, pe_now)
                sDVE.bump(e.tensor_copy(out=xbf[:, 0:8 * W], in_=xf[:, 0:8 * W]))
            sec("vector", _dve)
            c1 = sDVE.n

            def _act(e, c1=c1):
                e.wait_ge(sDVE.h, c1)
                sACT.bump(e.activation(out=sqbf[:, 0:8 * W], in_=xbf[:, 0:8 * W], func=AF.Square))
            sec("scalar", _act)
            c2 = sACT.n

            def _pe(e, c2=c2):
                e.wait_ge(sACT.h, c2)
                for dc in range(8):
                    e.matmul(p_sx[0:1, 0:W], ones_col_b[:, 0:1], xbf[:, dc * W:(dc + 1) * W],
                             start=(dc == 0), stop=(dc == 7))
                for dc in range(8):
                    inst = e.matmul(p_sq[0:1, 0:W], ones_col_b[:, 0:1], sqbf[:, dc * W:(dc + 1) * W],
                                    start=(dc == 0), stop=(dc == 7))
                sPE.bump(inst)
            sec("tensor", _pe)
            c3 = sPE.n

            mean_r, rstd_r, t0_r, t1_r = (rows[0:1, i * W:(i + 1) * W] for i in range(4))

            def _dve2(e, c3=c3):
                e.wait_ge(sPE.h, c3)
                e.tensor_scalar_mul(out=mean_r, in0=p_sx[0:1, 0:W], scalar1=1.0 / D)
                e.tensor_scalar_mul(out=t0_r, in0=p_sq[0:1, 0:W], scalar1=1.0 / D)
                e.tensor_mul(out=t1_r, in0=mean_r, in1=mean_r)
                e.tensor_sub(out=t0_r, in0=t0_r, in1=t1_r)
                sDVE.bump(e.tensor_scalar_add(out=t0_r, in0=t0_r, scalar1=EPS))
            sec("vector", _dve2)
            c4 = sDVE.n

            def _act2(e, c4=c4):
                e.wait_ge(sDVE.h, c4)
                e.activation(out=t1_r, in_=t0_r, func=AF.Ln)
                sACT.bump(e.activation(out=rstd_r, in_=t1_r, func=AF.Exp, scale=-0.5))
            sec("scalar", _act2)
            c5 = sACT.n

            def _pe2(e, c5=c5):
                e.wait_ge(sACT.h, c5)
                e.matmul(p_b1[:, 0:W], ones_row_f[0:1, :], mean_r, start=True, stop=True)
                sPE.bump(e.matmul(p_b2[:, 0:W], ones_row_f[0:1, :], rstd_r, start=True, stop=True))
            sec("tensor", _pe2)
            c6 = sPE.n

            def _act3(e, c6=c6):
                e.wait_ge(sPE.h, c6)
                e.activation(out=bcast[:, 0:W], in_=p_b1[:, 0:W], func=AF.Copy)
                sACT.bump(e.activation(out=bcast[:, W:2 * W], in_=p_b2[:, 0:W], func=AF.Copy))
            sec("scalar", _act3)
            c7 = sACT.n

            def _dve3(e, c7=c7):
                e.wait_ge(sACT.h, c7)
                for dc in range(8):
                    e.tensor_sub(out=xf[:, dc * W:(dc + 1) * W],
                                 in0=xf[:, dc * W:(dc + 1) * W], in1=bcast[:, 0:W])
                    inst = e.tensor_mul(out=out_bf[:, out_off + dc * out_stride:
                                               out_off + dc * out_stride + W],
                                        in0=xf[:, dc * W:(dc + 1) * W], in1=bcast[:, W:2 * W])
                sDVE.bump(inst)
            sec("vector", _dve3)
            return sDVE.n

        def mamba(b):
            with ExitStack() as ms:
                x_sb = ms.enter_context(sbuf("m_x", [128, 8 * TC], F32))
                xi_sb = ms.enter_context(sbuf("m_xi", [128, 16 * (TC + 3)], F32))
                xi2_sb = ms.enter_context(sbuf("m_xi2", [128, 16 * TC], F32))
                res_sb = ms.enter_context(sbuf("m_res", [128, 16 * TC], F32))
                dlt_sb = ms.enter_context(sbuf("m_dlt", [128, 16 * TC], F32))
                y_sb = ms.enter_context(sbuf("m_y", [128, 16 * TC], F32))
                xi2b_sb = ms.enter_context(sbuf("m_xi2b", [128, 16 * TC], BF16))
                xbf_sb = xi2b_sb[:, 0:8 * TC]
                sqbf_sb = xi2b_sb[:, 8 * TC:16 * TC]
                xn_sb = xi2b_sb[:, 0:8 * TC]
                drb_sb = ms.enter_context(sbuf("m_drb", [64, TC], BF16))
                bc_sb = ms.enter_context(sbuf("m_bc", [32, TC], F32))
                bb_sb = ms.enter_context(sbuf("m_bb", [128, NSTG * TC], F32))
                cb_sb = ms.enter_context(sbuf("m_cb", [128, NSTG * TC], F32))
                da_sb = [xi_sb[:, 0:NSTG * SEG],
                         xi_sb[:, NSTG * SEG:2 * NSTG * SEG]]
                du_sb = [ms.enter_context(sbuf(f"m_du{i}", [128, NSTG * SEG], F32))
                         for i in range(2)]
                s_sb = [ms.enter_context(sbuf(f"m_s{i}", [128, NSTG * SEG], F32))
                        for i in range(2)]
                ct_sb = [ms.enter_context(sbuf(f"m_ct{i}", [128, NSTG * TC], F32))
                         for i in range(2)]
                yt_sb = ms.enter_context(sbuf("m_yt", [128, TC], F32))
                st_sb = ms.enter_context(sbuf("m_st", [128, 256], F32))
                halo_sb = ms.enter_context(sbuf("m_halo", [128, 16 * 3], F32))
                row_sb = ms.enter_context(sbuf("m_row", [1, 6 * TC], F32))
                bcst_sb = ms.enter_context(sbuf("m_bcst", [128, 3 * TC], F32))
                wi_sb = [ms.enter_context(sbuf(f"m_wi{i}", [128, 8 * 128], BF16))
                         for i in range(2)]
                xp_sb = ms.enter_context(sbuf("m_xp", [128, 16 * 96], BF16))
                dp_sb = ms.enter_context(sbuf("m_dp", [64, DI], BF16))
                xpf_sb = y_sb[:, 0:16 * 96]
                dpf_sb = dlt_sb[0:64, 0:DI]
                sptmp = [ms.enter_context(sbuf(f"m_spt{i}", [128, TC], F32))
                         for i in range(2)]
                sel = ms.enter_context(sbuf("m_sel", [32, 32 * 128], F32))
                xld_sb = ct_sb[0][:, 0:4 * TC] if DEBUG_TAPS else None

                def _gpsel(e):
                    e.memset(sel[:], 0.0)
                    e.affine_select(out=sel[:], in_=sel[:], compare_op=ALU.not_equal,
                                    fill=1.0, base=0, pattern=[[-1, 32], [0, 128]],
                                    channel_multiplier=1)
                    sPOOL.bump(e.tensor_copy(out=sel[0:32, 0:4], in_=sel[0:32, 0:4]))
                sec("gpsimd", _gpsel)
                sel_done = sPOOL.n

                ps_w = [ms.enter_context(psum(f"m_psw{i}", [128, 512], F32))
                        for i in range(2)]
                ps_sx = ms.enter_context(psum("m_pssx", [1, TC], F32))
                ps_sq = ms.enter_context(psum("m_pssq", [1, TC], F32))
                ps_b1 = ms.enter_context(psum("m_psb1", [128, 512], F32))
                ps_b2 = ms.enter_context(psum("m_psb2", [128, 512], F32))
                ps_b3 = ms.enter_context(psum("m_psb3", [128, 512], F32))
                ps_x = ms.enter_context(psum("m_psx", [96, TC], F32))

                ld0 = sL.n

                def _sync0(e):
                    sv = xpw.rearrange("(c p) f -> p c f", p=128)
                    sL.bump(e.dma_start(out=xpf_sb.rearrange("p (c f) -> p c f", c=16),
                                        in_=sv), 16)
                    sL.bump(e.dma_start(out=dpf_sb, in_=dpw), 16)
                sec("sync", _sync0)

                def _dve0(e, ld0=sL.n):
                    e.wait_ge(sL.h, ld0)
                    e.tensor_copy(out=xp_sb[:], in_=xpf_sb)
                    e.tensor_copy(out=dp_sb[:], in_=dpf_sb)
                    e.memset(st_sb[:], 0.0)
                    sDVE.bump(e.memset(halo_sb[:], 0.0))
                sec("vector", _dve0)
                init_done = sDVE.n

                chunk_last_dve = init_done
                chunk_last_store = sSt.n

                for c in range(NCH):
                    col0 = b * S + c * TC

                    def _syncx(e, col0=col0, cld=chunk_last_dve, cls=chunk_last_store):
                        e.wait_ge(sDVE.h, cld)
                        e.wait_ge(sSt.h, cls)
                        sv = resid.rearrange("(c p) t -> p c t", p=128)[:, :, col0:col0 + TC]
                        sL.bump(e.dma_start(out=x_sb[:].rearrange("p (c t) -> p c t", c=8),
                                            in_=sv), 16)
                    sec("sync", _syncx)
                    ld_x = sL.n

                    def _dveA(e, ld_x=ld_x):
                        e.wait_ge(sL.h, ld_x)
                        if DEBUG_TAPS:
                            e.tensor_copy(out=xld_sb, in_=x_sb[:, 0:4 * TC])
                        sDVE.bump(e.tensor_copy(out=xbf_sb, in_=x_sb[:]))
                    sec("vector", _dveA)
                    c1 = sDVE.n

                    def _actA(e, c1=c1):
                        e.wait_ge(sDVE.h, c1)
                        sACT.bump(e.activation(out=sqbf_sb, in_=xbf_sb, func=AF.Square))
                    sec("scalar", _actA)
                    c2 = sACT.n

                    def _peA(e, c2=c2):
                        e.wait_ge(sACT.h, c2)
                        for dc in range(8):
                            e.matmul(ps_sx[0:1, :], ones_col_b[:, 0:1],
                                     xbf_sb[:, dc * TC:(dc + 1) * TC],
                                     start=(dc == 0), stop=(dc == 7))
                        for dc in range(8):
                            e.matmul(ps_sq[0:1, :], ones_col_b[:, 0:1],
                                     sqbf_sb[:, dc * TC:(dc + 1) * TC],
                                     start=(dc == 0), stop=(dc == 7))
                        sPE.bump(e.matmul(ps_b1[:, 0:128], ones_row_f[0:1, :],
                                          ones_row_f[0:1, 0:128], start=True, stop=True))
                    sec("tensor", _peA)
                    c3 = sPE.n

                    mean_r = row_sb[0:1, 0:TC]
                    rs1_r = row_sb[0:1, TC:2 * TC]
                    rA_r = row_sb[0:1, 2 * TC:3 * TC]
                    t0_r = row_sb[0:1, 3 * TC:4 * TC]
                    t1_r = row_sb[0:1, 4 * TC:5 * TC]
                    var_r = row_sb[0:1, 5 * TC:6 * TC]

                    def _dveB(e, c3=c3):
                        e.wait_ge(sPE.h, c3)
                        e.tensor_scalar_mul(out=mean_r, in0=ps_sx[0:1, :], scalar1=1.0 / D)
                        e.tensor_scalar_mul(out=t0_r, in0=ps_sq[0:1, :], scalar1=1.0 / D)
                        e.tensor_mul(out=t1_r, in0=mean_r, in1=mean_r)
                        e.tensor_sub(out=var_r, in0=t0_r, in1=t1_r)
                        sDVE.bump(e.tensor_scalar_add(out=t0_r, in0=var_r, scalar1=EPS))
                    sec("vector", _dveB)
                    c4 = sDVE.n

                    def _actB(e, c4=c4):
                        e.wait_ge(sDVE.h, c4)
                        e.activation(out=t1_r, in_=t0_r, func=AF.Ln)
                        sACT.bump(e.activation(out=rs1_r, in_=t1_r, func=AF.Exp, scale=-0.5))
                    sec("scalar", _actB)
                    c5 = sACT.n

                    def _dveC(e, c5=c5):
                        e.wait_ge(sACT.h, c5)
                        e.tensor_mul(out=t1_r, in0=rs1_r, in1=rs1_r)
                        e.tensor_mul(out=t1_r, in0=var_r, in1=t1_r)
                        sDVE.bump(e.tensor_scalar_add(out=t1_r, in0=t1_r, scalar1=EPS))
                    sec("vector", _dveC)
                    c6 = sDVE.n

                    def _actC(e, c6=c6):
                        e.wait_ge(sDVE.h, c6)
                        e.activation(out=t0_r, in_=t1_r, func=AF.Ln)
                        sACT.bump(e.activation(out=t1_r, in_=t0_r, func=AF.Exp, scale=-0.5))
                    sec("scalar", _actC)
                    c7 = sACT.n

                    def _dveD(e, c7=c7):
                        e.wait_ge(sACT.h, c7)
                        sDVE.bump(e.tensor_mul(out=rA_r, in0=rs1_r, in1=t1_r))
                    sec("vector", _dveD)
                    c8 = sDVE.n

                    def _peB(e, c8=c8):
                        e.wait_ge(sDVE.h, c8)
                        e.matmul(ps_b1[:, 0:TC], ones_row_f[0:1, :], mean_r, start=True, stop=True)
                        e.matmul(ps_b2[:, 0:TC], ones_row_f[0:1, :], rs1_r, start=True, stop=True)
                        sPE.bump(e.matmul(ps_b3[:, 0:TC], ones_row_f[0:1, :], rA_r,
                                          start=True, stop=True))
                    sec("tensor", _peB)
                    c9 = sPE.n

                    def _actD(e, c9=c9):
                        e.wait_ge(sPE.h, c9)
                        e.activation(out=bcst_sb[:, 0:TC], in_=ps_b1[:, 0:TC], func=AF.Copy)
                        e.activation(out=bcst_sb[:, TC:2 * TC], in_=ps_b2[:, 0:TC], func=AF.Copy)
                        sACT.bump(e.activation(out=bcst_sb[:, 2 * TC:3 * TC], in_=ps_b3[:, 0:TC],
                                               func=AF.Copy))
                    sec("scalar", _actD)
                    c10 = sACT.n

                    def _dveE(e, c10=c10):
                        e.wait_ge(sACT.h, c10)
                        mb = bcst_sb[:, 0:TC].unsqueeze(1).broadcast_to([128, 8, TC])
                        rab = bcst_sb[:, 2 * TC:3 * TC].unsqueeze(1).broadcast_to([128, 8, TC])
                        yv = y_sb[:, 0:8 * TC].rearrange("p (c t) -> p c t", c=8)
                        e.tensor_sub(out=yv,
                                     in0=x_sb[:].rearrange("p (c t) -> p c t", c=8), in1=mb)
                        sDVE.bump(e.tensor_mul(out=yv, in0=yv, in1=rab))
                    sec("vector", _dveE)
                    c11 = sDVE.n

                    def _actE(e, c11=c11):
                        e.wait_ge(sDVE.h, c11)
                        for dc in range(8):
                            inst = e.activation(out=xn_sb[:, dc * TC:(dc + 1) * TC],
                                                in_=y_sb[:, dc * TC:(dc + 1) * TC], func=AF.Copy,
                                                scale=sp[:, SP_NW + dc:SP_NW + dc + 1])
                        sACT.bump(inst)
                    sec("scalar", _actE)
                    act_xn = sACT.n

                    ip_pe0, ip_act0 = sPE.n, sACT.n
                    ip_ldb = [sLw[0].n, sLw[1].n]

                    def _syncW(e, ip_pe0=ip_pe0, ip_act0=ip_act0):
                        for m in range(32):
                            if SERIAL_W and m >= 1:
                                e.wait_ge(sACT.h, ip_act0 + m)
                            elif m >= 2:
                                e.wait_ge(sPE.h, ip_pe0 + m - 1)
                            sLw[m % 2].bump(e.dma_start(
                                out=wi_sb[m % 2][:].rearrange("p (c f) -> p c f", c=8),
                                in_=ipb[m, :, :, :]), 16)
                    sec("sync", _syncW)

                    def _peC(e, act_xn=act_xn, ip_ldb=ip_ldb, ip_act0=ip_act0):
                        e.wait_ge(sACT.h, act_xn)
                        for m in range(32):
                            pb = ps_w[m % 2]
                            e.wait_ge(sLw[m % 2].h, ip_ldb[m % 2] + 16 * (m // 2 + 1))
                            if m >= 2:
                                e.wait_ge(sACT.h, ip_act0 + m - 1)
                            for kc in range(8):
                                inst = e.matmul(pb[:, 0:TC],
                                                wi_sb[m % 2][:, kc * 128:(kc + 1) * 128],
                                                xn_sb[:, kc * TC:(kc + 1) * TC],
                                                start=(kc == 0), stop=(kc == 7))
                            sPE.bump(inst)
                    sec("tensor", _peC)
                    pe_after_ip = sPE.n

                    def _actF(e, ip_pe0=ip_pe0):
                        for m in range(32):
                            pb = ps_w[m % 2]
                            e.wait_ge(sPE.h, ip_pe0 + m + 1)
                            if m < 16:
                                dst = xi_sb[:, m * (TC + 3) + 3: m * (TC + 3) + 3 + TC]
                                inst = e.activation(out=dst, in_=pb[:, 0:TC], func=AF.Copy)
                            else:
                                dst = res_sb[:, (m - 16) * TC:(m - 15) * TC]
                                inst = e.activation(out=dst, in_=pb[:, 0:TC], func=AF.Silu)
                            sACT.bump(inst)
                    sec("scalar", _actF)
                    act_ip = sACT.n

                    def _dveF(e, act_ip=act_ip):
                        e.wait_ge(sACT.h, act_ip)
                        xiv = xi_sb[:].rearrange("p (c t) -> p c t", c=16)
                        e.tensor_copy(out=xiv[:, :, 0:3],
                                      in_=halo_sb[:].rearrange("p (c h) -> p c h", c=16))
                        for dc in range(16):
                            off = dc * (TC + 3)
                            for j in range(4):
                                src = xi_sb[:, off + j: off + j + TC]
                                scl = sp[:, SP_CW + j * 16 + dc: SP_CW + j * 16 + dc + 1]
                                if j == 0:
                                    e.tensor_scalar_mul(out=xi2_sb[:, dc * TC:(dc + 1) * TC],
                                                        in0=src, scalar1=scl)
                                else:
                                    e.scalar_tensor_tensor(
                                        out=xi2_sb[:, dc * TC:(dc + 1) * TC], in0=src, scalar=scl,
                                        in1=xi2_sb[:, dc * TC:(dc + 1) * TC],
                                        op0=ALU.mult, op1=ALU.add)
                        e.tensor_copy(out=halo_sb[:].rearrange("p (c h) -> p c h", c=16),
                                      in_=xiv[:, :, TC:TC + 3])
                        e.memset(da_sb[0].rearrange("p (n s) -> p n s", n=NSTG)[:, :, 0:1], 0.0)
                        sDVE.bump(e.memset(
                            da_sb[1].rearrange("p (n s) -> p n s", n=NSTG)[:, :, 0:1], 0.0))
                    sec("vector", _dveF)
                    c12 = sDVE.n

                    def _actG(e, c12=c12):
                        e.wait_ge(sDVE.h, c12)
                        for dc in range(16):
                            inst = e.activation(out=xi2_sb[:, dc * TC:(dc + 1) * TC],
                                                in_=xi2_sb[:, dc * TC:(dc + 1) * TC], func=AF.Silu,
                                                bias=sp[:, SP_CB + dc:SP_CB + dc + 1])
                        sACT.bump(inst)
                    sec("scalar", _actG)
                    c13 = sACT.n

                    def _dveG(e, c13=c13):
                        e.wait_ge(sACT.h, c13)
                        sDVE.bump(e.tensor_copy(out=xi2b_sb[:], in_=xi2_sb[:]))
                    sec("vector", _dveG)
                    c14 = sDVE.n

                    def _peD(e, c14=c14):
                        e.wait_ge(sDVE.h, c14)
                        for kc in range(16):
                            inst = e.matmul(ps_x[:, :], xp_sb[:, kc * 96:(kc + 1) * 96],
                                            xi2b_sb[:, kc * TC:(kc + 1) * TC],
                                            start=(kc == 0), stop=(kc == 15))
                        sPE.bump(inst)
                    sec("tensor", _peD)
                    c15 = sPE.n

                    def _actH(e, c15=c15):
                        e.wait_ge(sPE.h, c15)
                        e.activation(out=drb_sb[:, :], in_=ps_x[0:64, :], func=AF.Copy)
                        sACT.bump(e.activation(out=bc_sb[:, :], in_=ps_x[64:96, :], func=AF.Copy))
                    sec("scalar", _actH)
                    act_xdbl = sACT.n

                    dt_pe0, dt_act0, dt_dve0 = sPE.n, sACT.n, sDVE.n

                    def _peE(e, act_xdbl=act_xdbl, dt_dve0=dt_dve0):
                        e.wait_ge(sACT.h, act_xdbl)
                        for mc in range(16):
                            pb = ps_w[mc % 2]
                            if mc >= 2:
                                e.wait_ge(sDVE.h, dt_dve0 + 2 * mc - 3)
                            inst = e.matmul(pb[:, 0:TC], dp_sb[0:64, mc * 128:(mc + 1) * 128],
                                            drb_sb[0:64, :], start=True, stop=True)
                            sPE.bump(inst)
                        sPE.bump(e.matmul(ps_x[:, 0:64], ones_row_f[0:1, 0:96],
                                          ones_row_f[0:1, 0:64], start=True, stop=True))
                    sec("tensor", _peE)

                    def _dveH1(e, dt_pe0=dt_pe0, dt_act0=dt_act0):
                        for mc in range(16):
                            pb = ps_w[mc % 2]
                            e.wait_ge(sPE.h, dt_pe0 + mc + 2)
                            z = dlt_sb[:, mc * TC:(mc + 1) * TC]
                            sDVE.bump(e.tensor_scalar_add(
                                out=z, in0=pb[:, 0:TC],
                                scalar1=sp[:, SP_DTB + mc:SP_DTB + mc + 1]))
                            e.wait_ge(sACT.h, dt_act0 + mc + 1)
                            inst = e.scalar_tensor_tensor(out=z, in0=z, scalar=0.0,
                                                          in1=sptmp[mc % 2][:, :],
                                                          op0=ALU.max, op1=ALU.add)
                            sDVE.bump(inst)
                    sec("vector", _dveH1)

                    def _actI(e, dt_dve0=dt_dve0):
                        for mc in range(16):
                            e.wait_ge(sDVE.h, dt_dve0 + 2 * mc + 1)
                            if mc >= 2:
                                e.wait_ge(sDVE.h, dt_dve0 + 2 * mc - 2)
                            z = dlt_sb[:, mc * TC:(mc + 1) * TC]
                            t = sptmp[mc % 2][:, :]
                            e.activation(out=t, in_=z, func=AF.Abs)
                            e.activation(out=t, in_=t, func=AF.Exp, scale=-1.0)
                            inst = e.activation(out=t, in_=t, func=AF.Ln, bias=1.0)
                            sACT.bump(inst)
                    sec("scalar", _actI)
                    dve_dt = sDVE.n

                    def _dveI(e, dve_dt=dve_dt):
                        e.wait_ge(sDVE.h, dve_dt)
                        for dc in range(16):
                            e.tensor_scalar_mul(out=y_sb[:, dc * TC:(dc + 1) * TC],
                                                in0=xi2_sb[:, dc * TC:(dc + 1) * TC],
                                                scalar1=sp[:, SP_D + dc:SP_D + dc + 1])
                        sDVE.bump(e.tensor_mul(out=xi2_sb[:], in0=dlt_sb[:], in1=xi2_sb[:]))
                    sec("vector", _dveI)
                    dve_u0 = sDVE.n
                    u0_sb = xi2_sb

                    A0, P0, D0, PE0 = sACT.n, sPOOL.n, sDVE.n, sPE.n
                    bank_b = [ps_b1, ps_b2]
                    bank_c = [ps_b3, ps_w[0]]

                    a_evac_c = {}; a_evac_done_c = {}; a_cover_c = {}; a_ext_c = {}
                    p_ct_c = {}; p_dbu_c = {}; d_tts_c = {}; d_acc_c = {}; pe_mm_c = {}
                    _a, _p, _d, _pe = A0, P0, D0, PE0
                    for g2 in range(2):
                        for mi in range(17):
                            _pe += 1; pe_mm_c[(g2, mi)] = _pe
                        for ei in range(8):
                            _a += 1; a_evac_c[(g2, ei)] = _a
                        _a += 1; a_evac_done_c[g2] = _a
                        for dc in range(16):
                            it = g2 * 16 + dc
                            _a += 1; a_cover_c[it] = _a
                            if it >= 1:
                                _a += 1; a_ext_c[it - 1] = _a
                    _a += 1; a_ext_c[NIT - 1] = _a
                    pool_sched = []
                    _nx = 0
                    for it in range(NIT):
                        if it >= 1:
                            pool_sched.append(("ct", it - 1))
                        while _nx < NIT and _nx <= it + 1 and (_nx < 16 or it >= 16):
                            pool_sched.append(("dbu", _nx))
                            _nx += 1
                    pool_sched.append(("ct", NIT - 1))
                    for kind, j in pool_sched:
                        _p += 1
                        (p_ct_c if kind == "ct" else p_dbu_c)[j] = _p
                    for it in range(NIT):
                        _d += 1; d_tts_c[it] = _d
                        if it >= 1:
                            _d += 1; d_acc_c[it - 1] = _d
                    _d += 1; d_acc_c[NIT - 1] = _d

                    if not MAMBA_NOSCAN:
                        def _peBC(e, act_xdbl=act_xdbl, dve_dt=dve_dt):
                            for g2 in range(2):
                                if g2 == 0:
                                    e.wait_ge(sACT.h, act_xdbl)
                                    e.wait_ge(sPOOL.h, sel_done)
                                    e.wait_ge(sDVE.h, dve_dt)
                                else:
                                    e.wait_ge(sACT.h, a_evac_done_c[0])
                                    e.wait_ge(sPOOL.h, p_ct_c[15])
                                for p in range(4):
                                    if p >= 2:
                                        e.wait_ge(sACT.h, a_evac_c[(g2, 2 * (p - 2) + 1)])
                                    for h in range(2):
                                        n = g2 * NSTG + 2 * p + h
                                        sPE.bump(e.matmul(
                                            bank_b[p % 2][:, h * TC:(h + 1) * TC],
                                            sel[:, n * 128:(n + 1) * 128],
                                            bc_sb[0:32, :], start=True, stop=True))
                                    for h in range(2):
                                        n = g2 * NSTG + 2 * p + h
                                        sPE.bump(e.matmul(
                                            bank_c[p % 2][:, h * TC:(h + 1) * TC],
                                            sel[:, (16 + n) * 128:(17 + n) * 128],
                                            bc_sb[0:32, :], start=True, stop=True))
                                n = g2 * NSTG + 7
                                sPE.bump(e.matmul(bank_c[1][:, TC:2 * TC],
                                                  sel[:, (16 + n) * 128:(17 + n) * 128],
                                                  bc_sb[0:32, :], start=True, stop=True))
                        sec("tensor", _peBC)

                        def _actSC(e, dve_dt=dve_dt):
                            e.wait_ge(sDVE.h, dve_dt)
                            for g2 in range(2):
                                for p in range(4):
                                    e.wait_ge(sPE.h, pe_mm_c[(g2, 4 * p + 2)])
                                    sACT.bump(e.activation(
                                        out=bb_sb[:, 2 * p * TC:(2 * p + 2) * TC],
                                        in_=bank_b[p % 2][:, 0:2 * TC], func=AF.Copy))
                                    e.wait_ge(sPE.h, pe_mm_c[(g2, 16) if p == 3
                                                             else (g2, 4 * p + 4)])
                                    sACT.bump(e.activation(
                                        out=cb_sb[:, 2 * p * TC:(2 * p + 2) * TC],
                                        in_=bank_c[p % 2][:, 0:2 * TC], func=AF.Copy))
                                sACT.bump(e.activation(out=sptmp[0][:, 4:8],
                                                       in_=sptmp[0][:, 4:8], func=AF.Copy))
                                for dc in range(16):
                                    it = g2 * 16 + dc
                                    par = it % 2
                                    if it >= 2:
                                        e.wait_ge(sDVE.h, d_tts_c[it - 2])
                                    for n8 in range(NSTG):
                                        n = g2 * NSTG + n8
                                        e.activation(
                                            out=da_sb[par][:, n8 * SEG + 1:(n8 + 1) * SEG],
                                            in_=dlt_sb[:, dc * TC:(dc + 1) * TC], func=AF.Exp,
                                            scale=a_sb[:, dc * 16 + n: dc * 16 + n + 1])
                                    duv = du_sb[par][:].rearrange("p (n s) -> p n s", n=NSTG)
                                    e.activation(
                                        out=duv[:, :, 0:1].squeeze(2),
                                        in_=st_sb[:, dc * 16 + g2 * NSTG:
                                                  dc * 16 + g2 * NSTG + NSTG], func=AF.Copy)
                                    sACT.bump(e.activation(out=sptmp[0][:, 0:4],
                                                           in_=sptmp[0][:, 0:4], func=AF.Copy))
                                    if it >= 1:
                                        j = it - 1
                                        e.wait_ge(sDVE.h, d_tts_c[j])
                                        sv = s_sb[j % 2][:].rearrange("p (n s) -> p n s",
                                                                      n=NSTG)
                                        sACT.bump(e.activation(
                                            out=st_sb[:, (j % 16) * 16 + (j // 16) * NSTG:
                                                      (j % 16) * 16 + (j // 16) * NSTG + NSTG],
                                            in_=sv[:, :, SEG - 1:SEG].squeeze(2), func=AF.Copy))
                            j = NIT - 1
                            e.wait_ge(sDVE.h, d_tts_c[j])
                            sv = s_sb[j % 2][:].rearrange("p (n s) -> p n s", n=NSTG)
                            sACT.bump(e.activation(
                                out=st_sb[:, 15 * 16 + NSTG: 15 * 16 + 2 * NSTG],
                                in_=sv[:, :, SEG - 1:SEG].squeeze(2), func=AF.Copy))
                        sec("scalar", _actSC)
                        assert sACT.n == a_ext_c[NIT - 1], (sACT.n, a_ext_c[NIT - 1])

                        def _poolSC(e, dve_u0=dve_u0):
                            e.wait_ge(sDVE.h, dve_u0)
                            for kind, j in pool_sched:
                                if kind == "ct":
                                    e.wait_ge(sDVE.h, d_tts_c[j])
                                    if j >= 2:
                                        e.wait_ge(sDVE.h, d_acc_c[j - 2])
                                    sjv = s_sb[j % 2][:].rearrange("p (n s) -> p n s",
                                                                   n=NSTG)
                                    e.tensor_mul(
                                        out=ct_sb[j % 2][:].rearrange("p (n t) -> p n t",
                                                                      n=NSTG),
                                        in0=sjv[:, :, 1:SEG],
                                        in1=cb_sb[:].rearrange("p (n t) -> p n t", n=NSTG))
                                    sPOOL.bump(e.tensor_copy(out=sptmp[1][:, 0:4],
                                                             in_=sptmp[1][:, 0:4]))
                                else:
                                    g2, dc = j // 16, j % 16
                                    par = j % 2
                                    if dc == 0:
                                        e.wait_ge(sACT.h, a_evac_done_c[g2])
                                    if j >= 2:
                                        e.wait_ge(sDVE.h, d_tts_c[j - 2])
                                    u0v = u0_sb[:, dc * TC:(dc + 1) * TC].unsqueeze(1) \
                                        .broadcast_to([128, NSTG, TC])
                                    duv = du_sb[par][:].rearrange("p (n s) -> p n s",
                                                                  n=NSTG)[:, :, 1:SEG]
                                    e.tensor_mul(out=duv, in0=u0v,
                                                 in1=bb_sb[:].rearrange("p (n t) -> p n t",
                                                                        n=NSTG))
                                    sPOOL.bump(e.tensor_copy(out=sptmp[1][:, 0:4],
                                                             in_=sptmp[1][:, 0:4]))
                        sec("gpsimd", _poolSC)
                        assert sPOOL.n == p_ct_c[NIT - 1], (sPOOL.n, p_ct_c[NIT - 1])

                        def _dveSC(e):
                            for it in range(NIT):
                                par = it % 2
                                e.wait_ge(sACT.h, a_cover_c[it])
                                e.wait_ge(sPOOL.h, p_dbu_c[it])
                                if it >= 2:
                                    e.wait_ge(sPOOL.h, p_ct_c[it - 2])
                                    e.wait_ge(sACT.h, a_ext_c[it - 2])
                                e.tensor_tensor_scan(out=s_sb[par][:], data0=da_sb[par],
                                                     data1=du_sb[par][:], initial=0.0,
                                                     op0=ALU.mult, op1=ALU.add)
                                sDVE.bump(e.tensor_copy(out=yt_sb[:, 0:4],
                                                        in_=yt_sb[:, 0:4]))
                                if it >= 1:
                                    j = it - 1
                                    e.wait_ge(sPOOL.h, p_ct_c[j])
                                    e.tensor_reduce(
                                        out=yt_sb[:, :],
                                        in_=ct_sb[j % 2][:].rearrange("p (n t) -> p t n",
                                                                      n=NSTG),
                                        axis=AX.X, op=ALU.add)
                                    jdc = j % 16
                                    sDVE.bump(e.tensor_add(
                                        out=y_sb[:, jdc * TC:(jdc + 1) * TC],
                                        in0=y_sb[:, jdc * TC:(jdc + 1) * TC],
                                        in1=yt_sb[:, :]))
                            j = NIT - 1
                            e.wait_ge(sPOOL.h, p_ct_c[j])
                            e.tensor_reduce(out=yt_sb[:, :],
                                            in_=ct_sb[j % 2][:].rearrange("p (n t) -> p t n",
                                                                          n=NSTG),
                                            axis=AX.X, op=ALU.add)
                            sDVE.bump(e.tensor_add(out=y_sb[:, 15 * TC:16 * TC],
                                                   in0=y_sb[:, 15 * TC:16 * TC],
                                                   in1=yt_sb[:, :]))
                        sec("vector", _dveSC)
                        assert sPE.n == pe_mm_c[(1, 16)], (sPE.n, pe_mm_c[(1, 16)])

                    scan_end = d_acc_c[NIT - 1] if not MAMBA_NOSCAN else dve_u0
                    assert scan_end == sDVE.n, (scan_end, sDVE.n)

                    def _dveJ(e, scan_end=scan_end):
                        e.wait_ge(sDVE.h, scan_end)
                        e.tensor_mul(out=y_sb[:], in0=y_sb[:], in1=res_sb[:])
                        sDVE.bump(e.tensor_copy(out=xi2b_sb[:], in_=y_sb[:]))
                    sec("vector", _dveJ)
                    dve_ybf = sDVE.n
                    ybf_sb = xi2b_sb

                    op_pe0, op_dve0 = sPE.n, sDVE.n
                    op_ldb = [sLw[0].n, sLw[1].n]

                    def _syncO(e, op_pe0=op_pe0, pe_after_ip=pe_after_ip):
                        for u in range(16):
                            m, half = u // 2, u % 2
                            if u < 2:
                                e.wait_ge(sPE.h, pe_after_ip)
                            else:
                                e.wait_ge(sPE.h, op_pe0 + u - 1)
                            sLw[u % 2].bump(e.dma_start(
                                out=wi_sb[u % 2][:].rearrange("p (c f) -> p c f", c=8),
                                in_=opb[m, :, half * 8:(half + 1) * 8, :]), 16)
                    sec("sync", _syncO)

                    def _peG(e, dve_ybf=dve_ybf, op_ldb=op_ldb, op_dve0=op_dve0):
                        e.wait_ge(sDVE.h, dve_ybf)
                        for u in range(16):
                            m, half = u // 2, u % 2
                            pb = ps_w[m % 2]
                            e.wait_ge(sLw[u % 2].h, op_ldb[u % 2] + 16 * (u // 2 + 1))
                            if half == 0 and m >= 2:
                                e.wait_ge(sDVE.h, op_dve0 + 4 * (m - 1))
                            for kc in range(8):
                                inst = e.matmul(pb[:, 0:TC],
                                                wi_sb[u % 2][:, kc * 128:(kc + 1) * 128],
                                                ybf_sb[:, (half * 8 + kc) * TC:
                                                       (half * 8 + kc + 1) * TC],
                                                start=(half == 0 and kc == 0),
                                                stop=(half == 1 and kc == 7))
                            sPE.bump(inst)
                        sPE.bump(e.matmul(ps_x[:, 0:64], ones_row_f[0:1, 0:96],
                                          ones_row_f[0:1, 0:64], start=True, stop=True))
                    sec("tensor", _peG)

                    def _dveK(e, op_pe0=op_pe0):
                        for m in range(8):
                            pb = ps_w[m % 2]
                            e.wait_ge(sPE.h, op_pe0 + min(2 * (m + 1) + 1, 17))
                            mb = bcst_sb[:, 0:TC]
                            rb = bcst_sb[:, TC:2 * TC]
                            yv = y_sb[:, m * TC:(m + 1) * TC]
                            e.tensor_sub(out=yv, in0=x_sb[:, m * TC:(m + 1) * TC], in1=mb)
                            e.tensor_mul(out=yv, in0=yv, in1=rb)
                            e.tensor_add(out=yv, in0=yv, in1=pb[:, 0:TC])
                            inst = e.tensor_add(out=x_sb[:, m * TC:(m + 1) * TC],
                                                in0=x_sb[:, m * TC:(m + 1) * TC],
                                                in1=yv)
                            sDVE.bump(inst, 4)
                    sec("vector", _dveK)
                    chunk_last_dve = sDVE.n

                    def _gpO(e, col0=col0, cld=chunk_last_dve, b=b, c=c):
                        e.wait_ge(sDVE.h, cld)
                        dv = resid.rearrange("(c p) t -> p c t", p=128)[:, :, col0:col0 + TC]
                        sSt.bump(e.dma_start(out=dv,
                                             in_=x_sb[:].rearrange("p (c t) -> p c t", c=8)), 16)
                        if DEBUG_TAPS and b == 0:
                            for nm, buf in (("xn", xn_sb), ("xi", xi_sb), ("res", res_sb),
                                            ("dlt", dlt_sb), ("u0", xi2_sb), ("y", y_sb),
                                            ("newx", x_sb), ("sub", y_sb),
                                            ("bcst", bcst_sb), ("row", row_sb),
                                            ("xld", None)):
                                if nm == "xld":
                                    sSt.bump(e.dma_start(out=taps[nm][c], in_=xld_sb), 16)
                                    continue
                                sSt.bump(e.dma_start(out=taps[nm][c], in_=buf[:]), 16)
                    sec("gpsimd", _gpO)
                    chunk_last_store = sSt.n

        def ffn():
            with ExitStack() as fs:
                xf = fs.enter_context(sbuf("f_xf", [128, 8 * FT], F32))
                xn = fs.enter_context(sbuf("f_xn", [128, 8 * FT], BF16))
                h_sb = fs.enter_context(sbuf("f_h", [128, 32 * FT], BF16))
                g_sb = fs.enter_context(sbuf("f_g", [128, 32 * FT], BF16))
                tf = [fs.enter_context(sbuf(f"f_tf{i}", [128, 2 * FT], F32))
                      for i in range(2)]
                rows = fs.enter_context(sbuf("f_rows", [1, 4 * FT], F32))
                bcast = fs.enter_context(sbuf("f_bcast", [128, 2 * FT], F32))
                wh = [fs.enter_context(sbuf(f"f_wh{i}", [128, 32 * 128], BF16))
                      for i in range(2)]
                psA = [fs.enter_context(psum(f"f_psA{i}", [128, FT], F32))
                       for i in range(2)]
                psB = [fs.enter_context(psum(f"f_psB{i}", [128, FT], F32))
                       for i in range(2)]
                p_sx = fs.enter_context(psum("f_pssx", [1, FT], F32))
                p_sq = fs.enter_context(psum("f_pssq", [1, FT], F32))
                p_b1 = fs.enter_context(psum("f_psb1", [128, FT], F32))
                p_b2 = fs.enter_context(psum("f_psb2", [128, FT], F32))

                prev_dve = sDVE.n
                prev_st = sSt.n

                for ti in range(NFT):
                    col0 = ti * FT
                    xbf = g_sb[:, 0:8 * FT]
                    sqbf = h_sb[:, 0:8 * FT]

                    def _syncx(e, col0=col0, pd=prev_dve, pst=prev_st):
                        e.wait_ge(sDVE.h, pd)
                        e.wait_ge(sSt.h, pst)
                        sv = resid.rearrange("(c p) t -> p c t", p=128)[:, :, col0:col0 + FT]
                        sL.bump(e.dma_start(out=xf[:].rearrange("p (c t) -> p c t", c=8),
                                            in_=sv), 16)
                    sec("sync", _syncx)

                    apply_done = emit_ln(FT, xf[:], xbf, sqbf, xn[:], rows[:], bcast[:],
                                         p_sx, p_sq, p_b1, p_b2, sL.n)

                    s_pe0, s_act0, s_dve0 = sPE.n, sACT.n, sDVE.n
                    s_ldb = [sLw[0].n, sLw[1].n]

                    def _syncW(e, s_pe0=s_pe0):
                        for m in range(32):
                            par = m % 2
                            if m >= 2:
                                e.wait_ge(sPE.h, s_pe0 + 2 * (m - 1))
                            e.dma_start(
                                out=wh[par][:, 0:1024].rearrange("p (c f) -> p c f", c=8),
                                in_=w1b[m, :, :, :]).then_inc(sLw[par].h, 16)
                            sLw[par].n += 16
                            sLw[par].bump(e.dma_start(
                                out=wh[par][:, 1024:2048].rearrange("p (c f) -> p c f", c=8),
                                in_=w2b[m, :, :, :]), 16)
                    sec("sync", _syncW)

                    def _peW(e, apply_done=apply_done, s_ldb=s_ldb, s_act0=s_act0):
                        e.wait_ge(sDVE.h, apply_done)
                        for m in range(32):
                            par = m % 2
                            e.wait_ge(sLw[par].h, s_ldb[par] + 32 * (m // 2 + 1))
                            if m >= 2:
                                e.wait_ge(sACT.h, s_act0 + 2 * (m - 1))
                            for kc in range(8):
                                e.matmul(psA[par][:, :], wh[par][:, kc * 128:(kc + 1) * 128],
                                         xn[:, kc * FT:(kc + 1) * FT],
                                         start=(kc == 0), stop=(kc == 7))
                            for kc in range(8):
                                inst = e.matmul(psB[par][:, :],
                                                wh[par][:, 1024 + kc * 128:1024 + (kc + 1) * 128],
                                                xn[:, kc * FT:(kc + 1) * FT],
                                                start=(kc == 0), stop=(kc == 7))
                            sPE.bump(inst, 2)
                    sec("tensor", _peW)

                    def _actW(e, s_pe0=s_pe0, s_dve0=s_dve0):
                        for m in range(32):
                            par = m % 2
                            e.wait_ge(sPE.h, s_pe0 + 2 * (m + 1))
                            if m >= 2:
                                e.wait_ge(sDVE.h, s_dve0 + m - 1)
                            mc = m
                            e.activation(out=tf[par][:, 0:FT], in_=psA[par][:, :], func=AF.Silu,
                                         scale=sp[:, SP_S1 + mc:SP_S1 + mc + 1])
                            inst = e.activation(out=tf[par][:, FT:2 * FT], in_=psB[par][:, :],
                                                func=AF.Copy, scale=sp[:, SP_S2 + mc:SP_S2 + mc + 1])
                            sACT.bump(inst, 2)
                    sec("scalar", _actW)

                    def _dveW(e, s_act0=s_act0):
                        for m in range(32):
                            par = m % 2
                            e.wait_ge(sACT.h, s_act0 + 2 * (m + 1))
                            sDVE.bump(e.tensor_mul(out=h_sb[:, m * FT:(m + 1) * FT],
                                                   in0=tf[par][:, 0:FT], in1=tf[par][:, FT:2 * FT]))
                    sec("vector", _dveW)
                    h_done = sDVE.n

                    def hstage(wsrc, rhs_sb, out_sb, scol, ready_dve, ready_act):
                        t_pe0, t_act0 = sPE.n, sACT.n
                        t_ldb = [sLw[0].n, sLw[1].n]

                        def _syncH(e, t_pe0=t_pe0):
                            for m in range(32):
                                par = m % 2
                                if m < 2:
                                    e.wait_ge(sPE.h, t_pe0)
                                else:
                                    e.wait_ge(sPE.h, t_pe0 + m - 1)
                                sLw[par].bump(e.dma_start(
                                    out=wh[par][:].rearrange("p (c f) -> p c f", c=32),
                                    in_=wsrc[m, :, :, :]), 16)
                        sec("sync", _syncH)

                        def _peH(e, t_ldb=t_ldb, t_act0=t_act0, ready_dve=ready_dve,
                                 ready_act=ready_act):
                            if ready_dve is not None:
                                e.wait_ge(sDVE.h, ready_dve)
                            if ready_act is not None:
                                e.wait_ge(sACT.h, ready_act)
                            for m in range(32):
                                par = m % 2
                                e.wait_ge(sLw[par].h, t_ldb[par] + 16 * (m // 2 + 1))
                                if m >= 2:
                                    e.wait_ge(sACT.h, t_act0 + m - 1)
                                for kc in range(32):
                                    inst = e.matmul(psA[par][:, :],
                                                    wh[par][:, kc * 128:(kc + 1) * 128],
                                                    rhs_sb[:, kc * FT:(kc + 1) * FT],
                                                    start=(kc == 0), stop=(kc == 31))
                                sPE.bump(inst)
                        sec("tensor", _peH)

                        def _actHs(e, t_pe0=t_pe0):
                            for m in range(32):
                                par = m % 2
                                e.wait_ge(sPE.h, t_pe0 + m + 1)
                                inst = e.activation(out=out_sb[:, m * FT:(m + 1) * FT],
                                                    in_=psA[par][:, :], func=AF.Gelu,
                                                    scale=sp[:, scol + m:scol + m + 1])
                                sACT.bump(inst)
                        sec("scalar", _actHs)
                        return sACT.n

                    g_done = hstage(h1b, h_sb, g_sb, SP_H1S, h_done, None)
                    hh_done = hstage(h2b, g_sb, h_sb, SP_H2S, None, g_done)

                    def _syncx2(e, col0=col0, apply_done=apply_done):
                        e.wait_ge(sDVE.h, apply_done)
                        sv = resid.rearrange("(c p) t -> p c t", p=128)[:, :, col0:col0 + FT]
                        sL.bump(e.dma_start(out=xf[:].rearrange("p (c t) -> p c t", c=8),
                                            in_=sv), 16)
                    sec("sync", _syncx2)
                    xre_ld = sL.n

                    w_pe0, w_act0, w_dve0 = sPE.n, sACT.n, sDVE.n
                    w_ldb = [sLw[0].n, sLw[1].n]

                    def _syncW3(e, w_pe0=w_pe0):
                        for m in range(8):
                            par = m % 2
                            if m < 2:
                                e.wait_ge(sPE.h, w_pe0)
                            else:
                                e.wait_ge(sPE.h, w_pe0 + m - 1)
                            sLw[par].bump(e.dma_start(
                                out=wh[par][:].rearrange("p (c f) -> p c f", c=32),
                                in_=w3b[m, :, :, :]), 16)
                    sec("sync", _syncW3)

                    def _peW3(e, hh_done=hh_done, w_ldb=w_ldb, w_act0=w_act0):
                        e.wait_ge(sACT.h, hh_done)
                        for m in range(8):
                            par = m % 2
                            e.wait_ge(sLw[par].h, w_ldb[par] + 16 * (m // 2 + 1))
                            if m >= 2:
                                e.wait_ge(sACT.h, w_act0 + m - 1)
                            for kc in range(32):
                                inst = e.matmul(psA[par][:, :],
                                                wh[par][:, kc * 128:(kc + 1) * 128],
                                                h_sb[:, kc * FT:(kc + 1) * FT],
                                                start=(kc == 0), stop=(kc == 31))
                            sPE.bump(inst)
                    sec("tensor", _peW3)

                    def _actW3(e, w_pe0=w_pe0, w_dve0=w_dve0):
                        for m in range(8):
                            par = m % 2
                            e.wait_ge(sPE.h, w_pe0 + m + 1)
                            if m >= 2:
                                e.wait_ge(sDVE.h, w_dve0 + m - 1)
                            inst = e.activation(out=tf[par][:, 0:FT], in_=psA[par][:, :],
                                                func=AF.Copy,
                                                scale=sp[:, SP_W3S + m:SP_W3S + m + 1])
                            sACT.bump(inst)
                    sec("scalar", _actW3)

                    def _dveW3(e, w_act0=w_act0, xre_ld=xre_ld):
                        e.wait_ge(sL.h, xre_ld)
                        for m in range(8):
                            par = m % 2
                            e.wait_ge(sACT.h, w_act0 + m + 1)
                            sDVE.bump(e.tensor_add(out=xf[:, m * FT:(m + 1) * FT],
                                                   in0=xf[:, m * FT:(m + 1) * FT],
                                                   in1=tf[par][:, 0:FT]))
                    sec("vector", _dveW3)
                    prev_dve = sDVE.n

                    def _gpS(e, col0=col0, pd=prev_dve):
                        e.wait_ge(sDVE.h, pd)
                        dv = resid.rearrange("(c p) t -> p c t", p=128)[:, :, col0:col0 + FT]
                        sSt.bump(e.dma_start(out=dv,
                                             in_=xf[:].rearrange("p (c t) -> p c t", c=8)), 16)
                    sec("gpsimd", _gpS)
                    prev_st = sSt.n

        def attn(b):
            with ExitStack() as ats:
                ln3 = ats.enter_context(sbuf("a_ln3", [128, 8 * S], BF16))
                qb = ats.enter_context(sbuf("a_q", [128, 8 * S], BF16))
                kb = ats.enter_context(sbuf("a_k", [128, 8 * S], BF16))
                vb = ats.enter_context(sbuf("a_v", [128, 16 * 1024], BF16))
                s_f = ats.enter_context(sbuf("a_s", [128, S], F32))
                p_bf = ats.enter_context(sbuf("a_p", [128, S], BF16))
                pt_bf = ats.enter_context(sbuf("a_pt", [128, 16 * 128], BF16))
                o1 = ats.enter_context(sbuf("a_o1", [128, 8 * FT], BF16))
                rf = ats.enter_context(sbuf("a_rf", [128, 8 * FT], F32))
                nm = ats.enter_context(sbuf("a_nm", [128, 2], F32))
                ws = [ats.enter_context(sbuf(f"a_ws{i}", [128, 8 * 128], BF16))
                      for i in range(2)]
                wv_sb = ats.enter_context(sbuf("a_wv", [128, 8 * 1024], BF16))

                with ExitStack() as lns:
                    rows = lns.enter_context(sbuf("a_rows", [1, 4 * FT], F32))
                    bcast = lns.enter_context(sbuf("a_bcast", [128, 2 * FT], F32))
                    p_sx = lns.enter_context(psum("a_pssx", [1, FT], F32))
                    p_sq = lns.enter_context(psum("a_pssq", [1, FT], F32))
                    p_b1 = lns.enter_context(psum("a_psb1", [128, FT], F32))
                    p_b2 = lns.enter_context(psum("a_psb2", [128, FT], F32))
                    prev_apply = None
                    for sl in range(4):
                        col0 = b * S + sl * FT

                        def _syncx(e, col0=col0, pa=prev_apply):
                            if pa is not None:
                                e.wait_ge(sDVE.h, pa)
                            sv = resid.rearrange("(c p) t -> p c t", p=128)[:, :, col0:col0 + FT]
                            sL.bump(e.dma_start(out=rf[:].rearrange("p (c t) -> p c t", c=8),
                                                in_=sv), 16)
                        sec("sync", _syncx)
                        prev_apply = emit_ln(FT, rf[:], qb[:, 0:8 * FT], kb[:, 0:8 * FT],
                                             ln3[:], rows[:], bcast[:], p_sx, p_sq, p_b1, p_b2,
                                             sL.n, out_off=sl * FT, out_stride=S)
                    ln_done = prev_apply

                with ExitStack() as qs:
                    ps_s = [qs.enter_context(psum(f"a_pss{i}", [128, FT], F32))
                            for i in range(2)]
                    ps_t = [qs.enter_context(psum(f"a_pst{i}", [128, 128], BF16))
                            for i in range(2)]
                    ps_av = [qs.enter_context(psum(f"a_psav{i}", [128, 128], F32))
                             for i in range(2)]
                    ps_o = [qs.enter_context(psum(f"a_pso{i}", [128, FT], F32))
                            for i in range(2)]

                    def qk_stage(wsrc, out_sb, scale):
                        t_pe0, t_act0 = sPE.n, sACT.n
                        t_ldb = [sLw[0].n, sLw[1].n]

                        def _syncq(e, t_pe0=t_pe0):
                            for dkc in range(8):
                                par = dkc % 2
                                if dkc < 2:
                                    e.wait_ge(sPE.h, t_pe0)
                                else:
                                    e.wait_ge(sPE.h, t_pe0 + 4 * (dkc - 1))
                                sLw[par].bump(e.dma_start(
                                    out=ws[par][:].rearrange("p (c f) -> p c f", c=8),
                                    in_=wsrc[dkc, :, :, :]), 16)
                        sec("sync", _syncq)

                        def _peq(e, t_ldb=t_ldb, t_act0=t_act0):
                            e.wait_ge(sDVE.h, ln_done)
                            for dkc in range(8):
                                par = dkc % 2
                                e.wait_ge(sLw[par].h, t_ldb[par] + 16 * (dkc // 2 + 1))
                                for ts in range(4):
                                    u = dkc * 4 + ts
                                    if u >= 2:
                                        e.wait_ge(sACT.h, t_act0 + u - 1)
                                    for kc in range(8):
                                        inst = e.matmul(ps_s[u % 2][:, :],
                                                        ws[par][:, kc * 128:(kc + 1) * 128],
                                                        ln3[:, kc * S + ts * FT: kc * S + (ts + 1) * FT],
                                                        start=(kc == 0), stop=(kc == 7))
                                    sPE.bump(inst)
                        sec("tensor", _peq)

                        def _actq(e, t_pe0=t_pe0):
                            for dkc in range(8):
                                for ts in range(4):
                                    u = dkc * 4 + ts
                                    e.wait_ge(sPE.h, t_pe0 + u + 1)
                                    inst = e.activation(
                                        out=out_sb[:, dkc * S + ts * FT: dkc * S + (ts + 1) * FT],
                                        in_=ps_s[u % 2][:, :], func=AF.Copy, scale=scale)
                                    sACT.bump(inst)
                        sec("scalar", _actq)

                    qk_stage(wqb, qb, 0.03125)
                    qk_stage(wkb, kb, 1.0)

                    v_pe0, v_act0 = sPE.n, sACT.n

                    def _syncv(e):
                        for kc in range(8):
                            sLw[kc % 2].bump(e.dma_start(out=wv_sb[:, kc * 1024:(kc + 1) * 1024],
                                                         in_=wvb[kc * 128:(kc + 1) * 128, :]), 16)
                    sec("sync", _syncv)

                    v_ldt = [sLw[0].n, sLw[1].n]

                    def _pev(e, v_ldt=v_ldt, v_act0=v_act0):
                        e.wait_ge(sLw[0].h, v_ldt[0])
                        e.wait_ge(sLw[1].h, v_ldt[1])
                        for tt in range(16):
                            for half in range(2):
                                u = tt * 2 + half
                                if u < 2:
                                    e.wait_ge(sACT.h, v_act0)
                                else:
                                    e.wait_ge(sACT.h, v_act0 + u - 1)
                                for kc in range(8):
                                    inst = e.matmul(ps_s[u % 2][:, :],
                                                    ln3[:, kc * S + tt * 128: kc * S + (tt + 1) * 128],
                                                    wv_sb[:, kc * 1024 + half * 512: kc * 1024 + (half + 1) * 512],
                                                    start=(kc == 0), stop=(kc == 7))
                                sPE.bump(inst)
                    sec("tensor", _pev)

                    def _actv(e, v_pe0=v_pe0):
                        for tt in range(16):
                            for half in range(2):
                                u = tt * 2 + half
                                e.wait_ge(sPE.h, v_pe0 + u + 1)
                                inst = e.activation(
                                    out=vb[:, tt * 1024 + half * 512: tt * 1024 + (half + 1) * 512],
                                    in_=ps_s[u % 2][:, :], func=AF.Copy)
                                sACT.bump(inst)
                    sec("scalar", _actv)
                    v_done = sACT.n

                    prev_store = sSt.n
                    prev_rf_dve = sDVE.n
                    for qg in range(4):
                        o1_ev_last = None
                        for qq in range(4):
                            qi = qg * 4 + qq
                            valid = (qi + 1) * 128
                            nks = (valid + FT - 1) // FT

                            q_pe0, q_act0 = sPE.n, sACT.n

                            def _peS(e, qi=qi, valid=valid, nks=nks, q_act0=q_act0,
                                     vd=v_done):
                                if qi == 0:
                                    e.wait_ge(sACT.h, vd)
                                for ks in range(nks):
                                    w = min(FT, valid - ks * FT)
                                    if ks < 2:
                                        e.wait_ge(sACT.h, q_act0)
                                    else:
                                        e.wait_ge(sACT.h, q_act0 + ks - 1)
                                    for kc in range(8):
                                        inst = e.matmul(
                                            ps_s[ks % 2][:, 0:w],
                                            qb[:, kc * S + qi * 128: kc * S + (qi + 1) * 128],
                                            kb[:, kc * S + ks * FT: kc * S + ks * FT + w],
                                            start=(kc == 0), stop=(kc == 7))
                                    sPE.bump(inst)
                            sec("tensor", _peS)

                            def _actS(e, qi=qi, valid=valid, nks=nks, q_pe0=q_pe0,
                                      prd=prev_rf_dve):
                                e.wait_ge(sDVE.h, prd)
                                for ks in range(nks):
                                    w = min(FT, valid - ks * FT)
                                    e.wait_ge(sPE.h, q_pe0 + ks + 1)
                                    inst = e.activation(out=s_f[:, ks * FT: ks * FT + w],
                                                        in_=ps_s[ks % 2][:, 0:w], func=AF.Copy)
                                    sACT.bump(inst)
                            sec("scalar", _actS)
                            s_ev = sACT.n

                            def _dveS(e, qi=qi, valid=valid, s_ev=s_ev):
                                e.wait_ge(sACT.h, s_ev)
                                e.tensor_add(out=s_f[:, qi * 128:(qi + 1) * 128],
                                             in0=s_f[:, qi * 128:(qi + 1) * 128], in1=mask_sb[:])
                                sDVE.bump(e.tensor_reduce(out=nm[:, 0:1], in_=s_f[:, 0:valid],
                                                          axis=AX.X, op=ALU.max, negate=True))
                            sec("vector", _dveS)
                            nm_done = sDVE.n

                            def _actE2(e, valid=valid, nm_done=nm_done):
                                e.wait_ge(sDVE.h, nm_done)
                                inst = e.activation(out=p_bf[:, 0:valid], in_=s_f[:, 0:valid],
                                                    func=AF.Exp, bias=nm[:, 0:1],
                                                    accum_out=nm[:, 1:2])
                                sACT.bump(inst)
                            sec("scalar", _actE2)
                            exp_done = sACT.n

                            rc_cnt = sDVE.n + 1

                            def _dveN(e, valid=valid, exp_done=exp_done, rc_cnt=rc_cnt):
                                e.wait_ge(sACT.h, exp_done)
                                sDVE.bump(e.reciprocal(out=nm[:, 0:1], in_=nm[:, 1:2]))
                                e.wait_ge(sDVE.h, rc_cnt)
                                sDVE.bump(e.tensor_scalar_mul(out=p_bf[:, 0:valid],
                                                              in0=p_bf[:, 0:valid],
                                                              scalar1=nm[:, 0:1]))
                            sec("vector", _dveN)
                            pn_done = sDVE.n

                            t_pe0, t_dve0 = sPE.n, sDVE.n

                            def _peT(e, qi=qi, pn_done=pn_done, t_dve0=t_dve0):
                                e.wait_ge(sDVE.h, pn_done)
                                for kc in range(qi + 1):
                                    if kc >= 2:
                                        e.wait_ge(sDVE.h, t_dve0 + kc - 1)
                                    sPE.bump(e.transpose(ps_t[kc % 2][:, :],
                                                         p_bf[:, kc * 128:(kc + 1) * 128],
                                                         ident_b[:]))
                                sPE.bump(e.transpose(ps_t[qi % 2][:, :],
                                                     p_bf[:, qi * 128:(qi + 1) * 128],
                                                     ident_b[:]))
                            sec("tensor", _peT)

                            def _dveT(e, qi=qi, t_pe0=t_pe0):
                                for kc in range(qi + 1):
                                    e.wait_ge(sPE.h, t_pe0 + min(kc + 2, qi + 2))
                                    sDVE.bump(e.tensor_copy(out=pt_bf[:, kc * 128:(kc + 1) * 128],
                                                            in_=ps_t[kc % 2][:, :]))
                            sec("vector", _dveT)
                            pt_done = sDVE.n

                            a_pe0, a_act0 = sPE.n, sACT.n

                            def _peAV(e, qi=qi, qq=qq, pt_done=pt_done, a_act0=a_act0):
                                e.wait_ge(sDVE.h, pt_done)
                                for dvc in range(8):
                                    if dvc < 2:
                                        e.wait_ge(sACT.h, a_act0)
                                    else:
                                        e.wait_ge(sACT.h, a_act0 + dvc - 1)
                                    for kc in range(qi + 1):
                                        inst = e.matmul(ps_av[dvc % 2][:, :],
                                                        vb[:, kc * 1024 + dvc * 128: kc * 1024 + (dvc + 1) * 128],
                                                        pt_bf[:, kc * 128:(kc + 1) * 128],
                                                        start=(kc == 0), stop=(kc == qi))
                                    sPE.bump(inst)
                            sec("tensor", _peAV)

                            def _actAV(e, qq=qq, a_pe0=a_pe0):
                                for dvc in range(8):
                                    e.wait_ge(sPE.h, a_pe0 + dvc + 1)
                                    inst = e.activation(
                                        out=o1[:, dvc * FT + qq * 128: dvc * FT + (qq + 1) * 128],
                                        in_=ps_av[dvc % 2][:, :], func=AF.Copy)
                                    sACT.bump(inst)
                            sec("scalar", _actAV)
                            o1_ev_last = sACT.n

                        def _syncrf(e, qg=qg, ps=prev_store, prd=prev_rf_dve):
                            e.wait_ge(sSt.h, ps)
                            e.wait_ge(sDVE.h, prd)
                            sv = resid.rearrange("(c p) t -> p c t", p=128)[
                                :, :, b * S + qg * FT: b * S + (qg + 1) * FT]
                            sL.bump(e.dma_start(out=rf[:].rearrange("p (c t) -> p c t", c=8),
                                                in_=sv), 16)
                        sec("sync", _syncrf)
                        rf_ld = sL.n

                        o_pe0, o_act0, o_dve0 = sPE.n, sACT.n, sDVE.n
                        o_ldb = [sLw[0].n, sLw[1].n]

                        def _syncWo(e, o_pe0=o_pe0):
                            for dmc in range(8):
                                par = dmc % 2
                                if dmc < 2:
                                    e.wait_ge(sPE.h, o_pe0)
                                else:
                                    e.wait_ge(sPE.h, o_pe0 + dmc - 1)
                                sLw[par].bump(e.dma_start(
                                    out=ws[par][:].rearrange("p (c f) -> p c f", c=8),
                                    in_=wob[dmc, :, :, :]), 16)
                        sec("sync", _syncWo)

                        def _peWo(e, o1_ev_last=o1_ev_last, o_ldb=o_ldb, o_act0=o_act0):
                            e.wait_ge(sACT.h, o1_ev_last)
                            for dmc in range(8):
                                par = dmc % 2
                                e.wait_ge(sLw[par].h, o_ldb[par] + 16 * (dmc // 2 + 1))
                                if dmc >= 2:
                                    e.wait_ge(sACT.h, o_act0 + dmc - 1)
                                for dvc in range(8):
                                    inst = e.matmul(ps_o[dmc % 2][:, :],
                                                    ws[par][:, dvc * 128:(dvc + 1) * 128],
                                                    o1[:, dvc * FT:(dvc + 1) * FT],
                                                    start=(dvc == 0), stop=(dvc == 7))
                                sPE.bump(inst)
                        sec("tensor", _peWo)

                        def _actWo(e, o_pe0=o_pe0, o_dve0=o_dve0):
                            for dmc in range(8):
                                par = dmc % 2
                                e.wait_ge(sPE.h, o_pe0 + dmc + 1)
                                if dmc >= 2:
                                    e.wait_ge(sDVE.h, o_dve0 + dmc - 1)
                                inst = e.activation(out=s_f[:, (dmc % 2) * FT:(dmc % 2) * FT + FT],
                                                    in_=ps_o[dmc % 2][:, :], func=AF.Copy)
                                sACT.bump(inst)
                        sec("scalar", _actWo)

                        def _dveWo(e, o_act0=o_act0, rf_ld=rf_ld):
                            e.wait_ge(sL.h, rf_ld)
                            for dmc in range(8):
                                e.wait_ge(sACT.h, o_act0 + dmc + 1)
                                sDVE.bump(e.tensor_add(
                                    out=rf[:, dmc * FT:(dmc + 1) * FT],
                                    in0=rf[:, dmc * FT:(dmc + 1) * FT],
                                    in1=s_f[:, (dmc % 2) * FT:(dmc % 2) * FT + FT]))
                        sec("vector", _dveWo)
                        prev_rf_dve = sDVE.n

                        def _gpWo(e, qg=qg, prd=prev_rf_dve):
                            e.wait_ge(sDVE.h, prd)
                            dv = resid.rearrange("(c p) t -> p c t", p=128)[
                                :, :, b * S + qg * FT: b * S + (qg + 1) * FT]
                            sSt.bump(e.dma_start(out=dv,
                                                 in_=rf[:].rearrange("p (c t) -> p c t", c=8)), 16)
                        sec("gpsimd", _gpWo)
                        prev_store = sSt.n

        parts = BUILD_PARTS
        p0()
        if "cast" in parts:
            cast_weights()
        if "ingest" in parts:
            ingest()
        if "barrier" in parts:
            barrier()
        if stages >= 2:
            for b in range(BLOC):
                mamba(b)
                barrier()
        if stages >= 3:
            ffn()
            barrier()
        if stages >= 4:
            for b in range(BLOC):
                attn(b)
                barrier()
        if stages >= 5:
            ffn()
            barrier()
        output_phase()

    return nc


def _sp_pack(inp):
    sp = np.zeros((128, SPC), np.float32)
    cw = np.asarray(inp["L0_attn_a_conv_w"], np.float32).reshape(DI, 4)
    for j in range(4):
        sp[:, SP_CW + j * 16: SP_CW + (j + 1) * 16] = cw[:, j].reshape(16, 128).T
    sp[:, SP_CB:SP_CB + 16] = np.asarray(inp["L0_attn_a_conv_b"], np.float32).reshape(16, 128).T
    sp[:, SP_DTB:SP_DTB + 16] = np.asarray(inp["L0_attn_a_dt_proj_b"], np.float32).reshape(16, 128).T
    sp[:, SP_D:SP_D + 16] = np.asarray(inp["L0_attn_a_D"], np.float32).reshape(16, 128).T
    al = np.asarray(inp["L0_attn_a_A_log"], np.float32).reshape(16, 128, NST)
    sp[:, SP_AL:SP_AL + 256] = al.transpose(1, 0, 2).reshape(128, 256)
    sp[:, SP_NW:SP_NW + 8] = np.asarray(inp["L0_attn_a_norm_w"], np.float32).reshape(8, 128).T
    sp[:, SP_S1:SP_S1 + 32] = np.asarray(inp["L0_ffn_w1_s"], np.float32).reshape(32, 128).T
    sp[:, SP_S2:SP_S2 + 32] = np.asarray(inp["L0_ffn_w2_s"], np.float32).reshape(32, 128).T
    sp[:, SP_H1S:SP_H1S + 32] = np.asarray(inp["L0_ffn_h1_s"], np.float32).reshape(32, 128).T
    sp[:, SP_H2S:SP_H2S + 32] = np.asarray(inp["L0_ffn_h2_s"], np.float32).reshape(32, 128).T
    sp[:, SP_W3S:SP_W3S + 8] = np.asarray(inp["L0_ffn_w3_s"], np.float32).reshape(8, 128).T
    return sp


def kernel(**inp):
    x = np.ascontiguousarray(np.asarray(inp["L0_attn_a_x"], np.float32))
    sp = _sp_pack(inp)
    shared = dict(
        sp_in=sp,
        ipw=np.ascontiguousarray(np.asarray(inp["L0_attn_a_in_proj_w"], np.float32)),
        xpw=np.ascontiguousarray(np.asarray(inp["L0_attn_a_x_proj_w"], np.float32)),
        dpw=np.ascontiguousarray(np.asarray(inp["L0_attn_a_dt_proj_w"], np.float32)),
        opw=np.ascontiguousarray(np.asarray(inp["L0_attn_a_out_proj_w"], np.float32)),
        wq=np.ascontiguousarray(np.asarray(inp["L0_attn_b_Wq"], np.float32)),
        wk=np.ascontiguousarray(np.asarray(inp["L0_attn_b_Wk"], np.float32)),
        wv=np.ascontiguousarray(np.asarray(inp["L0_attn_b_Wv"], np.float32)),
        wo=np.ascontiguousarray(np.asarray(inp["L0_attn_b_Wo"], np.float32)),
        w1t=np.ascontiguousarray(np.asarray(inp["L0_ffn_w1_w"], np.float32).T),
        w2t=np.ascontiguousarray(np.asarray(inp["L0_ffn_w2_w"], np.float32).T),
        h1t=np.ascontiguousarray(np.asarray(inp["L0_ffn_h1_w"], np.float32).T),
        h2t=np.ascontiguousarray(np.asarray(inp["L0_ffn_h2_w"], np.float32).T),
        w3t=np.ascontiguousarray(np.asarray(inp["L0_ffn_w3_w"], np.float32).T),
    )
    nc = build(STAGES)
    in_maps = []
    for c in range(NCORE):
        m = dict(shared)
        m["x_in"] = np.ascontiguousarray(x[c * BLOC:(c + 1) * BLOC].reshape(TOK, D))
        in_maps.append(m)
    res = run_bass_kernel_spmd(nc, in_maps, core_ids=list(range(NCORE)))
    out = np.concatenate(
        [np.asarray(r["out"], np.float32).reshape(BLOC, S, D) for r in res.results], axis=0)
    return out


# version 04
# speedup vs baseline: 1.0792x; 1.0088x over previous
import numpy as np
from contextlib import ExitStack

import concourse.bass as bass
import concourse.mybir as mybir
from concourse.bass_utils import run_bass_kernel_spmd

F32 = mybir.dt.float32
BF16 = mybir.dt.bfloat16
AF = mybir.ActivationFunctionType
ALU = mybir.AluOpType
AX = mybir.AxisListType

NCORE = 8
BLOC = 2
S = 2048
D = 1024
DI = 2048
NST = 16
DTR = 64
DH = 4096
TOK = BLOC * S
EPS = 1e-5

TC = 256
NCH = S // TC
SEG = TC + 1
NG = 4
NSTG = 8
NIT = 32
FT = 512
NFT = TOK // FT

SP_CW = 0
SP_CB = 64
SP_DTB = 80
SP_D = 96
SP_AL = 112
SP_NW = 368
SP_S1 = 376
SP_S2 = 408
SP_H1S = 440
SP_H2S = 472
SP_W3S = 504
SPC = 512

STAGES = 5
BUILD_PARTS = ("cast", "ingest", "barrier")
PARANOID = False
MAMBA_NOSCAN = False
DBU_DVE = False
import os as _os
SCAN_DELAY = _os.environ.get("SCAN_DELAY", "")
DEBUG_TAPS = False
SERIAL_W = False


class Sem:
    def __init__(self, nc, es, name):
        self.h = es.enter_context(nc.semaphore(name))
        self.n = 0

    def bump(self, inst, k=1):
        inst.then_inc(self.h, k)
        self.n += k
        return inst


def build(stages=STAGES):
    nc = bass.Bass("TRN2", target_bir_lowering=False, detect_race_conditions=False)

    x_in = nc.dram_tensor("x_in", [TOK, D], F32, kind="ExternalInput").ap()
    sp_in = nc.dram_tensor("sp_in", [128, SPC], F32, kind="ExternalInput").ap()
    ipw = nc.dram_tensor("ipw", [D, 2 * DI], F32, kind="ExternalInput").ap()
    xpw = nc.dram_tensor("xpw", [DI, DTR + 2 * NST], F32, kind="ExternalInput").ap()
    dpw = nc.dram_tensor("dpw", [DTR, DI], F32, kind="ExternalInput").ap()
    opw = nc.dram_tensor("opw", [DI, D], F32, kind="ExternalInput").ap()
    wq = nc.dram_tensor("wq", [D, D], F32, kind="ExternalInput").ap()
    wk = nc.dram_tensor("wk", [D, D], F32, kind="ExternalInput").ap()
    wv = nc.dram_tensor("wv", [D, D], F32, kind="ExternalInput").ap()
    wo = nc.dram_tensor("wo", [D, D], F32, kind="ExternalInput").ap()
    w1t = nc.dram_tensor("w1t", [D, DH], F32, kind="ExternalInput").ap()
    w2t = nc.dram_tensor("w2t", [D, DH], F32, kind="ExternalInput").ap()
    h1t = nc.dram_tensor("h1t", [DH, DH], F32, kind="ExternalInput").ap()
    h2t = nc.dram_tensor("h2t", [DH, DH], F32, kind="ExternalInput").ap()
    w3t = nc.dram_tensor("w3t", [DH, D], F32, kind="ExternalInput").ap()

    out_t = nc.dram_tensor("out", [TOK, D], F32, kind="ExternalOutput").ap()

    resid = nc.dram_tensor("resid", [D, TOK], F32, kind="Internal").ap()
    ipb = nc.dram_tensor("ipb", [32, 128, 8, 128], BF16, kind="Internal").ap()
    opb = nc.dram_tensor("opb", [8, 128, 16, 128], BF16, kind="Internal").ap()
    wqb = nc.dram_tensor("wqb", [8, 128, 8, 128], BF16, kind="Internal").ap()
    wkb = nc.dram_tensor("wkb", [8, 128, 8, 128], BF16, kind="Internal").ap()
    wob = nc.dram_tensor("wob", [8, 128, 8, 128], BF16, kind="Internal").ap()
    wvb = nc.dram_tensor("wvb", [D, D], BF16, kind="Internal").ap()
    w1b = nc.dram_tensor("w1b", [32, 128, 8, 128], BF16, kind="Internal").ap()
    w2b = nc.dram_tensor("w2b", [32, 128, 8, 128], BF16, kind="Internal").ap()
    h1b = nc.dram_tensor("h1b", [32, 128, 32, 128], BF16, kind="Internal").ap()
    h2b = nc.dram_tensor("h2b", [32, 128, 32, 128], BF16, kind="Internal").ap()
    w3b = nc.dram_tensor("w3b", [8, 128, 32, 128], BF16, kind="Internal").ap()

    taps = {}
    if DEBUG_TAPS:
        taps["xn"] = nc.dram_tensor("tap_xn", [NCH, 128, 8 * TC], BF16, kind="ExternalOutput").ap()
        taps["xi"] = nc.dram_tensor("tap_xi", [NCH, 128, 16 * (TC + 3)], F32, kind="ExternalOutput").ap()
        taps["res"] = nc.dram_tensor("tap_res", [NCH, 128, 16 * TC], F32, kind="ExternalOutput").ap()
        taps["dlt"] = nc.dram_tensor("tap_dlt", [NCH, 128, 16 * TC], F32, kind="ExternalOutput").ap()
        taps["u0"] = nc.dram_tensor("tap_u0", [NCH, 128, 16 * TC], F32, kind="ExternalOutput").ap()
        taps["y"] = nc.dram_tensor("tap_y", [NCH, 128, 16 * TC], F32, kind="ExternalOutput").ap()
        taps["newx"] = nc.dram_tensor("tap_newx", [NCH, 128, 8 * TC], F32, kind="ExternalOutput").ap()
        taps["sub"] = nc.dram_tensor("tap_sub", [NCH, 128, 8 * TC], F32, kind="ExternalOutput").ap()
        taps["bcst"] = nc.dram_tensor("tap_bcst", [NCH, 128, 3 * TC], F32, kind="ExternalOutput").ap()
        taps["row"] = nc.dram_tensor("tap_row", [NCH, 1, 6 * TC], F32, kind="ExternalOutput").ap()
        taps["xld"] = nc.dram_tensor("tap_xld", [NCH, 128, 4 * TC], F32, kind="ExternalOutput").ap()

    es = ExitStack()
    with es:
        sp = es.enter_context(nc.sbuf_tensor("sp", [128, SPC], F32))
        a_sb = es.enter_context(nc.sbuf_tensor("a_sb", [128, 256], F32))
        ident_f = es.enter_context(nc.sbuf_tensor("ident_f", [128, 128], F32))
        ident_b = es.enter_context(nc.sbuf_tensor("ident_b", [128, 128], BF16))
        mask_sb = es.enter_context(nc.sbuf_tensor("mask_sb", [128, 128], F32))
        ones_col_b = es.enter_context(nc.sbuf_tensor("ones_col_b", [128, 1], BF16))
        ones_row_f = es.enter_context(nc.sbuf_tensor("ones_row_f", [1, 128], F32))

        sL = Sem(nc, es, "sL")
        sLw = [Sem(nc, es, "sLw0"), Sem(nc, es, "sLw1")]
        sSt = Sem(nc, es, "sSt")
        sStp = [Sem(nc, es, "sStp0"), Sem(nc, es, "sStp1")]
        sPE = Sem(nc, es, "sPE")
        sACT = Sem(nc, es, "sACT")
        sDVE = Sem(nc, es, "sDVE")
        sPOOL = Sem(nc, es, "sPOOL")
        sBar = Sem(nc, es, "sBar")

        _uid = [0]

        def sbuf(name, shape, dtype):
            _uid[0] += 1
            return nc.sbuf_tensor(f"{name}_{_uid[0]}", shape, dtype)

        def psum(name, shape, dtype=F32):
            _uid[0] += 1
            return nc.psum_tensor(f"{name}_{_uid[0]}", shape, dtype)

        block = es.enter_context(nc.Block())
        ENG = dict(tensor=block.tensor, vector=block.vector, scalar=block.scalar,
                   gpsimd=block.gpsimd, sync=block.sync)

        _in_barrier = [False]

        def sec(name, fn):
            ENG[name](fn)
            if PARANOID and not _in_barrier[0]:
                barrier()

        def barrier():
            _in_barrier[0] = True
            for en in ("tensor", "vector", "scalar", "gpsimd"):
                def _f(e):
                    e.drain()
                    e.sem_inc(sBar.h, 1)
                sec(en, _f)
                sBar.n += 1
            tgt, pl = sBar.n, sPOOL.n
            dma_tot = [(s.h, s.n) for s in (sL, sLw[0], sLw[1], sSt, sStp[0], sStp[1])]
            for en in ("tensor", "vector", "scalar", "gpsimd", "sync"):
                def _f(e, tgt=tgt, pl=pl, dma_tot=dma_tot):
                    e.wait_ge(sBar.h, tgt)
                    for h, v in dma_tot:
                        e.wait_ge(h, v)
                    e.wait_ge(sPOOL.h, pl)
                sec(en, _f)
            _in_barrier[0] = False

        def p0():
            def _sync(e):
                sL.bump(e.dma_start(out=sp[:], in_=sp_in), 16)
            sec("sync", _sync)
            ld_sp = sL.n

            def _gp(e):
                e.memset(ident_f[:], 0.0)
                e.memset(ident_b[:], 0.0)
                e.memset(mask_sb[:], 0.0)
                e.memset(ones_col_b[:], 1.0)
                e.memset(ones_row_f[:], 1.0)
                e.affine_select(out=ident_f[:], in_=ident_f[:], compare_op=ALU.not_equal,
                                fill=1.0, base=0, pattern=[[-1, 128]], channel_multiplier=1)
                e.affine_select(out=ident_b[:], in_=ident_b[:], compare_op=ALU.not_equal,
                                fill=1.0, base=0, pattern=[[-1, 128]], channel_multiplier=1)
                inst = e.affine_select(out=mask_sb[:], in_=mask_sb[:], compare_op=ALU.is_ge,
                                       fill=-3.0e38, base=0, pattern=[[-1, 128]], channel_multiplier=1)
                sPOOL.bump(inst)
            sec("gpsimd", _gp)

            def _act(e, ld_sp=ld_sp):
                e.wait_ge(sL.h, ld_sp)
                sACT.bump(e.activation(out=a_sb[:], in_=sp[:, SP_AL:SP_AL + 256], func=AF.Exp))
            sec("scalar", _act)
            act_a = sACT.n

            def _dve(e, act_a=act_a):
                e.wait_ge(sACT.h, act_a)
                sDVE.bump(e.tensor_scalar_mul(out=a_sb[:], in0=a_sb[:], scalar1=-1.0))
            sec("vector", _dve)

        def cast_weights():
            jobs = [(ipw, ipb, 8, 4096), (opw, opb, 16, 1024)]
            if stages >= 4:
                jobs += [(wq, wqb, 8, 1024), (wk, wkb, 8, 1024), (wo, wob, 8, 1024),
                         (wv, None, 8, 1024)]
            if stages >= 3:
                jobs += [(w1t, w1b, 8, 4096), (w2t, w2b, 8, 4096),
                         (h1t, h1b, 32, 4096), (h2t, h2b, 32, 4096), (w3t, w3b, 32, 1024)]
            tiles = []
            for src, dst, KC, M in jobs:
                for kc in range(KC):
                    tiles.append((src, dst, kc, M))
            n = len(tiles)
            with (sbuf("castf", [128, 2 * 4096], F32) as castf,
                  sbuf("castb", [128, 2 * 4096], BF16) as castb):
                dve0 = sDVE.n
                ldb = [sLw[0].n, sLw[1].n]
                stb = [sStp[0].n, sStp[1].n]
                st_counts = []
                acc = [stb[0], stb[1]]
                for i, (src, dst, kc, M) in enumerate(tiles):
                    n16 = 16 if dst is None else 16 * ((M // 128 + 7) // 8)
                    acc[i % 2] += n16
                    st_counts.append(acc[i % 2])

                def _sync(e):
                    for i, (src, dst, kc, M) in enumerate(tiles):
                        par = i % 2
                        if i >= 2:
                            e.wait_ge(sDVE.h, dve0 + (i - 1))
                        sLw[par].bump(e.dma_start(out=castf[:, par * 4096: par * 4096 + M],
                                                  in_=src[kc * 128:(kc + 1) * 128, :]), 16)
                sec("sync", _sync)

                def _dve(e):
                    for i, (src, dst, kc, M) in enumerate(tiles):
                        par = i % 2
                        e.wait_ge(sLw[par].h, ldb[par] + 16 * (i // 2 + 1))
                        if i >= 2:
                            e.wait_ge(sStp[par].h, st_counts[i - 2])
                        sDVE.bump(e.tensor_copy(out=castb[:, par * 4096: par * 4096 + M],
                                                in_=castf[:, par * 4096: par * 4096 + M]))
                sec("vector", _dve)

                def _gp(e):
                    for i, (src, dst, kc, M) in enumerate(tiles):
                        par = i % 2
                        e.wait_ge(sDVE.h, dve0 + i + 1)
                        if dst is None:
                            sStp[par].bump(e.dma_start(out=wvb[kc * 128:(kc + 1) * 128, :],
                                                       in_=castb[:, par * 4096: par * 4096 + M]), 16)
                        else:
                            MC = M // 128
                            for m0 in range(0, MC, 8):
                                mn = min(8, MC - m0)
                                dv = dst[m0:m0 + mn, :, kc:kc + 1, :].transpose([1, 0, 2, 3])
                                sv = castb[:, par * 4096 + m0 * 128: par * 4096 + (m0 + mn) * 128]
                                sv = sv.rearrange("p (a b) -> p a b", b=128).unsqueeze(2)
                                sStp[par].bump(e.dma_start(out=dv, in_=sv), 16)
                sec("gpsimd", _gp)

        def ingest():
            with (sbuf("ing_f", [128, 2 * 1024], F32) as ing_f,
                  sbuf("ing_o", [128, 2 * 1024], F32) as ing_o,
                  psum("ps_ing0", [128, 512], F32) as ps0,
                  psum("ps_ing1", [128, 512], F32) as ps1,
                  psum("ps_ing2", [128, 512], F32) as ps2,
                  psum("ps_ing3", [128, 512], F32) as ps3):
                psl = [ps0, ps1, ps2, ps3]
                NTI = TOK // 128
                pe0, dve0 = sPE.n, sDVE.n
                ldb = [sLw[0].n, sLw[1].n]
                stb = [sStp[0].n, sStp[1].n]

                def _sync(e):
                    for i in range(NTI):
                        if i >= 2:
                            e.wait_ge(sPE.h, pe0 + (i - 1) * 9)
                        sLw[i % 2].bump(e.dma_start(out=ing_f[:, (i % 2) * 1024:(i % 2 + 1) * 1024],
                                                    in_=x_in[i * 128:(i + 1) * 128, :]), 16)
                sec("sync", _sync)

                def _pe(e):
                    for i in range(NTI):
                        par = i % 2
                        e.wait_ge(sLw[par].h, ldb[par] + 16 * (i // 2 + 1))
                        if i >= 1:
                            e.wait_ge(sDVE.h, dve0 + i * 8)
                        for dc in range(8):
                            pb = psl[dc % 4]
                            sPE.bump(e.transpose(pb[:, 0:128],
                                                 ing_f[:, par * 1024 + dc * 128: par * 1024 + (dc + 1) * 128],
                                                 ident_f[:]))
                            if dc >= 3:
                                e.wait_ge(sDVE.h, dve0 + i * 8 + dc - 2)
                        sPE.bump(e.transpose(psl[3][:, 0:128],
                                             ing_f[:, par * 1024 + 7 * 128: par * 1024 + 8 * 128],
                                             ident_f[:]))
                sec("tensor", _pe)

                def _dve(e):
                    for i in range(NTI):
                        par = i % 2
                        if i >= 2:
                            e.wait_ge(sStp[par].h, stb[par] + 16 * (i // 2))
                        for dc in range(8):
                            pb = psl[dc % 4]
                            e.wait_ge(sPE.h, pe0 + i * 9 + dc + 2)
                            sDVE.bump(e.tensor_copy(
                                out=ing_o[:, par * 1024 + dc * 128: par * 1024 + (dc + 1) * 128],
                                in_=pb[:, 0:128]))
                sec("vector", _dve)

                def _gp(e):
                    for i in range(NTI):
                        par = i % 2
                        e.wait_ge(sDVE.h, dve0 + (i + 1) * 8)
                        dv = resid.rearrange("(c p) t -> p c t", p=128)[:, :, i * 128:(i + 1) * 128]
                        sv = ing_o[:, par * 1024:(par + 1) * 1024].rearrange("p (c t) -> p c t", c=8)
                        sStp[par].bump(e.dma_start(out=dv, in_=sv), 16)
                sec("gpsimd", _gp)

        def output_phase():
            with (sbuf("o_f", [128, 2 * 1024], F32) as o_f,
                  sbuf("o_o", [128, 2 * 1024], F32) as o_o,
                  psum("ps_of0", [128, 512], F32) as ps0,
                  psum("ps_of1", [128, 512], F32) as ps1,
                  psum("ps_of2", [128, 512], F32) as ps2,
                  psum("ps_of3", [128, 512], F32) as ps3):
                psl = [ps0, ps1, ps2, ps3]
                NTI = TOK // 128
                pe0, dve0 = sPE.n, sDVE.n
                ldb = [sLw[0].n, sLw[1].n]
                stb = [sStp[0].n, sStp[1].n]

                def _sync(e):
                    for i in range(NTI):
                        if i >= 2:
                            e.wait_ge(sPE.h, pe0 + (i - 1) * 9)
                        sv = resid.rearrange("(c p) t -> p c t", p=128)[:, :, i * 128:(i + 1) * 128]
                        dv = o_f[:, (i % 2) * 1024:(i % 2 + 1) * 1024].rearrange("p (c t) -> p c t", c=8)
                        sLw[i % 2].bump(e.dma_start(out=dv, in_=sv), 16)
                sec("sync", _sync)

                def _pe(e):
                    for i in range(NTI):
                        par = i % 2
                        e.wait_ge(sLw[par].h, ldb[par] + 16 * (i // 2 + 1))
                        if i >= 1:
                            e.wait_ge(sDVE.h, dve0 + i * 8)
                        for dc in range(8):
                            pb = psl[dc % 4]
                            sPE.bump(e.transpose(pb[:, 0:128],
                                                 o_f[:, par * 1024 + dc * 128: par * 1024 + (dc + 1) * 128],
                                                 ident_f[:]))
                            if dc >= 3:
                                e.wait_ge(sDVE.h, dve0 + i * 8 + dc - 2)
                        sPE.bump(e.transpose(psl[3][:, 0:128],
                                             o_f[:, par * 1024 + 7 * 128: par * 1024 + 8 * 128],
                                             ident_f[:]))
                sec("tensor", _pe)

                def _dve(e):
                    for i in range(NTI):
                        par = i % 2
                        if i >= 2:
                            e.wait_ge(sStp[par].h, stb[par] + 16 * (i // 2))
                        for dc in range(8):
                            pb = psl[dc % 4]
                            e.wait_ge(sPE.h, pe0 + i * 9 + dc + 2)
                            sDVE.bump(e.tensor_copy(
                                out=o_o[:, par * 1024 + dc * 128: par * 1024 + (dc + 1) * 128],
                                in_=pb[:, 0:128]))
                sec("vector", _dve)

                def _gp(e):
                    for i in range(NTI):
                        par = i % 2
                        e.wait_ge(sDVE.h, dve0 + (i + 1) * 8)
                        sStp[par].bump(e.dma_start(out=out_t[i * 128:(i + 1) * 128, :],
                                                   in_=o_o[:, par * 1024:(par + 1) * 1024]), 16)
                    e.wait_ge(sStp[0].h, sStp[0].n)
                    e.wait_ge(sStp[1].h, sStp[1].n)
                    e.wait_ge(sSt.h, sSt.n)
                sec("gpsimd", _gp)

        def emit_ln(W, xf, xbf, sqbf, out_bf, rows, bcast, p_sx, p_sq, p_b1, p_b2, wait_ld,
                    out_off=0, out_stride=None):
            if out_stride is None:
                out_stride = W

            pe_now = sPE.n

            def _dve(e, pe_now=pe_now):
                e.wait_ge(sL.h, wait_ld)
                e.wait_ge(sPE.h, pe_now)
                sDVE.bump(e.tensor_copy(out=xbf[:, 0:8 * W], in_=xf[:, 0:8 * W]))
            sec("vector", _dve)
            c1 = sDVE.n

            def _act(e, c1=c1):
                e.wait_ge(sDVE.h, c1)
                sACT.bump(e.activation(out=sqbf[:, 0:8 * W], in_=xbf[:, 0:8 * W], func=AF.Square))
            sec("scalar", _act)
            c2 = sACT.n

            def _pe(e, c2=c2):
                e.wait_ge(sACT.h, c2)
                for dc in range(8):
                    e.matmul(p_sx[0:1, 0:W], ones_col_b[:, 0:1], xbf[:, dc * W:(dc + 1) * W],
                             start=(dc == 0), stop=(dc == 7))
                for dc in range(8):
                    inst = e.matmul(p_sq[0:1, 0:W], ones_col_b[:, 0:1], sqbf[:, dc * W:(dc + 1) * W],
                                    start=(dc == 0), stop=(dc == 7))
                sPE.bump(inst)
            sec("tensor", _pe)
            c3 = sPE.n

            mean_r, rstd_r, t0_r, t1_r = (rows[0:1, i * W:(i + 1) * W] for i in range(4))

            def _dve2(e, c3=c3):
                e.wait_ge(sPE.h, c3)
                e.tensor_scalar_mul(out=mean_r, in0=p_sx[0:1, 0:W], scalar1=1.0 / D)
                e.tensor_scalar_mul(out=t0_r, in0=p_sq[0:1, 0:W], scalar1=1.0 / D)
                e.tensor_mul(out=t1_r, in0=mean_r, in1=mean_r)
                e.tensor_sub(out=t0_r, in0=t0_r, in1=t1_r)
                sDVE.bump(e.tensor_scalar_add(out=t0_r, in0=t0_r, scalar1=EPS))
            sec("vector", _dve2)
            c4 = sDVE.n

            def _act2(e, c4=c4):
                e.wait_ge(sDVE.h, c4)
                e.activation(out=t1_r, in_=t0_r, func=AF.Ln)
                sACT.bump(e.activation(out=rstd_r, in_=t1_r, func=AF.Exp, scale=-0.5))
            sec("scalar", _act2)
            c5 = sACT.n

            def _pe2(e, c5=c5):
                e.wait_ge(sACT.h, c5)
                e.matmul(p_b1[:, 0:W], ones_row_f[0:1, :], mean_r, start=True, stop=True)
                sPE.bump(e.matmul(p_b2[:, 0:W], ones_row_f[0:1, :], rstd_r, start=True, stop=True))
            sec("tensor", _pe2)
            c6 = sPE.n

            def _act3(e, c6=c6):
                e.wait_ge(sPE.h, c6)
                e.activation(out=bcast[:, 0:W], in_=p_b1[:, 0:W], func=AF.Copy)
                sACT.bump(e.activation(out=bcast[:, W:2 * W], in_=p_b2[:, 0:W], func=AF.Copy))
            sec("scalar", _act3)
            c7 = sACT.n

            def _dve3(e, c7=c7):
                e.wait_ge(sACT.h, c7)
                for dc in range(8):
                    e.tensor_sub(out=xf[:, dc * W:(dc + 1) * W],
                                 in0=xf[:, dc * W:(dc + 1) * W], in1=bcast[:, 0:W])
                    inst = e.tensor_mul(out=out_bf[:, out_off + dc * out_stride:
                                               out_off + dc * out_stride + W],
                                        in0=xf[:, dc * W:(dc + 1) * W], in1=bcast[:, W:2 * W])
                sDVE.bump(inst)
            sec("vector", _dve3)
            return sDVE.n

        def mamba(b):
            with ExitStack() as ms:
                x_sb = ms.enter_context(sbuf("m_x", [128, 8 * TC], F32))
                xi_sb = ms.enter_context(sbuf("m_xi", [128, 16 * (TC + 3)], F32))
                xi2_sb = ms.enter_context(sbuf("m_xi2", [128, 16 * TC], F32))
                res_sb = ms.enter_context(sbuf("m_res", [128, 16 * TC], F32))
                dlt_sb = ms.enter_context(sbuf("m_dlt", [128, 16 * TC], F32))
                y_sb = ms.enter_context(sbuf("m_y", [128, 16 * TC], F32))
                xi2b_sb = ms.enter_context(sbuf("m_xi2b", [128, 16 * TC], BF16))
                xbf_sb = xi2b_sb[:, 0:8 * TC]
                sqbf_sb = xi2b_sb[:, 8 * TC:16 * TC]
                xn_sb = xi2b_sb[:, 0:8 * TC]
                drb_sb = ms.enter_context(sbuf("m_drb", [64, TC], BF16))
                bc_sb = ms.enter_context(sbuf("m_bc", [32, TC], F32))
                bb_sb = ms.enter_context(sbuf("m_bb", [128, NSTG * TC], F32))
                cb_sb = ms.enter_context(sbuf("m_cb", [128, NSTG * TC], F32))
                da_sb = [xi_sb[:, 0:NSTG * SEG],
                         xi_sb[:, NSTG * SEG:2 * NSTG * SEG]]
                du_sb = [ms.enter_context(sbuf(f"m_du{i}", [128, NSTG * SEG], F32))
                         for i in range(2)]
                s_sb = [ms.enter_context(sbuf(f"m_s{i}", [128, NSTG * SEG], F32))
                        for i in range(2)]
                ct_sb = [ms.enter_context(sbuf(f"m_ct{i}", [128, NSTG * TC], F32))
                         for i in range(2)]
                yt_sb = ms.enter_context(sbuf("m_yt", [128, TC], F32))
                st_sb = ms.enter_context(sbuf("m_st", [128, 256], F32))
                halo_sb = ms.enter_context(sbuf("m_halo", [128, 16 * 3], F32))
                row_sb = ms.enter_context(sbuf("m_row", [1, 6 * TC], F32))
                bcst_sb = ms.enter_context(sbuf("m_bcst", [128, 3 * TC], F32))
                wi_sb = [ms.enter_context(sbuf(f"m_wi{i}", [128, 8 * 128], BF16))
                         for i in range(2)]
                xp_sb = ms.enter_context(sbuf("m_xp", [128, 16 * 96], BF16))
                dp_sb = ms.enter_context(sbuf("m_dp", [64, DI], BF16))
                xpf_sb = y_sb[:, 0:16 * 96]
                dpf_sb = dlt_sb[0:64, 0:DI]
                sptmp = [ms.enter_context(sbuf(f"m_spt{i}", [128, TC], F32))
                         for i in range(2)]
                sel = ms.enter_context(sbuf("m_sel", [32, 32 * 128], F32))
                xld_sb = ct_sb[0][:, 0:4 * TC] if DEBUG_TAPS else None

                def _gpsel(e):
                    e.memset(sel[:], 0.0)
                    e.affine_select(out=sel[:], in_=sel[:], compare_op=ALU.not_equal,
                                    fill=1.0, base=0, pattern=[[-1, 32], [0, 128]],
                                    channel_multiplier=1)
                    sPOOL.bump(e.tensor_copy(out=sel[0:32, 0:4], in_=sel[0:32, 0:4]))
                sec("gpsimd", _gpsel)
                sel_done = sPOOL.n

                ps_w = [ms.enter_context(psum(f"m_psw{i}", [128, 512], F32))
                        for i in range(2)]
                ps_sx = ms.enter_context(psum("m_pssx", [1, TC], F32))
                ps_sq = ms.enter_context(psum("m_pssq", [1, TC], F32))
                ps_b1 = ms.enter_context(psum("m_psb1", [128, 512], F32))
                ps_b2 = ms.enter_context(psum("m_psb2", [128, 512], F32))
                ps_b3 = ms.enter_context(psum("m_psb3", [128, 512], F32))
                ps_x = ms.enter_context(psum("m_psx", [96, TC], F32))

                ld0 = sL.n

                def _sync0(e):
                    sv = xpw.rearrange("(c p) f -> p c f", p=128)
                    sL.bump(e.dma_start(out=xpf_sb.rearrange("p (c f) -> p c f", c=16),
                                        in_=sv), 16)
                    sL.bump(e.dma_start(out=dpf_sb, in_=dpw), 16)
                sec("sync", _sync0)

                def _dve0(e, ld0=sL.n):
                    e.wait_ge(sL.h, ld0)
                    e.tensor_copy(out=xp_sb[:], in_=xpf_sb)
                    e.tensor_copy(out=dp_sb[:], in_=dpf_sb)
                    e.memset(st_sb[:], 0.0)
                    sDVE.bump(e.memset(halo_sb[:], 0.0))
                sec("vector", _dve0)
                init_done = sDVE.n

                chunk_last_dve = init_done
                chunk_last_store = sSt.n

                for c in range(NCH):
                    col0 = b * S + c * TC

                    def _syncx(e, col0=col0, cld=chunk_last_dve, cls=chunk_last_store):
                        e.wait_ge(sDVE.h, cld)
                        e.wait_ge(sSt.h, cls)
                        sv = resid.rearrange("(c p) t -> p c t", p=128)[:, :, col0:col0 + TC]
                        sL.bump(e.dma_start(out=x_sb[:].rearrange("p (c t) -> p c t", c=8),
                                            in_=sv), 16)
                    sec("sync", _syncx)
                    ld_x = sL.n

                    def _dveA(e, ld_x=ld_x):
                        e.wait_ge(sL.h, ld_x)
                        if DEBUG_TAPS:
                            e.tensor_copy(out=xld_sb, in_=x_sb[:, 0:4 * TC])
                        sDVE.bump(e.tensor_copy(out=xbf_sb, in_=x_sb[:]))
                    sec("vector", _dveA)
                    c1 = sDVE.n

                    def _actA(e, c1=c1):
                        e.wait_ge(sDVE.h, c1)
                        sACT.bump(e.activation(out=sqbf_sb, in_=xbf_sb, func=AF.Square))
                    sec("scalar", _actA)
                    c2 = sACT.n

                    def _peA(e, c2=c2):
                        e.wait_ge(sACT.h, c2)
                        for dc in range(8):
                            e.matmul(ps_sx[0:1, :], ones_col_b[:, 0:1],
                                     xbf_sb[:, dc * TC:(dc + 1) * TC],
                                     start=(dc == 0), stop=(dc == 7))
                        for dc in range(8):
                            e.matmul(ps_sq[0:1, :], ones_col_b[:, 0:1],
                                     sqbf_sb[:, dc * TC:(dc + 1) * TC],
                                     start=(dc == 0), stop=(dc == 7))
                        sPE.bump(e.matmul(ps_b1[:, 0:128], ones_row_f[0:1, :],
                                          ones_row_f[0:1, 0:128], start=True, stop=True))
                    sec("tensor", _peA)
                    c3 = sPE.n

                    mean_r = row_sb[0:1, 0:TC]
                    rs1_r = row_sb[0:1, TC:2 * TC]
                    rA_r = row_sb[0:1, 2 * TC:3 * TC]
                    t0_r = row_sb[0:1, 3 * TC:4 * TC]
                    t1_r = row_sb[0:1, 4 * TC:5 * TC]
                    var_r = row_sb[0:1, 5 * TC:6 * TC]

                    def _dveB(e, c3=c3):
                        e.wait_ge(sPE.h, c3)
                        e.tensor_scalar_mul(out=mean_r, in0=ps_sx[0:1, :], scalar1=1.0 / D)
                        e.tensor_scalar_mul(out=t0_r, in0=ps_sq[0:1, :], scalar1=1.0 / D)
                        e.tensor_mul(out=t1_r, in0=mean_r, in1=mean_r)
                        e.tensor_sub(out=var_r, in0=t0_r, in1=t1_r)
                        sDVE.bump(e.tensor_scalar_add(out=t0_r, in0=var_r, scalar1=EPS))
                    sec("vector", _dveB)
                    c4 = sDVE.n

                    def _actB(e, c4=c4):
                        e.wait_ge(sDVE.h, c4)
                        e.activation(out=t1_r, in_=t0_r, func=AF.Ln)
                        sACT.bump(e.activation(out=rs1_r, in_=t1_r, func=AF.Exp, scale=-0.5))
                    sec("scalar", _actB)
                    c5 = sACT.n

                    def _dveC(e, c5=c5):
                        e.wait_ge(sACT.h, c5)
                        e.tensor_mul(out=t1_r, in0=rs1_r, in1=rs1_r)
                        e.tensor_mul(out=t1_r, in0=var_r, in1=t1_r)
                        sDVE.bump(e.tensor_scalar_add(out=t1_r, in0=t1_r, scalar1=EPS))
                    sec("vector", _dveC)
                    c6 = sDVE.n

                    def _actC(e, c6=c6):
                        e.wait_ge(sDVE.h, c6)
                        e.activation(out=t0_r, in_=t1_r, func=AF.Ln)
                        sACT.bump(e.activation(out=t1_r, in_=t0_r, func=AF.Exp, scale=-0.5))
                    sec("scalar", _actC)
                    c7 = sACT.n

                    def _dveD(e, c7=c7):
                        e.wait_ge(sACT.h, c7)
                        sDVE.bump(e.tensor_mul(out=rA_r, in0=rs1_r, in1=t1_r))
                    sec("vector", _dveD)
                    c8 = sDVE.n

                    def _peB(e, c8=c8):
                        e.wait_ge(sDVE.h, c8)
                        e.matmul(ps_b1[:, 0:TC], ones_row_f[0:1, :], mean_r, start=True, stop=True)
                        e.matmul(ps_b2[:, 0:TC], ones_row_f[0:1, :], rs1_r, start=True, stop=True)
                        sPE.bump(e.matmul(ps_b3[:, 0:TC], ones_row_f[0:1, :], rA_r,
                                          start=True, stop=True))
                    sec("tensor", _peB)
                    c9 = sPE.n

                    def _actD(e, c9=c9):
                        e.wait_ge(sPE.h, c9)
                        e.activation(out=bcst_sb[:, 0:TC], in_=ps_b1[:, 0:TC], func=AF.Copy)
                        e.activation(out=bcst_sb[:, TC:2 * TC], in_=ps_b2[:, 0:TC], func=AF.Copy)
                        sACT.bump(e.activation(out=bcst_sb[:, 2 * TC:3 * TC], in_=ps_b3[:, 0:TC],
                                               func=AF.Copy))
                    sec("scalar", _actD)
                    c10 = sACT.n

                    def _dveE(e, c10=c10):
                        e.wait_ge(sACT.h, c10)
                        mb = bcst_sb[:, 0:TC].unsqueeze(1).broadcast_to([128, 8, TC])
                        rab = bcst_sb[:, 2 * TC:3 * TC].unsqueeze(1).broadcast_to([128, 8, TC])
                        yv = y_sb[:, 0:8 * TC].rearrange("p (c t) -> p c t", c=8)
                        e.tensor_sub(out=yv,
                                     in0=x_sb[:].rearrange("p (c t) -> p c t", c=8), in1=mb)
                        sDVE.bump(e.tensor_mul(out=yv, in0=yv, in1=rab))
                    sec("vector", _dveE)
                    c11 = sDVE.n

                    def _actE(e, c11=c11):
                        e.wait_ge(sDVE.h, c11)
                        for dc in range(8):
                            inst = e.activation(out=xn_sb[:, dc * TC:(dc + 1) * TC],
                                                in_=y_sb[:, dc * TC:(dc + 1) * TC], func=AF.Copy,
                                                scale=sp[:, SP_NW + dc:SP_NW + dc + 1])
                        sACT.bump(inst)
                    sec("scalar", _actE)
                    act_xn = sACT.n

                    ip_pe0, ip_act0 = sPE.n, sACT.n
                    ip_ldb = [sLw[0].n, sLw[1].n]

                    def _syncW(e, ip_pe0=ip_pe0, ip_act0=ip_act0):
                        for m in range(32):
                            if SERIAL_W and m >= 1:
                                e.wait_ge(sACT.h, ip_act0 + m)
                            elif m >= 2:
                                e.wait_ge(sPE.h, ip_pe0 + m - 1)
                            sLw[m % 2].bump(e.dma_start(
                                out=wi_sb[m % 2][:].rearrange("p (c f) -> p c f", c=8),
                                in_=ipb[m, :, :, :]), 16)
                    sec("sync", _syncW)

                    def _peC(e, act_xn=act_xn, ip_ldb=ip_ldb, ip_act0=ip_act0):
                        e.wait_ge(sACT.h, act_xn)
                        for m in range(32):
                            pb = ps_w[m % 2]
                            e.wait_ge(sLw[m % 2].h, ip_ldb[m % 2] + 16 * (m // 2 + 1))
                            if m >= 2:
                                e.wait_ge(sACT.h, ip_act0 + m - 1)
                            for kc in range(8):
                                inst = e.matmul(pb[:, 0:TC],
                                                wi_sb[m % 2][:, kc * 128:(kc + 1) * 128],
                                                xn_sb[:, kc * TC:(kc + 1) * TC],
                                                start=(kc == 0), stop=(kc == 7))
                            sPE.bump(inst)
                    sec("tensor", _peC)
                    pe_after_ip = sPE.n

                    def _actF(e, ip_pe0=ip_pe0):
                        for m in range(32):
                            pb = ps_w[m % 2]
                            e.wait_ge(sPE.h, ip_pe0 + m + 1)
                            if m < 16:
                                dst = xi_sb[:, m * (TC + 3) + 3: m * (TC + 3) + 3 + TC]
                                inst = e.activation(out=dst, in_=pb[:, 0:TC], func=AF.Copy)
                            else:
                                dst = res_sb[:, (m - 16) * TC:(m - 15) * TC]
                                inst = e.activation(out=dst, in_=pb[:, 0:TC], func=AF.Silu)
                            sACT.bump(inst)
                    sec("scalar", _actF)
                    act_ip = sACT.n

                    def _dveF(e, act_ip=act_ip):
                        e.wait_ge(sACT.h, act_ip)
                        xiv = xi_sb[:].rearrange("p (c t) -> p c t", c=16)
                        e.tensor_copy(out=xiv[:, :, 0:3],
                                      in_=halo_sb[:].rearrange("p (c h) -> p c h", c=16))
                        for dc in range(16):
                            off = dc * (TC + 3)
                            for j in range(4):
                                src = xi_sb[:, off + j: off + j + TC]
                                scl = sp[:, SP_CW + j * 16 + dc: SP_CW + j * 16 + dc + 1]
                                if j == 0:
                                    e.tensor_scalar_mul(out=xi2_sb[:, dc * TC:(dc + 1) * TC],
                                                        in0=src, scalar1=scl)
                                else:
                                    e.scalar_tensor_tensor(
                                        out=xi2_sb[:, dc * TC:(dc + 1) * TC], in0=src, scalar=scl,
                                        in1=xi2_sb[:, dc * TC:(dc + 1) * TC],
                                        op0=ALU.mult, op1=ALU.add)
                        e.tensor_copy(out=halo_sb[:].rearrange("p (c h) -> p c h", c=16),
                                      in_=xiv[:, :, TC:TC + 3])
                        e.memset(da_sb[0].rearrange("p (n s) -> p n s", n=NSTG)[:, :, 0:1], 0.0)
                        sDVE.bump(e.memset(
                            da_sb[1].rearrange("p (n s) -> p n s", n=NSTG)[:, :, 0:1], 0.0))
                    sec("vector", _dveF)
                    c12 = sDVE.n

                    def _actG(e, c12=c12):
                        e.wait_ge(sDVE.h, c12)
                        for dc in range(16):
                            inst = e.activation(out=xi2_sb[:, dc * TC:(dc + 1) * TC],
                                                in_=xi2_sb[:, dc * TC:(dc + 1) * TC], func=AF.Silu,
                                                bias=sp[:, SP_CB + dc:SP_CB + dc + 1])
                        sACT.bump(inst)
                    sec("scalar", _actG)
                    c13 = sACT.n

                    def _dveG(e, c13=c13):
                        e.wait_ge(sACT.h, c13)
                        sDVE.bump(e.tensor_copy(out=xi2b_sb[:], in_=xi2_sb[:]))
                    sec("vector", _dveG)
                    c14 = sDVE.n

                    def _peD(e, c14=c14):
                        e.wait_ge(sDVE.h, c14)
                        for kc in range(16):
                            inst = e.matmul(ps_x[:, :], xp_sb[:, kc * 96:(kc + 1) * 96],
                                            xi2b_sb[:, kc * TC:(kc + 1) * TC],
                                            start=(kc == 0), stop=(kc == 15))
                        sPE.bump(inst)
                    sec("tensor", _peD)
                    c15 = sPE.n

                    def _actH(e, c15=c15):
                        e.wait_ge(sPE.h, c15)
                        e.activation(out=drb_sb[:, :], in_=ps_x[0:64, :], func=AF.Copy)
                        sACT.bump(e.activation(out=bc_sb[:, :], in_=ps_x[64:96, :], func=AF.Copy))
                    sec("scalar", _actH)
                    act_xdbl = sACT.n

                    dt_pe0, dt_act0, dt_dve0 = sPE.n, sACT.n, sDVE.n

                    def _peE(e, act_xdbl=act_xdbl, dt_dve0=dt_dve0):
                        e.wait_ge(sACT.h, act_xdbl)
                        for mc in range(16):
                            pb = ps_w[mc % 2]
                            if mc >= 2:
                                e.wait_ge(sDVE.h, dt_dve0 + 2 * mc - 3)
                            inst = e.matmul(pb[:, 0:TC], dp_sb[0:64, mc * 128:(mc + 1) * 128],
                                            drb_sb[0:64, :], start=True, stop=True)
                            sPE.bump(inst)
                        sPE.bump(e.matmul(ps_x[:, 0:64], ones_row_f[0:1, 0:96],
                                          ones_row_f[0:1, 0:64], start=True, stop=True))
                    sec("tensor", _peE)

                    def _dveH1(e, dt_pe0=dt_pe0, dt_act0=dt_act0):
                        for mc in range(16):
                            pb = ps_w[mc % 2]
                            e.wait_ge(sPE.h, dt_pe0 + mc + 2)
                            z = dlt_sb[:, mc * TC:(mc + 1) * TC]
                            sDVE.bump(e.tensor_scalar_add(
                                out=z, in0=pb[:, 0:TC],
                                scalar1=sp[:, SP_DTB + mc:SP_DTB + mc + 1]))
                            e.wait_ge(sACT.h, dt_act0 + mc + 1)
                            inst = e.scalar_tensor_tensor(out=z, in0=z, scalar=0.0,
                                                          in1=sptmp[mc % 2][:, :],
                                                          op0=ALU.max, op1=ALU.add)
                            sDVE.bump(inst)
                    sec("vector", _dveH1)

                    def _actI(e, dt_dve0=dt_dve0):
                        for mc in range(16):
                            e.wait_ge(sDVE.h, dt_dve0 + 2 * mc + 1)
                            if mc >= 2:
                                e.wait_ge(sDVE.h, dt_dve0 + 2 * mc - 2)
                            z = dlt_sb[:, mc * TC:(mc + 1) * TC]
                            t = sptmp[mc % 2][:, :]
                            e.activation(out=t, in_=z, func=AF.Abs)
                            e.activation(out=t, in_=t, func=AF.Exp, scale=-1.0)
                            inst = e.activation(out=t, in_=t, func=AF.Ln, bias=1.0)
                            sACT.bump(inst)
                    sec("scalar", _actI)
                    dve_dt = sDVE.n

                    def _dveI(e, dve_dt=dve_dt):
                        e.wait_ge(sDVE.h, dve_dt)
                        for dc in range(16):
                            e.tensor_scalar_mul(out=y_sb[:, dc * TC:(dc + 1) * TC],
                                                in0=xi2_sb[:, dc * TC:(dc + 1) * TC],
                                                scalar1=sp[:, SP_D + dc:SP_D + dc + 1])
                        sDVE.bump(e.tensor_mul(out=xi2_sb[:], in0=dlt_sb[:], in1=xi2_sb[:]))
                    sec("vector", _dveI)
                    dve_u0 = sDVE.n
                    u0_sb = xi2_sb

                    A0, P0, D0, PE0 = sACT.n, sPOOL.n, sDVE.n, sPE.n
                    bank_b = [ps_b1, ps_b2]
                    bank_c = [ps_b3, ps_w[0]]

                    a_evac_c = {}; a_evac_done_c = {}; a_cover_c = {}; a_ext_c = {}
                    p_ct_c = {}; p_dbu_c = {}; d_tts_c = {}; d_acc_c = {}; pe_mm_c = {}
                    _a, _p, _d, _pe = A0, P0, D0, PE0
                    for g2 in range(2):
                        for mi in range(17):
                            _pe += 1; pe_mm_c[(g2, mi)] = _pe
                        for ei in range(8):
                            _a += 1; a_evac_c[(g2, ei)] = _a
                        _a += 1; a_evac_done_c[g2] = _a
                        for dc in range(16):
                            it = g2 * 16 + dc
                            _a += 1; a_cover_c[it] = _a
                            if it >= 1:
                                _a += 1; a_ext_c[it - 1] = _a
                    _a += 1; a_ext_c[NIT - 1] = _a
                    pool_sched = []
                    _nx = 0
                    _ect = -1
                    for it in range(NIT):
                        if _nx == 16 and it >= 16 and _ect < 15:
                            pool_sched.append(("ct", 15)); _ect = 15
                        while _nx < NIT and _nx <= it + 1 and (_nx < 16 or it >= 16):
                            pool_sched.append(("dbu", _nx))
                            _nx += 1
                        if it >= 1 and _ect < it - 1:
                            pool_sched.append(("ct", it - 1)); _ect = it - 1
                    pool_sched.append(("ct", NIT - 1))
                    for kind, j in pool_sched:
                        _p += 1
                        (p_ct_c if kind == "ct" else p_dbu_c)[j] = _p
                    for it in range(NIT):
                        _d += 1; d_tts_c[it] = _d
                        if it >= 2:
                            _d += 1; d_acc_c[it - 2] = _d
                    _d += 1; d_acc_c[NIT - 2] = _d
                    _d += 1; d_acc_c[NIT - 1] = _d

                    if not MAMBA_NOSCAN:
                        def _peBC(e, act_xdbl=act_xdbl, dve_dt=dve_dt):
                            for g2 in range(2):
                                if g2 == 0:
                                    e.wait_ge(sACT.h, act_xdbl)
                                    e.wait_ge(sPOOL.h, sel_done)
                                    e.wait_ge(sDVE.h, dve_dt)
                                else:
                                    e.wait_ge(sACT.h, a_evac_done_c[0])
                                    e.wait_ge(sPOOL.h, p_ct_c[15])
                                for p in range(4):
                                    if p >= 2:
                                        e.wait_ge(sACT.h, a_evac_c[(g2, 2 * (p - 2) + 1)])
                                    for h in range(2):
                                        n = g2 * NSTG + 2 * p + h
                                        sPE.bump(e.matmul(
                                            bank_b[p % 2][:, h * TC:(h + 1) * TC],
                                            sel[:, n * 128:(n + 1) * 128],
                                            bc_sb[0:32, :], start=True, stop=True))
                                    for h in range(2):
                                        n = g2 * NSTG + 2 * p + h
                                        sPE.bump(e.matmul(
                                            bank_c[p % 2][:, h * TC:(h + 1) * TC],
                                            sel[:, (16 + n) * 128:(17 + n) * 128],
                                            bc_sb[0:32, :], start=True, stop=True))
                                n = g2 * NSTG + 7
                                sPE.bump(e.matmul(bank_c[1][:, TC:2 * TC],
                                                  sel[:, (16 + n) * 128:(17 + n) * 128],
                                                  bc_sb[0:32, :], start=True, stop=True))
                        sec("tensor", _peBC)

                        def _actSC(e, dve_dt=dve_dt):
                            e.wait_ge(sDVE.h, dve_dt)
                            for g2 in range(2):
                                for p in range(4):
                                    e.wait_ge(sPE.h, pe_mm_c[(g2, 4 * p + 2)])
                                    sACT.bump(e.activation(
                                        out=bb_sb[:, 2 * p * TC:(2 * p + 2) * TC],
                                        in_=bank_b[p % 2][:, 0:2 * TC], func=AF.Copy))
                                    e.wait_ge(sPE.h, pe_mm_c[(g2, 16) if p == 3
                                                             else (g2, 4 * p + 4)])
                                    sACT.bump(e.activation(
                                        out=cb_sb[:, 2 * p * TC:(2 * p + 2) * TC],
                                        in_=bank_c[p % 2][:, 0:2 * TC], func=AF.Copy))
                                sACT.bump(e.activation(out=sptmp[0][:, 4:8],
                                                       in_=sptmp[0][:, 4:8], func=AF.Copy))
                                for dc in range(16):
                                    it = g2 * 16 + dc
                                    par = it % 2
                                    if it >= 2:
                                        e.wait_ge(sDVE.h, d_tts_c[it - 2])
                                    for n8 in range(NSTG):
                                        n = g2 * NSTG + n8
                                        e.activation(
                                            out=da_sb[par][:, n8 * SEG + 1:(n8 + 1) * SEG],
                                            in_=dlt_sb[:, dc * TC:(dc + 1) * TC], func=AF.Exp,
                                            scale=a_sb[:, dc * 16 + n: dc * 16 + n + 1])
                                    duv = du_sb[par][:].rearrange("p (n s) -> p n s", n=NSTG)
                                    e.activation(
                                        out=duv[:, :, 0:1].squeeze(2),
                                        in_=st_sb[:, dc * 16 + g2 * NSTG:
                                                  dc * 16 + g2 * NSTG + NSTG], func=AF.Copy)
                                    sACT.bump(e.activation(out=sptmp[0][:, 0:4],
                                                           in_=sptmp[0][:, 0:4], func=AF.Copy))
                                    if it >= 1:
                                        j = it - 1
                                        e.wait_ge(sDVE.h, d_tts_c[j])
                                        sv = s_sb[j % 2][:].rearrange("p (n s) -> p n s",
                                                                      n=NSTG)
                                        sACT.bump(e.activation(
                                            out=st_sb[:, (j % 16) * 16 + (j // 16) * NSTG:
                                                      (j % 16) * 16 + (j // 16) * NSTG + NSTG],
                                            in_=sv[:, :, SEG - 1:SEG].squeeze(2), func=AF.Copy))
                            j = NIT - 1
                            e.wait_ge(sDVE.h, d_tts_c[j])
                            sv = s_sb[j % 2][:].rearrange("p (n s) -> p n s", n=NSTG)
                            sACT.bump(e.activation(
                                out=st_sb[:, 15 * 16 + NSTG: 15 * 16 + 2 * NSTG],
                                in_=sv[:, :, SEG - 1:SEG].squeeze(2), func=AF.Copy))
                        sec("scalar", _actSC)
                        assert sACT.n == a_ext_c[NIT - 1], (sACT.n, a_ext_c[NIT - 1])

                        def _poolSC(e, dve_u0=dve_u0):
                            e.wait_ge(sDVE.h, dve_u0)
                            for kind, j in pool_sched:
                                if kind == "ct":
                                    e.wait_ge(sDVE.h, d_tts_c[j])
                                    if j >= 2:
                                        e.wait_ge(sDVE.h, d_acc_c[j - 2])
                                    sjv = s_sb[j % 2][:].rearrange("p (n s) -> p n s",
                                                                   n=NSTG)
                                    e.tensor_mul(
                                        out=ct_sb[j % 2][:].rearrange("p (n t) -> p n t",
                                                                      n=NSTG),
                                        in0=sjv[:, :, 1:SEG],
                                        in1=cb_sb[:].rearrange("p (n t) -> p n t", n=NSTG))
                                    sPOOL.bump(e.tensor_copy(out=sptmp[1][:, 0:4],
                                                             in_=sptmp[1][:, 0:4]))
                                else:
                                    g2, dc = j // 16, j % 16
                                    par = j % 2
                                    if dc == 0:
                                        e.wait_ge(sACT.h, a_evac_done_c[g2])
                                    if j >= 2:
                                        e.wait_ge(sDVE.h, d_tts_c[j - 2])
                                    u0v = u0_sb[:, dc * TC:(dc + 1) * TC].unsqueeze(1) \
                                        .broadcast_to([128, NSTG, TC])
                                    duv = du_sb[par][:].rearrange("p (n s) -> p n s",
                                                                  n=NSTG)[:, :, 1:SEG]
                                    e.tensor_mul(out=duv, in0=u0v,
                                                 in1=bb_sb[:].rearrange("p (n t) -> p n t",
                                                                        n=NSTG))
                                    sPOOL.bump(e.tensor_copy(out=sptmp[1][:, 0:4],
                                                             in_=sptmp[1][:, 0:4]))
                        sec("gpsimd", _poolSC)
                        assert sPOOL.n == p_ct_c[NIT - 1], (sPOOL.n, p_ct_c[NIT - 1])

                        def _dve_red(e, j):
                            e.wait_ge(sPOOL.h, p_ct_c[j])
                            e.tensor_reduce(
                                out=yt_sb[:, :],
                                in_=ct_sb[j % 2][:].rearrange("p (n t) -> p t n",
                                                              n=NSTG),
                                axis=AX.X, op=ALU.add)
                            jdc = j % 16
                            sDVE.bump(e.tensor_add(
                                out=y_sb[:, jdc * TC:(jdc + 1) * TC],
                                in0=y_sb[:, jdc * TC:(jdc + 1) * TC],
                                in1=yt_sb[:, :]))

                        def _dveSC(e):
                            for it in range(NIT):
                                par = it % 2
                                e.wait_ge(sACT.h, a_cover_c[it])
                                e.wait_ge(sPOOL.h, p_dbu_c[it])
                                if it >= 2:
                                    e.wait_ge(sPOOL.h, p_ct_c[it - 2])
                                    e.wait_ge(sACT.h, a_ext_c[it - 2])
                                e.tensor_tensor_scan(out=s_sb[par][:], data0=da_sb[par],
                                                     data1=du_sb[par][:], initial=0.0,
                                                     op0=ALU.mult, op1=ALU.add)
                                sDVE.bump(e.tensor_copy(out=yt_sb[:, 0:4],
                                                        in_=yt_sb[:, 0:4]))
                                if it >= 2:
                                    _dve_red(e, it - 2)
                            _dve_red(e, NIT - 2)
                            _dve_red(e, NIT - 1)
                        sec("vector", _dveSC)
                        assert sPE.n == pe_mm_c[(1, 16)], (sPE.n, pe_mm_c[(1, 16)])

                    scan_end = d_acc_c[NIT - 1] if not MAMBA_NOSCAN else dve_u0
                    assert scan_end == sDVE.n, (scan_end, sDVE.n)

                    def _dveJ(e, scan_end=scan_end):
                        e.wait_ge(sDVE.h, scan_end)
                        e.tensor_mul(out=y_sb[:], in0=y_sb[:], in1=res_sb[:])
                        sDVE.bump(e.tensor_copy(out=xi2b_sb[:], in_=y_sb[:]))
                    sec("vector", _dveJ)
                    dve_ybf = sDVE.n
                    ybf_sb = xi2b_sb

                    op_pe0, op_dve0 = sPE.n, sDVE.n
                    op_ldb = [sLw[0].n, sLw[1].n]

                    def _syncO(e, op_pe0=op_pe0, pe_after_ip=pe_after_ip):
                        for u in range(16):
                            m, half = u // 2, u % 2
                            if u < 2:
                                e.wait_ge(sPE.h, pe_after_ip)
                            else:
                                e.wait_ge(sPE.h, op_pe0 + u - 1)
                            sLw[u % 2].bump(e.dma_start(
                                out=wi_sb[u % 2][:].rearrange("p (c f) -> p c f", c=8),
                                in_=opb[m, :, half * 8:(half + 1) * 8, :]), 16)
                    sec("sync", _syncO)

                    def _peG(e, dve_ybf=dve_ybf, op_ldb=op_ldb, op_dve0=op_dve0):
                        e.wait_ge(sDVE.h, dve_ybf)
                        for u in range(16):
                            m, half = u // 2, u % 2
                            pb = ps_w[m % 2]
                            e.wait_ge(sLw[u % 2].h, op_ldb[u % 2] + 16 * (u // 2 + 1))
                            if half == 0 and m >= 2:
                                e.wait_ge(sDVE.h, op_dve0 + 4 * (m - 1))
                            for kc in range(8):
                                inst = e.matmul(pb[:, 0:TC],
                                                wi_sb[u % 2][:, kc * 128:(kc + 1) * 128],
                                                ybf_sb[:, (half * 8 + kc) * TC:
                                                       (half * 8 + kc + 1) * TC],
                                                start=(half == 0 and kc == 0),
                                                stop=(half == 1 and kc == 7))
                            sPE.bump(inst)
                        sPE.bump(e.matmul(ps_x[:, 0:64], ones_row_f[0:1, 0:96],
                                          ones_row_f[0:1, 0:64], start=True, stop=True))
                    sec("tensor", _peG)

                    def _dveK(e, op_pe0=op_pe0):
                        for m in range(8):
                            pb = ps_w[m % 2]
                            e.wait_ge(sPE.h, op_pe0 + min(2 * (m + 1) + 1, 17))
                            mb = bcst_sb[:, 0:TC]
                            rb = bcst_sb[:, TC:2 * TC]
                            yv = y_sb[:, m * TC:(m + 1) * TC]
                            e.tensor_sub(out=yv, in0=x_sb[:, m * TC:(m + 1) * TC], in1=mb)
                            e.tensor_mul(out=yv, in0=yv, in1=rb)
                            e.tensor_add(out=yv, in0=yv, in1=pb[:, 0:TC])
                            inst = e.tensor_add(out=x_sb[:, m * TC:(m + 1) * TC],
                                                in0=x_sb[:, m * TC:(m + 1) * TC],
                                                in1=yv)
                            sDVE.bump(inst, 4)
                    sec("vector", _dveK)
                    chunk_last_dve = sDVE.n

                    def _gpO(e, col0=col0, cld=chunk_last_dve, b=b, c=c):
                        e.wait_ge(sDVE.h, cld)
                        dv = resid.rearrange("(c p) t -> p c t", p=128)[:, :, col0:col0 + TC]
                        sSt.bump(e.dma_start(out=dv,
                                             in_=x_sb[:].rearrange("p (c t) -> p c t", c=8)), 16)
                        if DEBUG_TAPS and b == 0:
                            for nm, buf in (("xn", xn_sb), ("xi", xi_sb), ("res", res_sb),
                                            ("dlt", dlt_sb), ("u0", xi2_sb), ("y", y_sb),
                                            ("newx", x_sb), ("sub", y_sb),
                                            ("bcst", bcst_sb), ("row", row_sb),
                                            ("xld", None)):
                                if nm == "xld":
                                    sSt.bump(e.dma_start(out=taps[nm][c], in_=xld_sb), 16)
                                    continue
                                sSt.bump(e.dma_start(out=taps[nm][c], in_=buf[:]), 16)
                    sec("gpsimd", _gpO)
                    chunk_last_store = sSt.n

        def ffn():
            with ExitStack() as fs:
                xf = fs.enter_context(sbuf("f_xf", [128, 8 * FT], F32))
                xn = fs.enter_context(sbuf("f_xn", [128, 8 * FT], BF16))
                h_sb = fs.enter_context(sbuf("f_h", [128, 32 * FT], BF16))
                g_sb = fs.enter_context(sbuf("f_g", [128, 32 * FT], BF16))
                tf = [fs.enter_context(sbuf(f"f_tf{i}", [128, 2 * FT], F32))
                      for i in range(2)]
                rows = fs.enter_context(sbuf("f_rows", [1, 4 * FT], F32))
                bcast = fs.enter_context(sbuf("f_bcast", [128, 2 * FT], F32))
                wh = [fs.enter_context(sbuf(f"f_wh{i}", [128, 32 * 128], BF16))
                      for i in range(2)]
                psA = [fs.enter_context(psum(f"f_psA{i}", [128, FT], F32))
                       for i in range(2)]
                psB = [fs.enter_context(psum(f"f_psB{i}", [128, FT], F32))
                       for i in range(2)]
                p_sx = fs.enter_context(psum("f_pssx", [1, FT], F32))
                p_sq = fs.enter_context(psum("f_pssq", [1, FT], F32))
                p_b1 = fs.enter_context(psum("f_psb1", [128, FT], F32))
                p_b2 = fs.enter_context(psum("f_psb2", [128, FT], F32))

                prev_dve = sDVE.n
                prev_st = sSt.n

                for ti in range(NFT):
                    col0 = ti * FT
                    xbf = g_sb[:, 0:8 * FT]
                    sqbf = h_sb[:, 0:8 * FT]

                    def _syncx(e, col0=col0, pd=prev_dve, pst=prev_st):
                        e.wait_ge(sDVE.h, pd)
                        e.wait_ge(sSt.h, pst)
                        sv = resid.rearrange("(c p) t -> p c t", p=128)[:, :, col0:col0 + FT]
                        sL.bump(e.dma_start(out=xf[:].rearrange("p (c t) -> p c t", c=8),
                                            in_=sv), 16)
                    sec("sync", _syncx)

                    apply_done = emit_ln(FT, xf[:], xbf, sqbf, xn[:], rows[:], bcast[:],
                                         p_sx, p_sq, p_b1, p_b2, sL.n)

                    s_pe0, s_act0, s_dve0 = sPE.n, sACT.n, sDVE.n
                    s_ldb = [sLw[0].n, sLw[1].n]

                    def _syncW(e, s_pe0=s_pe0):
                        for m in range(32):
                            par = m % 2
                            if m >= 2:
                                e.wait_ge(sPE.h, s_pe0 + 2 * (m - 1))
                            e.dma_start(
                                out=wh[par][:, 0:1024].rearrange("p (c f) -> p c f", c=8),
                                in_=w1b[m, :, :, :]).then_inc(sLw[par].h, 16)
                            sLw[par].n += 16
                            sLw[par].bump(e.dma_start(
                                out=wh[par][:, 1024:2048].rearrange("p (c f) -> p c f", c=8),
                                in_=w2b[m, :, :, :]), 16)
                    sec("sync", _syncW)

                    def _peW(e, apply_done=apply_done, s_ldb=s_ldb, s_act0=s_act0):
                        e.wait_ge(sDVE.h, apply_done)
                        for m in range(32):
                            par = m % 2
                            e.wait_ge(sLw[par].h, s_ldb[par] + 32 * (m // 2 + 1))
                            if m >= 2:
                                e.wait_ge(sACT.h, s_act0 + 2 * (m - 1))
                            for kc in range(8):
                                e.matmul(psA[par][:, :], wh[par][:, kc * 128:(kc + 1) * 128],
                                         xn[:, kc * FT:(kc + 1) * FT],
                                         start=(kc == 0), stop=(kc == 7))
                            for kc in range(8):
                                inst = e.matmul(psB[par][:, :],
                                                wh[par][:, 1024 + kc * 128:1024 + (kc + 1) * 128],
                                                xn[:, kc * FT:(kc + 1) * FT],
                                                start=(kc == 0), stop=(kc == 7))
                            sPE.bump(inst, 2)
                    sec("tensor", _peW)

                    def _actW(e, s_pe0=s_pe0, s_dve0=s_dve0):
                        for m in range(32):
                            par = m % 2
                            e.wait_ge(sPE.h, s_pe0 + 2 * (m + 1))
                            if m >= 2:
                                e.wait_ge(sDVE.h, s_dve0 + m - 1)
                            mc = m
                            e.activation(out=tf[par][:, 0:FT], in_=psA[par][:, :], func=AF.Silu,
                                         scale=sp[:, SP_S1 + mc:SP_S1 + mc + 1])
                            inst = e.activation(out=tf[par][:, FT:2 * FT], in_=psB[par][:, :],
                                                func=AF.Copy, scale=sp[:, SP_S2 + mc:SP_S2 + mc + 1])
                            sACT.bump(inst, 2)
                    sec("scalar", _actW)

                    def _dveW(e, s_act0=s_act0):
                        for m in range(32):
                            par = m % 2
                            e.wait_ge(sACT.h, s_act0 + 2 * (m + 1))
                            sDVE.bump(e.tensor_mul(out=h_sb[:, m * FT:(m + 1) * FT],
                                                   in0=tf[par][:, 0:FT], in1=tf[par][:, FT:2 * FT]))
                    sec("vector", _dveW)
                    h_done = sDVE.n

                    def hstage(wsrc, rhs_sb, out_sb, scol, ready_dve, ready_act):
                        t_pe0, t_act0 = sPE.n, sACT.n
                        t_ldb = [sLw[0].n, sLw[1].n]

                        def _syncH(e, t_pe0=t_pe0):
                            for m in range(32):
                                par = m % 2
                                if m < 2:
                                    e.wait_ge(sPE.h, t_pe0)
                                else:
                                    e.wait_ge(sPE.h, t_pe0 + m - 1)
                                sLw[par].bump(e.dma_start(
                                    out=wh[par][:].rearrange("p (c f) -> p c f", c=32),
                                    in_=wsrc[m, :, :, :]), 16)
                        sec("sync", _syncH)

                        def _peH(e, t_ldb=t_ldb, t_act0=t_act0, ready_dve=ready_dve,
                                 ready_act=ready_act):
                            if ready_dve is not None:
                                e.wait_ge(sDVE.h, ready_dve)
                            if ready_act is not None:
                                e.wait_ge(sACT.h, ready_act)
                            for m in range(32):
                                par = m % 2
                                e.wait_ge(sLw[par].h, t_ldb[par] + 16 * (m // 2 + 1))
                                if m >= 2:
                                    e.wait_ge(sACT.h, t_act0 + m - 1)
                                for kc in range(32):
                                    inst = e.matmul(psA[par][:, :],
                                                    wh[par][:, kc * 128:(kc + 1) * 128],
                                                    rhs_sb[:, kc * FT:(kc + 1) * FT],
                                                    start=(kc == 0), stop=(kc == 31))
                                sPE.bump(inst)
                        sec("tensor", _peH)

                        def _actHs(e, t_pe0=t_pe0):
                            for m in range(32):
                                par = m % 2
                                e.wait_ge(sPE.h, t_pe0 + m + 1)
                                inst = e.activation(out=out_sb[:, m * FT:(m + 1) * FT],
                                                    in_=psA[par][:, :], func=AF.Gelu,
                                                    scale=sp[:, scol + m:scol + m + 1])
                                sACT.bump(inst)
                        sec("scalar", _actHs)
                        return sACT.n

                    g_done = hstage(h1b, h_sb, g_sb, SP_H1S, h_done, None)
                    hh_done = hstage(h2b, g_sb, h_sb, SP_H2S, None, g_done)

                    def _syncx2(e, col0=col0, apply_done=apply_done):
                        e.wait_ge(sDVE.h, apply_done)
                        sv = resid.rearrange("(c p) t -> p c t", p=128)[:, :, col0:col0 + FT]
                        sL.bump(e.dma_start(out=xf[:].rearrange("p (c t) -> p c t", c=8),
                                            in_=sv), 16)
                    sec("sync", _syncx2)
                    xre_ld = sL.n

                    w_pe0, w_act0, w_dve0 = sPE.n, sACT.n, sDVE.n
                    w_ldb = [sLw[0].n, sLw[1].n]

                    def _syncW3(e, w_pe0=w_pe0):
                        for m in range(8):
                            par = m % 2
                            if m < 2:
                                e.wait_ge(sPE.h, w_pe0)
                            else:
                                e.wait_ge(sPE.h, w_pe0 + m - 1)
                            sLw[par].bump(e.dma_start(
                                out=wh[par][:].rearrange("p (c f) -> p c f", c=32),
                                in_=w3b[m, :, :, :]), 16)
                    sec("sync", _syncW3)

                    def _peW3(e, hh_done=hh_done, w_ldb=w_ldb, w_act0=w_act0):
                        e.wait_ge(sACT.h, hh_done)
                        for m in range(8):
                            par = m % 2
                            e.wait_ge(sLw[par].h, w_ldb[par] + 16 * (m // 2 + 1))
                            if m >= 2:
                                e.wait_ge(sACT.h, w_act0 + m - 1)
                            for kc in range(32):
                                inst = e.matmul(psA[par][:, :],
                                                wh[par][:, kc * 128:(kc + 1) * 128],
                                                h_sb[:, kc * FT:(kc + 1) * FT],
                                                start=(kc == 0), stop=(kc == 31))
                            sPE.bump(inst)
                    sec("tensor", _peW3)

                    def _actW3(e, w_pe0=w_pe0, w_dve0=w_dve0):
                        for m in range(8):
                            par = m % 2
                            e.wait_ge(sPE.h, w_pe0 + m + 1)
                            if m >= 2:
                                e.wait_ge(sDVE.h, w_dve0 + m - 1)
                            inst = e.activation(out=tf[par][:, 0:FT], in_=psA[par][:, :],
                                                func=AF.Copy,
                                                scale=sp[:, SP_W3S + m:SP_W3S + m + 1])
                            sACT.bump(inst)
                    sec("scalar", _actW3)

                    def _dveW3(e, w_act0=w_act0, xre_ld=xre_ld):
                        e.wait_ge(sL.h, xre_ld)
                        for m in range(8):
                            par = m % 2
                            e.wait_ge(sACT.h, w_act0 + m + 1)
                            sDVE.bump(e.tensor_add(out=xf[:, m * FT:(m + 1) * FT],
                                                   in0=xf[:, m * FT:(m + 1) * FT],
                                                   in1=tf[par][:, 0:FT]))
                    sec("vector", _dveW3)
                    prev_dve = sDVE.n

                    def _gpS(e, col0=col0, pd=prev_dve):
                        e.wait_ge(sDVE.h, pd)
                        dv = resid.rearrange("(c p) t -> p c t", p=128)[:, :, col0:col0 + FT]
                        sSt.bump(e.dma_start(out=dv,
                                             in_=xf[:].rearrange("p (c t) -> p c t", c=8)), 16)
                    sec("gpsimd", _gpS)
                    prev_st = sSt.n

        def attn(b):
            with ExitStack() as ats:
                ln3 = ats.enter_context(sbuf("a_ln3", [128, 8 * S], BF16))
                qb = ats.enter_context(sbuf("a_q", [128, 8 * S], BF16))
                kb = ats.enter_context(sbuf("a_k", [128, 8 * S], BF16))
                vb = ats.enter_context(sbuf("a_v", [128, 16 * 1024], BF16))
                s_f = ats.enter_context(sbuf("a_s", [128, S], F32))
                p_bf = ats.enter_context(sbuf("a_p", [128, S], BF16))
                pt_bf = ats.enter_context(sbuf("a_pt", [128, 16 * 128], BF16))
                o1 = ats.enter_context(sbuf("a_o1", [128, 8 * FT], BF16))
                rf = ats.enter_context(sbuf("a_rf", [128, 8 * FT], F32))
                nm = ats.enter_context(sbuf("a_nm", [128, 2], F32))
                ws = [ats.enter_context(sbuf(f"a_ws{i}", [128, 8 * 128], BF16))
                      for i in range(2)]
                wv_sb = ats.enter_context(sbuf("a_wv", [128, 8 * 1024], BF16))

                with ExitStack() as lns:
                    rows = lns.enter_context(sbuf("a_rows", [1, 4 * FT], F32))
                    bcast = lns.enter_context(sbuf("a_bcast", [128, 2 * FT], F32))
                    p_sx = lns.enter_context(psum("a_pssx", [1, FT], F32))
                    p_sq = lns.enter_context(psum("a_pssq", [1, FT], F32))
                    p_b1 = lns.enter_context(psum("a_psb1", [128, FT], F32))
                    p_b2 = lns.enter_context(psum("a_psb2", [128, FT], F32))
                    prev_apply = None
                    for sl in range(4):
                        col0 = b * S + sl * FT

                        def _syncx(e, col0=col0, pa=prev_apply):
                            if pa is not None:
                                e.wait_ge(sDVE.h, pa)
                            sv = resid.rearrange("(c p) t -> p c t", p=128)[:, :, col0:col0 + FT]
                            sL.bump(e.dma_start(out=rf[:].rearrange("p (c t) -> p c t", c=8),
                                                in_=sv), 16)
                        sec("sync", _syncx)
                        prev_apply = emit_ln(FT, rf[:], qb[:, 0:8 * FT], kb[:, 0:8 * FT],
                                             ln3[:], rows[:], bcast[:], p_sx, p_sq, p_b1, p_b2,
                                             sL.n, out_off=sl * FT, out_stride=S)
                    ln_done = prev_apply

                with ExitStack() as qs:
                    ps_s = [qs.enter_context(psum(f"a_pss{i}", [128, FT], F32))
                            for i in range(2)]
                    ps_t = [qs.enter_context(psum(f"a_pst{i}", [128, 128], BF16))
                            for i in range(2)]
                    ps_av = [qs.enter_context(psum(f"a_psav{i}", [128, 128], F32))
                             for i in range(2)]
                    ps_o = [qs.enter_context(psum(f"a_pso{i}", [128, FT], F32))
                            for i in range(2)]

                    def qk_stage(wsrc, out_sb, scale):
                        t_pe0, t_act0 = sPE.n, sACT.n
                        t_ldb = [sLw[0].n, sLw[1].n]

                        def _syncq(e, t_pe0=t_pe0):
                            for dkc in range(8):
                                par = dkc % 2
                                if dkc < 2:
                                    e.wait_ge(sPE.h, t_pe0)
                                else:
                                    e.wait_ge(sPE.h, t_pe0 + 4 * (dkc - 1))
                                sLw[par].bump(e.dma_start(
                                    out=ws[par][:].rearrange("p (c f) -> p c f", c=8),
                                    in_=wsrc[dkc, :, :, :]), 16)
                        sec("sync", _syncq)

                        def _peq(e, t_ldb=t_ldb, t_act0=t_act0):
                            e.wait_ge(sDVE.h, ln_done)
                            for dkc in range(8):
                                par = dkc % 2
                                e.wait_ge(sLw[par].h, t_ldb[par] + 16 * (dkc // 2 + 1))
                                for ts in range(4):
                                    u = dkc * 4 + ts
                                    if u >= 2:
                                        e.wait_ge(sACT.h, t_act0 + u - 1)
                                    for kc in range(8):
                                        inst = e.matmul(ps_s[u % 2][:, :],
                                                        ws[par][:, kc * 128:(kc + 1) * 128],
                                                        ln3[:, kc * S + ts * FT: kc * S + (ts + 1) * FT],
                                                        start=(kc == 0), stop=(kc == 7))
                                    sPE.bump(inst)
                        sec("tensor", _peq)

                        def _actq(e, t_pe0=t_pe0):
                            for dkc in range(8):
                                for ts in range(4):
                                    u = dkc * 4 + ts
                                    e.wait_ge(sPE.h, t_pe0 + u + 1)
                                    inst = e.activation(
                                        out=out_sb[:, dkc * S + ts * FT: dkc * S + (ts + 1) * FT],
                                        in_=ps_s[u % 2][:, :], func=AF.Copy, scale=scale)
                                    sACT.bump(inst)
                        sec("scalar", _actq)

                    qk_stage(wqb, qb, 0.03125)
                    qk_stage(wkb, kb, 1.0)

                    v_pe0, v_act0 = sPE.n, sACT.n

                    def _syncv(e):
                        for kc in range(8):
                            sLw[kc % 2].bump(e.dma_start(out=wv_sb[:, kc * 1024:(kc + 1) * 1024],
                                                         in_=wvb[kc * 128:(kc + 1) * 128, :]), 16)
                    sec("sync", _syncv)

                    v_ldt = [sLw[0].n, sLw[1].n]

                    def _pev(e, v_ldt=v_ldt, v_act0=v_act0):
                        e.wait_ge(sLw[0].h, v_ldt[0])
                        e.wait_ge(sLw[1].h, v_ldt[1])
                        for tt in range(16):
                            for half in range(2):
                                u = tt * 2 + half
                                if u < 2:
                                    e.wait_ge(sACT.h, v_act0)
                                else:
                                    e.wait_ge(sACT.h, v_act0 + u - 1)
                                for kc in range(8):
                                    inst = e.matmul(ps_s[u % 2][:, :],
                                                    ln3[:, kc * S + tt * 128: kc * S + (tt + 1) * 128],
                                                    wv_sb[:, kc * 1024 + half * 512: kc * 1024 + (half + 1) * 512],
                                                    start=(kc == 0), stop=(kc == 7))
                                sPE.bump(inst)
                    sec("tensor", _pev)

                    def _actv(e, v_pe0=v_pe0):
                        for tt in range(16):
                            for half in range(2):
                                u = tt * 2 + half
                                e.wait_ge(sPE.h, v_pe0 + u + 1)
                                inst = e.activation(
                                    out=vb[:, tt * 1024 + half * 512: tt * 1024 + (half + 1) * 512],
                                    in_=ps_s[u % 2][:, :], func=AF.Copy)
                                sACT.bump(inst)
                    sec("scalar", _actv)
                    v_done = sACT.n

                    prev_store = sSt.n
                    prev_rf_dve = sDVE.n
                    for qg in range(4):
                        o1_ev_last = None
                        for qq in range(4):
                            qi = qg * 4 + qq
                            valid = (qi + 1) * 128
                            nks = (valid + FT - 1) // FT

                            q_pe0, q_act0 = sPE.n, sACT.n

                            def _peS(e, qi=qi, valid=valid, nks=nks, q_act0=q_act0,
                                     vd=v_done):
                                if qi == 0:
                                    e.wait_ge(sACT.h, vd)
                                for ks in range(nks):
                                    w = min(FT, valid - ks * FT)
                                    if ks < 2:
                                        e.wait_ge(sACT.h, q_act0)
                                    else:
                                        e.wait_ge(sACT.h, q_act0 + ks - 1)
                                    for kc in range(8):
                                        inst = e.matmul(
                                            ps_s[ks % 2][:, 0:w],
                                            qb[:, kc * S + qi * 128: kc * S + (qi + 1) * 128],
                                            kb[:, kc * S + ks * FT: kc * S + ks * FT + w],
                                            start=(kc == 0), stop=(kc == 7))
                                    sPE.bump(inst)
                            sec("tensor", _peS)

                            def _actS(e, qi=qi, valid=valid, nks=nks, q_pe0=q_pe0,
                                      prd=prev_rf_dve):
                                e.wait_ge(sDVE.h, prd)
                                for ks in range(nks):
                                    w = min(FT, valid - ks * FT)
                                    e.wait_ge(sPE.h, q_pe0 + ks + 1)
                                    inst = e.activation(out=s_f[:, ks * FT: ks * FT + w],
                                                        in_=ps_s[ks % 2][:, 0:w], func=AF.Copy)
                                    sACT.bump(inst)
                            sec("scalar", _actS)
                            s_ev = sACT.n

                            def _dveS(e, qi=qi, valid=valid, s_ev=s_ev):
                                e.wait_ge(sACT.h, s_ev)
                                e.tensor_add(out=s_f[:, qi * 128:(qi + 1) * 128],
                                             in0=s_f[:, qi * 128:(qi + 1) * 128], in1=mask_sb[:])
                                sDVE.bump(e.tensor_reduce(out=nm[:, 0:1], in_=s_f[:, 0:valid],
                                                          axis=AX.X, op=ALU.max, negate=True))
                            sec("vector", _dveS)
                            nm_done = sDVE.n

                            def _actE2(e, valid=valid, nm_done=nm_done):
                                e.wait_ge(sDVE.h, nm_done)
                                inst = e.activation(out=p_bf[:, 0:valid], in_=s_f[:, 0:valid],
                                                    func=AF.Exp, bias=nm[:, 0:1],
                                                    accum_out=nm[:, 1:2])
                                sACT.bump(inst)
                            sec("scalar", _actE2)
                            exp_done = sACT.n

                            rc_cnt = sDVE.n + 1

                            def _dveN(e, valid=valid, exp_done=exp_done, rc_cnt=rc_cnt):
                                e.wait_ge(sACT.h, exp_done)
                                sDVE.bump(e.reciprocal(out=nm[:, 0:1], in_=nm[:, 1:2]))
                                e.wait_ge(sDVE.h, rc_cnt)
                                sDVE.bump(e.tensor_scalar_mul(out=p_bf[:, 0:valid],
                                                              in0=p_bf[:, 0:valid],
                                                              scalar1=nm[:, 0:1]))
                            sec("vector", _dveN)
                            pn_done = sDVE.n

                            t_pe0, t_dve0 = sPE.n, sDVE.n

                            def _peT(e, qi=qi, pn_done=pn_done, t_dve0=t_dve0):
                                e.wait_ge(sDVE.h, pn_done)
                                for kc in range(qi + 1):
                                    if kc >= 2:
                                        e.wait_ge(sDVE.h, t_dve0 + kc - 1)
                                    sPE.bump(e.transpose(ps_t[kc % 2][:, :],
                                                         p_bf[:, kc * 128:(kc + 1) * 128],
                                                         ident_b[:]))
                                sPE.bump(e.transpose(ps_t[qi % 2][:, :],
                                                     p_bf[:, qi * 128:(qi + 1) * 128],
                                                     ident_b[:]))
                            sec("tensor", _peT)

                            def _dveT(e, qi=qi, t_pe0=t_pe0):
                                for kc in range(qi + 1):
                                    e.wait_ge(sPE.h, t_pe0 + min(kc + 2, qi + 2))
                                    sDVE.bump(e.tensor_copy(out=pt_bf[:, kc * 128:(kc + 1) * 128],
                                                            in_=ps_t[kc % 2][:, :]))
                            sec("vector", _dveT)
                            pt_done = sDVE.n

                            a_pe0, a_act0 = sPE.n, sACT.n

                            def _peAV(e, qi=qi, qq=qq, pt_done=pt_done, a_act0=a_act0):
                                e.wait_ge(sDVE.h, pt_done)
                                for dvc in range(8):
                                    if dvc < 2:
                                        e.wait_ge(sACT.h, a_act0)
                                    else:
                                        e.wait_ge(sACT.h, a_act0 + dvc - 1)
                                    for kc in range(qi + 1):
                                        inst = e.matmul(ps_av[dvc % 2][:, :],
                                                        vb[:, kc * 1024 + dvc * 128: kc * 1024 + (dvc + 1) * 128],
                                                        pt_bf[:, kc * 128:(kc + 1) * 128],
                                                        start=(kc == 0), stop=(kc == qi))
                                    sPE.bump(inst)
                            sec("tensor", _peAV)

                            def _actAV(e, qq=qq, a_pe0=a_pe0):
                                for dvc in range(8):
                                    e.wait_ge(sPE.h, a_pe0 + dvc + 1)
                                    inst = e.activation(
                                        out=o1[:, dvc * FT + qq * 128: dvc * FT + (qq + 1) * 128],
                                        in_=ps_av[dvc % 2][:, :], func=AF.Copy)
                                    sACT.bump(inst)
                            sec("scalar", _actAV)
                            o1_ev_last = sACT.n

                        def _syncrf(e, qg=qg, ps=prev_store, prd=prev_rf_dve):
                            e.wait_ge(sSt.h, ps)
                            e.wait_ge(sDVE.h, prd)
                            sv = resid.rearrange("(c p) t -> p c t", p=128)[
                                :, :, b * S + qg * FT: b * S + (qg + 1) * FT]
                            sL.bump(e.dma_start(out=rf[:].rearrange("p (c t) -> p c t", c=8),
                                                in_=sv), 16)
                        sec("sync", _syncrf)
                        rf_ld = sL.n

                        o_pe0, o_act0, o_dve0 = sPE.n, sACT.n, sDVE.n
                        o_ldb = [sLw[0].n, sLw[1].n]

                        def _syncWo(e, o_pe0=o_pe0):
                            for dmc in range(8):
                                par = dmc % 2
                                if dmc < 2:
                                    e.wait_ge(sPE.h, o_pe0)
                                else:
                                    e.wait_ge(sPE.h, o_pe0 + dmc - 1)
                                sLw[par].bump(e.dma_start(
                                    out=ws[par][:].rearrange("p (c f) -> p c f", c=8),
                                    in_=wob[dmc, :, :, :]), 16)
                        sec("sync", _syncWo)

                        def _peWo(e, o1_ev_last=o1_ev_last, o_ldb=o_ldb, o_act0=o_act0):
                            e.wait_ge(sACT.h, o1_ev_last)
                            for dmc in range(8):
                                par = dmc % 2
                                e.wait_ge(sLw[par].h, o_ldb[par] + 16 * (dmc // 2 + 1))
                                if dmc >= 2:
                                    e.wait_ge(sACT.h, o_act0 + dmc - 1)
                                for dvc in range(8):
                                    inst = e.matmul(ps_o[dmc % 2][:, :],
                                                    ws[par][:, dvc * 128:(dvc + 1) * 128],
                                                    o1[:, dvc * FT:(dvc + 1) * FT],
                                                    start=(dvc == 0), stop=(dvc == 7))
                                sPE.bump(inst)
                        sec("tensor", _peWo)

                        def _actWo(e, o_pe0=o_pe0, o_dve0=o_dve0):
                            for dmc in range(8):
                                par = dmc % 2
                                e.wait_ge(sPE.h, o_pe0 + dmc + 1)
                                if dmc >= 2:
                                    e.wait_ge(sDVE.h, o_dve0 + dmc - 1)
                                inst = e.activation(out=s_f[:, (dmc % 2) * FT:(dmc % 2) * FT + FT],
                                                    in_=ps_o[dmc % 2][:, :], func=AF.Copy)
                                sACT.bump(inst)
                        sec("scalar", _actWo)

                        def _dveWo(e, o_act0=o_act0, rf_ld=rf_ld):
                            e.wait_ge(sL.h, rf_ld)
                            for dmc in range(8):
                                e.wait_ge(sACT.h, o_act0 + dmc + 1)
                                sDVE.bump(e.tensor_add(
                                    out=rf[:, dmc * FT:(dmc + 1) * FT],
                                    in0=rf[:, dmc * FT:(dmc + 1) * FT],
                                    in1=s_f[:, (dmc % 2) * FT:(dmc % 2) * FT + FT]))
                        sec("vector", _dveWo)
                        prev_rf_dve = sDVE.n

                        def _gpWo(e, qg=qg, prd=prev_rf_dve):
                            e.wait_ge(sDVE.h, prd)
                            dv = resid.rearrange("(c p) t -> p c t", p=128)[
                                :, :, b * S + qg * FT: b * S + (qg + 1) * FT]
                            sSt.bump(e.dma_start(out=dv,
                                                 in_=rf[:].rearrange("p (c t) -> p c t", c=8)), 16)
                        sec("gpsimd", _gpWo)
                        prev_store = sSt.n

        parts = BUILD_PARTS
        p0()
        if "cast" in parts:
            cast_weights()
        if "ingest" in parts:
            ingest()
        if "barrier" in parts:
            barrier()
        if stages >= 2:
            for b in range(BLOC):
                mamba(b)
                barrier()
        if stages >= 3:
            ffn()
            barrier()
        if stages >= 4:
            for b in range(BLOC):
                attn(b)
                barrier()
        if stages >= 5:
            ffn()
            barrier()
        output_phase()

    return nc


def _sp_pack(inp):
    sp = np.zeros((128, SPC), np.float32)
    cw = np.asarray(inp["L0_attn_a_conv_w"], np.float32).reshape(DI, 4)
    for j in range(4):
        sp[:, SP_CW + j * 16: SP_CW + (j + 1) * 16] = cw[:, j].reshape(16, 128).T
    sp[:, SP_CB:SP_CB + 16] = np.asarray(inp["L0_attn_a_conv_b"], np.float32).reshape(16, 128).T
    sp[:, SP_DTB:SP_DTB + 16] = np.asarray(inp["L0_attn_a_dt_proj_b"], np.float32).reshape(16, 128).T
    sp[:, SP_D:SP_D + 16] = np.asarray(inp["L0_attn_a_D"], np.float32).reshape(16, 128).T
    al = np.asarray(inp["L0_attn_a_A_log"], np.float32).reshape(16, 128, NST)
    sp[:, SP_AL:SP_AL + 256] = al.transpose(1, 0, 2).reshape(128, 256)
    sp[:, SP_NW:SP_NW + 8] = np.asarray(inp["L0_attn_a_norm_w"], np.float32).reshape(8, 128).T
    sp[:, SP_S1:SP_S1 + 32] = np.asarray(inp["L0_ffn_w1_s"], np.float32).reshape(32, 128).T
    sp[:, SP_S2:SP_S2 + 32] = np.asarray(inp["L0_ffn_w2_s"], np.float32).reshape(32, 128).T
    sp[:, SP_H1S:SP_H1S + 32] = np.asarray(inp["L0_ffn_h1_s"], np.float32).reshape(32, 128).T
    sp[:, SP_H2S:SP_H2S + 32] = np.asarray(inp["L0_ffn_h2_s"], np.float32).reshape(32, 128).T
    sp[:, SP_W3S:SP_W3S + 8] = np.asarray(inp["L0_ffn_w3_s"], np.float32).reshape(8, 128).T
    return sp


def kernel(**inp):
    x = np.ascontiguousarray(np.asarray(inp["L0_attn_a_x"], np.float32))
    sp = _sp_pack(inp)
    shared = dict(
        sp_in=sp,
        ipw=np.ascontiguousarray(np.asarray(inp["L0_attn_a_in_proj_w"], np.float32)),
        xpw=np.ascontiguousarray(np.asarray(inp["L0_attn_a_x_proj_w"], np.float32)),
        dpw=np.ascontiguousarray(np.asarray(inp["L0_attn_a_dt_proj_w"], np.float32)),
        opw=np.ascontiguousarray(np.asarray(inp["L0_attn_a_out_proj_w"], np.float32)),
        wq=np.ascontiguousarray(np.asarray(inp["L0_attn_b_Wq"], np.float32)),
        wk=np.ascontiguousarray(np.asarray(inp["L0_attn_b_Wk"], np.float32)),
        wv=np.ascontiguousarray(np.asarray(inp["L0_attn_b_Wv"], np.float32)),
        wo=np.ascontiguousarray(np.asarray(inp["L0_attn_b_Wo"], np.float32)),
        w1t=np.ascontiguousarray(np.asarray(inp["L0_ffn_w1_w"], np.float32).T),
        w2t=np.ascontiguousarray(np.asarray(inp["L0_ffn_w2_w"], np.float32).T),
        h1t=np.ascontiguousarray(np.asarray(inp["L0_ffn_h1_w"], np.float32).T),
        h2t=np.ascontiguousarray(np.asarray(inp["L0_ffn_h2_w"], np.float32).T),
        w3t=np.ascontiguousarray(np.asarray(inp["L0_ffn_w3_w"], np.float32).T),
    )
    nc = build(STAGES)
    in_maps = []
    for c in range(NCORE):
        m = dict(shared)
        m["x_in"] = np.ascontiguousarray(x[c * BLOC:(c + 1) * BLOC].reshape(TOK, D))
        in_maps.append(m)
    res = run_bass_kernel_spmd(nc, in_maps, core_ids=list(range(NCORE)))
    out = np.concatenate(
        [np.asarray(r["out"], np.float32).reshape(BLOC, S, D) for r in res.results], axis=0)
    return out
